# version 19
# baseline (speedup 1.0000x reference)
import numpy as np
import ml_dtypes
import contextlib
import concourse.bass as bass
import concourse.mybir as mybir
from concourse.bass_utils import run_bass_kernel_spmd

F32 = mybir.dt.float32
BF16 = mybir.dt.bfloat16
I16 = mybir.dt.int16
AF = mybir.ActivationFunctionType
ALU = mybir.AluOpType
AXX = mybir.AxisListType.X

NPBF = ml_dtypes.bfloat16
T = 2048
D = 1024
DFF = 4096
H = 8
NT = 16
MAGIC = float(3 * 2.0 ** 22)
QC16 = float(0x5F37)
EPS = 1e-5


def _np_consts():
    N = 128
    n = np.arange(N)
    j65 = np.arange(65)
    jj = np.arange(1, 64)
    w = 2 * np.pi / N
    Rre = np.cos(w * np.outer(j65, n))
    Rim = np.zeros((64, N)); Rim[1:64] = -np.sin(w * np.outer(jj, n))
    cj = np.ones(65); cj[1:64] = 2.0
    Are = (cj[None, :] / N) * np.cos(w * np.outer(n, j65))
    Aim = np.zeros((N, 64)); Aim[:, 1:64] = (-2.0 / N) * np.sin(w * np.outer(n, jj))
    return {
        "cRreT": np.ascontiguousarray(Rre.T).astype(np.float32),
        "cRimT": np.ascontiguousarray(Rim.T).astype(np.float32),
        "cAreT": np.ascontiguousarray(Are.T).astype(np.float32),
        "cAimT": np.ascontiguousarray(Aim.T).astype(np.float32),
        "ident": np.eye(128, dtype=np.float32),
        "onesrow": np.ones((1, 128), dtype=np.float32),
        "onescol": np.ones((128, 1), dtype=np.float32),
    }


class TinyV:

    def __init__(self, v, sem):
        self._v = v
        self._s = sem

    def __getattr__(self, name):
        f = getattr(self._v, name)
        if not callable(f):
            return f

        def wrapped(*a, **k):
            r = f(*a, **k)
            self._s.inc(r)
            self._v.wait_ge(self._s.h, self._s.n)
            return r
        return wrapped


class Sem:
    def __init__(self, stack, nc, name):
        self.h = stack.enter_context(nc.semaphore(name))
        self.n = 0

    def inc(self, instr, k=1):
        instr.then_inc(self.h, k)
        self.n += k


DEBUG = False


def build_nc():
    nc = bass.Bass("TRN2", target_bir_lowering=False)
    es = contextlib.ExitStack()
    marks = {}

    def mark(name, sem):
        marks[name] = (sem.h, sem.n)

    def wait(eng, *names):
        for name in names:
            h, n = marks[name]
            if n > 0:
                eng.wait_ge(h, n)

    dx = nc.dram_tensor("x", [T, D], F32, kind="ExternalInput").ap()
    dwqkv = nc.dram_tensor("wqkv", [D, 3 * D], F32, kind="ExternalInput").ap()
    dwout = nc.dram_tensor("wout", [D, D], F32, kind="ExternalInput").ap()
    dw1 = nc.dram_tensor("w1", [DFF, D], F32, kind="ExternalInput").ap()
    dw2 = nc.dram_tensor("w2", [D, DFF], F32, kind="ExternalInput").ap()
    db1 = nc.dram_tensor("b1r", [1, DFF], F32, kind="ExternalInput").ap()
    db2 = nc.dram_tensor("b2r", [1, D], F32, kind="ExternalInput").ap()
    dg1 = nc.dram_tensor("g1t", [128, D], F32, kind="ExternalInput").ap()
    db1g = nc.dram_tensor("b1t", [128, D], F32, kind="ExternalInput").ap()
    dg2 = nc.dram_tensor("g2t", [128, DFF], F32, kind="ExternalInput").ap()
    db2g = nc.dram_tensor("b2t", [128, DFF], F32, kind="ExternalInput").ap()
    dmaskr = nc.dram_tensor("maskr", [128, 8], F32, kind="ExternalInput").ap()
    donehot = nc.dram_tensor("onehot", [128, 8], F32, kind="ExternalInput").ap()
    dconst = {k: nc.dram_tensor(k, list(v.shape), F32, kind="ExternalInput").ap()
              for k, v in _np_consts().items()}
    dy = nc.dram_tensor("y", [T, D], F32, kind="ExternalOutput").ap()
    KIND = "ExternalOutput" if DEBUG else "Internal"
    dtbl_in = nc.dram_tensor("tbl_in", [128, 128], F32).ap()
    dtbl_out = nc.dram_tensor("tbl_out", [128, 128], F32).ap()
    dw1q = nc.dram_tensor("w1q", [DFF, D], BF16, kind=KIND).ap()
    dw2q = nc.dram_tensor("w2q", [D, DFF], BF16, kind=KIND).ap()
    dvals = nc.dram_tensor("valsd", [D, T], BF16, kind=KIND).ap()
    dkvre = nc.dram_tensor("dkvre", [H * 65, T], BF16, kind=KIND).ap()
    dkvim = nc.dram_tensor("dkvim", [H * 64, T], BF16, kind=KIND).ap()

    def sb(name, shp, dt):
        return es.enter_context(nc.sbuf_tensor(name, shp, dt))

    wout_sb = sb("wout_sb", [128, 8 * D], BF16)
    swcols = sb("swcols", [128, 2], F32)
    b1row = es.enter_context(nc.sbuf_tensor("b1row", [1, DFF], BF16, side="right"))
    b2row = es.enter_context(nc.sbuf_tensor("b2row", [1, D], BF16, side="right"))
    rowar = sb("rowar", [1, 512], BF16)
    colar = sb("colar", [128, 90], F32)
    colar2 = sb("colar2", [128, 48], F32)
    srowcols = sb("srowcols", [128, 4], BF16)
    qk_vb = sb("qk_vb", [128, 2], BF16)
    qk_bi = sb("qk_bi", [128, 2], F32)
    qk_v64 = sb("qk_v64", [128, 64], F32)
    qk_sd64 = sb("qk_sd64", [128, 64], BF16)
    qk_t64a = sb("qk_t64a", [128, 64], F32)
    qk_t64b = sb("qk_t64b", [128, 64], F32)
    qk_y64a = sb("qk_y64a", [128, 64], F32)
    qk_y64b = sb("qk_y64b", [128, 64], F32)


    sG = Sem(es, nc, "sG")
    sXL = Sem(es, nc, "sXL")
    sXT = Sem(es, nc, "sXT")
    sWL = Sem(es, nc, "sWL")
    sWQL = Sem(es, nc, "sWQL")
    sWS1 = Sem(es, nc, "sWS1")
    sWS2 = Sem(es, nc, "sWS2")
    sKVS = Sem(es, nc, "sKVS")
    sKVL = Sem(es, nc, "sKVL")
    sVST = Sem(es, nc, "sVST")
    sTBL = Sem(es, nc, "sTBL")
    sGB = Sem(es, nc, "sGB")
    sVLD = Sem(es, nc, "sVLD")
    sXIN = Sem(es, nc, "sXIN")
    sWT1 = Sem(es, nc, "sWT1")
    sWT2 = Sem(es, nc, "sWT2")
    sTP = Sem(es, nc, "sTP")
    sDBG = Sem(es, nc, "sDBG")
    sMM = Sem(es, nc, "sMM")
    sA = Sem(es, nc, "sA")
    sV = Sem(es, nc, "sV")
    sGP = Sem(es, nc, "sGP")
    sCC = Sem(es, nc, "sCC")
    sOUT = Sem(es, nc, "sOUT")
    sWLs = [Sem(es, nc, f"sWLs{k}") for k in range(4)]
    sWQs = [Sem(es, nc, f"sWQs{k}") for k in range(4)]
    sWT2e = [Sem(es, nc, f"sWT2e{k}") for k in range(8)]
    sG2 = [Sem(es, nc, f"sG2{k}") for k in range(4)]
    sB2 = [Sem(es, nc, f"sB2{k}") for k in range(4)]

    esA1 = contextlib.ExitStack()

    def sbA1(name, shp, dt):
        return esA1.enter_context(nc.sbuf_tensor(name, shp, dt, side="right"))
    wkv = sbA1("wkv", [128, 8 * T], BF16)
    wstg = [sbA1(f"wstg{i}", [128, D], F32) for i in range(4)]
    wqf32 = sbA1("wqf32", [128, D], F32)
    wqi16 = [sbA1(f"wqi16{i}", [128, D], I16) for i in range(2)]
    dqwn = [sbA1(f"dqwn{i}", [128, D], BF16) for i in range(2)]
    kh = sbA1("kh", [128, T], BF16)
    vh = sbA1("vh", [128, T], BF16)
    xtmp = [sbA1(f"xtmp{i}", [128, D], BF16) for i in range(2)]
    fkre = sbA1("fkre", [65, T], BF16)

    esA2 = contextlib.ExitStack()

    def sbA2(name, shp, dt):
        return esA2.enter_context(nc.sbuf_tensor(name, shp, dt))
    xT = sbA2("xT", [128, 8 * T], BF16)
    crre = sbA2("crre", [128, 65], BF16)
    crim = sbA2("crim", [128, 64], BF16)
    care = sbA2("care", [65, 128], BF16)
    caim = sbA2("caim", [64, 128], BF16)
    onesrow_f = sbA2("onesrow_f", [1, 128], F32)
    onescol_f = sbA2("onescol_f", [128, 1], F32)
    maskr = sbA2("maskr_sb", [128, 8], F32)
    onehot = sbA2("onehot_sb", [128, 8], F32)
    Smtx = sbA2("Smtx", [128, 16], F32)
    Sr64 = sbA2("Sr64", [128, 8], F32)
    P_sb = sbA2("P_sb", [128, 16], F32)
    wacc = sbA2("wacc", [128, 64], F32)
    wcolsum = sbA2("wcolsum", [128, 2], F32)
    swv = sbA2("swv", [1, 2], F32)
    cont = sbA2("cont", [128, 128], F32)
    G = sbA2("G_sb", [128, 128], F32)
    wq = sbA2("wq", [128, 8 * D], BF16)
    scrB = sbA2("scrB", [64, T], BF16)
    fqre = [sbA2(f"fqre{i}", [65, T], BF16) for i in range(2)]
    fqim = [sbA2(f"fqim{i}", [64, T], BF16) for i in range(2)]
    valstg = sbA2("valstg", [128, T], BF16)
    qh = sbA2("qh", [128, T], BF16)

    esA3 = contextlib.ExitStack()
    fkim = esA3.enter_context(nc.sbuf_tensor("fkim", [64, T], BF16))
    fvre = esA3.enter_context(nc.sbuf_tensor("fvre", [65, T], BF16))
    fvim = esA3.enter_context(nc.sbuf_tensor("fvim", [64, T], BF16))
    kvre_p1 = esA3.enter_context(nc.sbuf_tensor("kvre_p1", [65, T], BF16))
    kvim_p1 = esA3.enter_context(nc.sbuf_tensor("kvim_p1", [64, T], BF16))
    esA3b = contextlib.ExitStack()
    kvre = kvim = None

    esAP = contextlib.ExitStack()

    def psA(name, shp, dt):
        return esAP.enter_context(nc.psum_tensor(name, shp, dt))
    pk = psA("pk", [128, 512], F32)
    pv = psA("pv", [128, 512], F32)
    pfre = [psA(f"pfre{i}", [65, 512], F32) for i in range(2)]
    pfim = [psA(f"pfim{i}", [64, 512], F32) for i in range(2)]
    pmisc = psA("pmisc", [128, 512], F32)

    block = es.enter_context(nc.Block())
    if True:
        @block.scalar
        def _(sc):
            sc.preload_act_table(AF.Gelu)

        @block.gpsimd
        def _(g):
            g.memset(Smtx[:, :], 0.0)
            g.memset(Sr64[:, :], 0.0)
            for pad in (qk_v64, qk_t64a, qk_t64b, qk_y64a, qk_y64b):
                g.memset(pad[:, :], 0.0)
            g.memset(qk_sd64[:, :], 0.0)
            for nm, dst in (("cRreT", crre), ("cRimT", crim), ("cAreT", care),
                            ("cAimT", caim)):
                sG.inc(g.dma_start(out=dst[:, :], in_=dconst[nm][:, :]), 16)
            sG.inc(g.dma_start(out=onesrow_f[:, :], in_=dconst["onesrow"][:, :]), 16)
            sG.inc(g.dma_start(out=onescol_f[:, :], in_=dconst["onescol"][:, :]), 16)
            sG.inc(g.dma_start(out=maskr[:, :], in_=dmaskr[:, :]), 16)
            sG.inc(g.dma_start(out=onehot[:, :], in_=donehot[:, :]), 16)
            sG.inc(g.dma_start(out=b1row[:, :], in_=db1[:, :]), 16)
            sG.inc(g.dma_start(out=b2row[:, :], in_=db2[:, :]), 16)
            mark("ld_consts", sG)
            for dc in range(8):
                sG.inc(g.dma_start(out=wkv[:, dc * T:(dc + 1) * T],
                                   in_=dwqkv[dc * 128:(dc + 1) * 128, D:3 * D]), 16)
            mark("ld_wkv", sG)
            pass

    for tt in range(NT):
        if True:
            @block.gpsimd
            def _(g, tt=tt):
                if tt >= 2:
                    g.wait_ge(sXT.h, 16 * (tt - 1))
                sXL.inc(g.dma_start(out=xtmp[tt % 2][:, :],
                                    in_=dx[tt * 128:(tt + 1) * 128, :]), 16)
                mark(f"xld_{tt}", sXL)

            @block.sync
            def _(s, tt=tt):
                s.wait_ge(sXL.h, 16 * min(tt + 2, NT))
                sXT.inc(s.dma_start_transpose(
                    xT[:, :].rearrange("p (c u t) -> p c u t", c=8, u=NT)[:, :, tt, :],
                    xtmp[tt % 2][:, :].rearrange("p (c d) -> p c d", c=8)), 16)
                mark(f"xT_{tt}", sXT)
                if tt == NT - 1:
                    mark("xT_all", sXT)

    if True:
        @block.gpsimd
        def _(g):
            for dc in range(8):
                sG.inc(g.dma_start(out=wq[:, dc * D:(dc + 1) * D],
                                   in_=dwqkv[dc * 128:(dc + 1) * 128, 0:D]), 16)
            mark("ld_wq", sG)
            for dc in range(8):
                sG.inc(g.dma_start(out=wout_sb[:, dc * D:(dc + 1) * D],
                                   in_=dwout[dc * 128:(dc + 1) * 128, :]), 16)
            mark("ld_wout", sG)
            mark("ld_all", sG)

    def _wsrc(i):
        if i < 32:
            return dw1[i * 128:(i + 1) * 128, :]
        j = i - 32
        return dw2[(j // 4) * 128:(j // 4 + 1) * 128,
                   (j % 4) * 1024:(j % 4 + 1) * 1024]

    def emit_wscan(u):
        @block.gpsimd
        def _(g, u=u):
            i = 2 * u
            if i >= 4:
                wait(g, f"wred_{i - 4}")
            sWLs[i % 4].inc(g.dma_start(out=wstg[i % 4][:, :], in_=_wsrc(i)), 16)

        @block.sync
        def _(s, u=u):
            i = 2 * u + 1
            if i >= 4:
                wait(s, f"wred_{i - 4}")
            sWLs[i % 4].inc(s.dma_start(out=wstg[i % 4][:, :], in_=_wsrc(i)), 16)

        @block.vector
        def _(v, u=u):
            for i in (2 * u, 2 * u + 1):
                v.wait_ge(sWLs[i % 4].h, 16 * (i // 4 + 1))
                sV.inc(v.tensor_reduce(wacc[:, i:i + 1], wstg[i % 4][:, :], AXX,
                                       ALU.add, apply_absolute_value=True))
                mark(f"wred_{i}", sV)
                if i == 63:
                    mark("wred_all", sV)

    def emit_sw_chain():
        @block.vector
        def _(v):
            wait(v, "wred_all")
            v.tensor_reduce(wcolsum[:, 0:1], wacc[:, 0:32], AXX, ALU.add)
            sV.inc(v.tensor_reduce(wcolsum[:, 1:2], wacc[:, 32:64], AXX, ALU.add))
            mark("wcolsum", sV)

        @block.tensor
        def _(t):
            wait(t, "wcolsum", "ld_consts")
            t.matmul(pmisc[0:1, 0:1], wcolsum[:, 0:1], onescol_f[:, :],
                     start=True, stop=True)
            sMM.inc(t.matmul(pmisc[0:1, 1:2], wcolsum[:, 1:2], onescol_f[:, :],
                             start=True, stop=True))
            mark("wtot_mm", sMM)

        @block.scalar
        def _(sc):
            wait(sc, "wtot_mm")
            sA.inc(sc.copy(swv[0:1, 0:2], pmisc[0:1, 0:2]))
            mark("wtot_ev", sA)

        @block.tensor
        def _(t):
            wait(t, "wtot_ev")
            sMM.inc(t.matmul(pmisc[:, 8:10], onesrow_f[0:1, :], swv[0:1, 0:2],
                             start=True, stop=True))
            mark("swb_mm", sMM)

        @block.scalar
        def _(sc):
            wait(sc, "swb_mm")
            sA.inc(sc.copy(swcols[:, 0:2], pmisc[:, 8:10]))
            mark("sw_bc", sA)

        @block.vector
        def _(v):
            wait(v, "sw_bc")
            tv = TinyV(v, sV)
            tv.tensor_scalar(wcolsum[:, 0:2], swcols[:, 0:2], 1.0 / (DFF * D), 1e-5,
                             ALU.mult, ALU.max)
            sV.inc(v.reciprocal(swcols[:, 0:2], wcolsum[:, 0:2]))
            mark("sw_cols", sV)

    def emit_wquant(i):
        if True:
            @block.sync
            def _(s, i=i):
                if i == 0:
                    wait(s, "sw_cols")
                if i >= 4:
                    wait(s, f"wq_rd_{i - 4}")
                if i < 32:
                    src = dw1[i * 128:(i + 1) * 128, :]
                else:
                    j = i - 32
                    src = dw2[(j // 4) * 128:(j // 4 + 1) * 128,
                              (j % 4) * 1024:(j % 4 + 1) * 1024]
                sWQs[i % 4].inc(s.dma_start(out=wstg[i % 4][:, :], in_=src), 16)

            @block.scalar
            def _(sc, i=i):
                sc.wait_ge(sWQs[i % 4].h, 16 * (i // 4 + 1))
                if i >= 2:
                    wait(sc, f"wq_cl_{i - 2}")
                swsl = swcols[:, 0:1] if i < 32 else swcols[:, 1:2]
                sc.activation(wqf32[:, :], wstg[i % 4][:, :], AF.Copy,
                              scale=swsl, bias=MAGIC)
                sA.inc(sc.activation(wqi16[i % 2][:, :], wqf32[:, :], AF.Copy, bias=-MAGIC))
                mark(f"wq_rd_{i}", sA)

            @block.vector
            def _(v, i=i):
                wait(v, f"wq_rd_{i}")
                if i >= 2:
                    v.wait_ge(sWS1.h, 16 * min(i, 32))
                    if i > 32:
                        v.wait_ge(sWS2.h, 16 * (i - 32))
                sV.inc(v.tensor_scalar(dqwn[i % 2][:, :], wqi16[i % 2][:, :], -1.0, 1.0,
                                       ALU.max, ALU.min))
                mark(f"wq_cl_{i}", sV)

            @block.gpsimd
            def _(g, i=i):
                wait(g, f"wq_cl_{i}")
                if i < 32:
                    dst = dw1q[i * 128:(i + 1) * 128, :]
                    sWS1.inc(g.dma_start(out=dst, in_=dqwn[i % 2][:, :]), 16)
                else:
                    j = i - 32
                    dst = dw2q[(j // 4) * 128:(j // 4 + 1) * 128,
                               (j % 4) * 1024:(j % 4 + 1) * 1024]
                    sWS2.inc(g.dma_start(out=dst, in_=dqwn[i % 2][:, :]), 16)
                if i == 31:
                    mark("w1q_done", sWS1)
                if i == 63:
                    mark("w2q_done", sWS2)

    for h in range(H):
        par = 0
        for tb in range(4):
            if True:
                @block.tensor
                def _(t, h=h, tb=tb):
                    if h == 0:
                        wait(t, f"xT_{4 * tb + 3}")
                    if h == 0 and tb == 0:
                        wait(t, "ld_wkv")
                    else:
                        wait(t, f"evkv_{h}_{tb - 1}" if tb > 0 else f"evkv_{h - 1}_3")
                    for tau, ps in (("k", pk), ("v", pv)):
                        col0 = (0 if tau == "k" else D) + h * 128
                        for dc in range(8):
                            ins = t.matmul(ps[:, :],
                                           wkv[:, dc * T + col0:dc * T + col0 + 128],
                                           xT[:, dc * T + tb * 512:dc * T + tb * 512 + 512],
                                           start=(dc == 0), stop=(dc == 7))
                            if tau == "v" and dc == 7:
                                sMM.inc(ins)
                    mark(f"mmkv_{h}_{tb}", sMM)

                @block.scalar
                def _(sc, h=h, tb=tb):
                    wait(sc, f"mmkv_{h}_{tb}")
                    if h > 0 and tb == 0:
                        wait(sc, f"kvm_{h - 1}")
                    tsl = slice(tb * 512, tb * 512 + 512)
                    sc.copy(kh[:, tsl], pk[:, :])
                    sA.inc(sc.copy(vh[:, tsl], pv[:, :]))
                    mark(f"evkv_{h}_{tb}", sA)

                @block.tensor
                def _(t, h=h, tb=tb):
                    wait(t, f"evkv_{h}_{tb}")
                    if h > 0:
                        wait(t, f"evdv_{h - 1}_{tb}")
                    tsl = slice(tb * 512, tb * 512 + 512)
                    t.matmul(pfre[tb % 2][:, :], crre[:, :], kh[:, tsl], start=True, stop=True)
                    sMM.inc(t.matmul(pfim[tb % 2][:, :], crim[:, :], kh[:, tsl],
                                     start=True, stop=True))
                    mark(f"mmdk_{h}_{tb}", sMM)

                @block.scalar
                def _(sc, h=h, tb=tb):
                    tsl = slice(tb * 512, tb * 512 + 512)
                    wait(sc, f"mmdk_{h}_{tb}")
                    if h > 0 and tb == 0:
                        wait(sc, f"kvm_{h - 1}")
                    sc.copy(fkre[:, tsl], pfre[tb % 2][:, :])
                    sA.inc(sc.copy(fkim[:, tsl], pfim[tb % 2][:, :]))
                    mark(f"evdk_{h}_{tb}", sA)

                @block.tensor
                def _(t, h=h, tb=tb):
                    wait(t, f"evdk_{h}_{tb}")
                    tsl = slice(tb * 512, tb * 512 + 512)
                    t.matmul(pfre[tb % 2][:, :], crre[:, :], vh[:, tsl], start=True, stop=True)
                    sMM.inc(t.matmul(pfim[tb % 2][:, :], crim[:, :], vh[:, tsl],
                                     start=True, stop=True))
                    mark(f"mmdv_{h}_{tb}", sMM)

                @block.scalar
                def _(sc, h=h, tb=tb):
                    tsl = slice(tb * 512, tb * 512 + 512)
                    wait(sc, f"mmdv_{h}_{tb}")
                    sc.copy(fvre[:, tsl], pfre[tb % 2][:, :])
                    sA.inc(sc.copy(fvim[:, tsl], pfim[tb % 2][:, :]))
                    mark(f"evdv_{h}_{tb}", sA)

            emit_wscan(h * 4 + tb)

        if True:
            @block.vector
            def _(v, h=h, par=par):
                wait(v, f"evdv_{h}_3")
                if h >= 1:
                    wait(v, f"kvst_{h - 1}")
                v.scalar_tensor_tensor(kvre_p1[:, :], fkre[:, :], 1.0, fvre[:, :],
                                       ALU.bypass, ALU.mult,
                                       accum_out=Sr64[0:65, h:h + 1])
                v.tensor_tensor(scrB[:, :], fkim[:, :], fvim[:, :], ALU.mult)
                v.scalar_tensor_tensor(kvre_p1[0:64, :], scrB[:, :], -1.0,
                                       kvre_p1[0:64, :], ALU.mult, ALU.add,
                                       accum_out=Smtx[0:64, 2 * h:2 * h + 1])
                v.tensor_tensor(scrB[:, :], fkre[0:64, :], fvim[:, :], ALU.mult)
                v.tensor_tensor(kvim_p1[:, :], fkim[:, :], fvre[0:64, :], ALU.mult)
                sV.inc(v.scalar_tensor_tensor(kvim_p1[:, :], scrB[:, :], 1.0,
                                              kvim_p1[:, :], ALU.bypass, ALU.add,
                                              accum_out=Smtx[0:64, 2 * h + 1:2 * h + 2]))
                mark(f"kvm_{h}", sV)

            @block.scalar
            def _(sc, h=h):
                wait(sc, f"kvm_{h}")
                sA.inc(sc.copy(Smtx[64:65, 2 * h:2 * h + 1], Sr64[64:65, h:h + 1]))
                mark(f"sfix_{h}", sA)

            @block.gpsimd
            def _(g, h=h, par=par):
                wait(g, f"kvm_{h}")
                sKVS.inc(g.dma_start(out=dkvre[h * 65:(h + 1) * 65, :], in_=kvre_p1[:, :]), 16)
                sKVS.inc(g.dma_start(out=dkvim[h * 64:(h + 1) * 64, :], in_=kvim_p1[:, :]), 16)
                mark(f"kvst_{h}", sKVS)

    esA3.close()
    kvre = [esA3b.enter_context(nc.sbuf_tensor(f"kvre{i}", [65, T], BF16)) for i in range(2)]
    kvim = [esA3b.enter_context(nc.sbuf_tensor(f"kvim{i}", [64, T], BF16)) for i in range(2)]

    emit_sw_chain()

    if True:
        @block.vector
        def _(v):
            wait(v, "sfix_7")
            last = None
            for r in range(8):
                last = v.tensor_scalar(cont[:, r * 16:(r + 1) * 16], Smtx[:, :],
                                       onehot[:, r:r + 1], None, ALU.mult)
            sV.inc(last)
            mark("cont_ready", sV)

        @block.gpsimd
        def _(g):
            wait(g, "cont_ready")
            sTBL.inc(g.dma_start(out=dtbl_in[:, :], in_=cont[:, :]), 16)
            g.wait_ge(sTBL.h, sTBL.n)
            sCC.inc(g.collective_compute(
                "AllReduce", ALU.add, replica_groups=[list(range(8))],
                ins=[dtbl_in[:, :].opt()], outs=[dtbl_out[:, :].opt()]))
            g.wait_ge(sCC.h, sCC.n)
            sTBL.inc(g.dma_start(out=G[:, :], in_=dtbl_out[:, :]), 16)
            mark("G_ready", sTBL)

        @block.vector
        def _(v):
            wait(v, "G_ready")
            tv = TinyV(v, sV)
            tv.tensor_scalar(P_sb[:, :], G[:, 0:16], maskr[:, 0:1], None, ALU.mult)
            for r in range(1, 8):
                tv.scalar_tensor_tensor(P_sb[:, :], G[:, r * 16:(r + 1) * 16],
                                        maskr[:, r:r + 1], P_sb[:, :],
                                        ALU.mult, ALU.add)
            sV.inc(v.engine_nop())
            mark("P_ready", sV)

    def emit_p2F(h):
        par = h % 2
        for tb in range(4):
            @block.tensor
            def _(t, h=h, tb=tb):
                if h == 0 and tb == 0:
                    wait(t, "ld_wq")
                else:
                    wait(t, f"evq_{h}_{tb - 1}" if tb > 0 else f"evq_{h - 1}_3")
                for dc in range(8):
                    ins = t.matmul(pk[:, :],
                                   wq[:, dc * D + h * 128:dc * D + h * 128 + 128],
                                   xT[:, dc * T + tb * 512:dc * T + tb * 512 + 512],
                                   start=(dc == 0), stop=(dc == 7))
                sMM.inc(ins)
                mark(f"mmq_{h}_{tb}", sMM)

            @block.scalar
            def _(sc, h=h, tb=tb):
                wait(sc, f"mmq_{h}_{tb}")
                if h > 0 and tb == 0:
                    wait(sc, f"mmfq_{h - 1}_3")
                sA.inc(sc.copy(qh[:, tb * 512:tb * 512 + 512], pk[:, :]))
                mark(f"evq_{h}_{tb}", sA)

            @block.tensor
            def _(t, h=h, tb=tb):
                wait(t, f"evq_{h}_{tb}")
                tsl = slice(tb * 512, tb * 512 + 512)
                t.matmul(pfre[tb % 2][:, :], crre[:, :], qh[:, tsl], start=True, stop=True)
                sMM.inc(t.matmul(pfim[tb % 2][:, :], crim[:, :], qh[:, tsl],
                                 start=True, stop=True))
                mark(f"mmfq_{h}_{tb}", sMM)

            @block.scalar
            def _(sc, h=h, tb=tb, par=par):
                wait(sc, f"mmfq_{h}_{tb}")
                if h > 1 and tb == 0:
                    wait(sc, f"outf_{h - 2}")
                tsl = slice(tb * 512, tb * 512 + 512)
                sc.copy(fqre[par][:, tsl], pfre[tb % 2][:, :])
                sA.inc(sc.copy(fqim[par][:, tsl], pfim[tb % 2][:, :]))
                mark(f"evfq_{h}_{tb}", sA)

    def emit_p2B(h):
        par = h % 2
        if True:
            @block.gpsimd
            def _(g, h=h, par=par):
                wait(g, "kvst_7")
                if h >= 2:
                    wait(g, f"outf_{h - 2}")
                sKVL.inc(g.dma_start(out=kvre[par][:, :], in_=dkvre[h * 65:(h + 1) * 65, :]), 16)
                sKVL.inc(g.dma_start(out=kvim[par][:, :], in_=dkvim[h * 64:(h + 1) * 64, :]), 16)
                mark(f"kvld_{h}", sKVL)

            @block.vector
            def _(v, h=h, par=par):
                wait(v, "P_ready")
                wait(v, f"kvld_{h}", f"evfq_{h}_3")
                v.tensor_tensor_scan(kvre[par][:, :], kvre[par][:, :], kvre[par][:, :],
                                     P_sb[0:65, 2 * h:2 * h + 1], ALU.add, ALU.bypass)
                v.tensor_tensor_scan(kvim[par][:, :], kvim[par][:, :], kvim[par][:, :],
                                     P_sb[0:64, 2 * h + 1:2 * h + 2], ALU.add, ALU.bypass)
                v.tensor_tensor(scrB[:, :], kvim[par][:, :], fqre[par][0:64, :], ALU.mult)
                v.tensor_tensor(fqre[par][:, :], kvre[par][:, :], fqre[par][:, :], ALU.mult)
                v.tensor_tensor(kvim[par][:, :], kvim[par][:, :], fqim[par][:, :], ALU.mult)
                v.tensor_tensor(fqre[par][0:64, :], kvim[par][:, :], fqre[par][0:64, :], ALU.add)
                v.tensor_tensor(fqim[par][:, :], kvre[par][0:64, :], fqim[par][:, :], ALU.mult)
                sV.inc(v.tensor_tensor(fqim[par][:, :], scrB[:, :], fqim[par][:, :],
                                       ALU.subtract))
                mark(f"outf_{h}", sV)

        for tb in range(4):
            @block.tensor
            def _(t, h=h, tb=tb, par=par):
                wait(t, f"outf_{h}")
                if not (h == 0 and tb == 0):
                    wait(t, f"evval_{h}_{tb - 1}" if tb > 0 else f"evval_{h - 1}_3")
                tsl = slice(tb * 512, tb * 512 + 512)
                t.matmul(pmisc[:, :], care[:, :], fqre[par][:, tsl], start=True, stop=False)
                sMM.inc(t.matmul(pmisc[:, :], caim[:, :], fqim[par][:, tsl],
                                 start=False, stop=True))
                mark(f"mmval_{h}_{tb}", sMM)

            @block.scalar
            def _(sc, h=h, tb=tb):
                wait(sc, f"mmval_{h}_{tb}")
                if tb == 0 and h > 0:
                    wait(sc, f"valst_{h - 1}")
                sA.inc(sc.copy(valstg[:, tb * 512:tb * 512 + 512], pmisc[:, :]))
                mark(f"evval_{h}_{tb}", sA)

        if True:
            @block.gpsimd
            def _(g, h=h):
                wait(g, f"evval_{h}_3")
                sVST.inc(g.dma_start(out=dvals[h * 128:(h + 1) * 128, :], in_=valstg[:, :]), 16)
                mark(f"valst_{h}", sVST)

        for k in range(8):
            emit_wquant(8 * h + k)

    for h in range(H):
        emit_p2F(h)
        if h >= 1:
            emit_p2B(h - 1)
    emit_p2B(H - 1)

    marks["vals_all"] = marks["valst_7"]

    if DEBUG:
        ddbg = nc.dram_tensor("dbg1", [128, 256], F32, kind="ExternalOutput").ap()
        if True:
            @block.gpsimd
            def _(g):
                wait(g, "sw_cols", "G_ready", "P_ready", "sfix_7", "wred_all")
                sDBG.inc(g.dma_start(out=ddbg[:, 0:64], in_=wacc[:, :]), 16)
                sDBG.inc(g.dma_start(out=ddbg[:, 64:66], in_=wcolsum[:, :]), 16)
                sDBG.inc(g.dma_start(out=ddbg[:, 66:68], in_=swcols[:, :]), 16)
                sDBG.inc(g.dma_start(out=ddbg[:, 68:84], in_=P_sb[:, :]), 16)
                sDBG.inc(g.dma_start(out=ddbg[:, 84:100], in_=Smtx[:, :]), 16)
                sDBG.inc(g.dma_start(out=ddbg[0:1, 100:102], in_=swv[:, :]), 16)
                sDBG.inc(g.dma_start(out=ddbg[:, 102:230], in_=G[:, :]), 16)
                g.wait_ge(sDBG.h, sDBG.n)

    esA3b.close()
    esA1.close()
    w1T = es.enter_context(nc.sbuf_tensor("w1T", [128, 8 * DFF], BF16, side="right"))

    if True:
        @block.sync
        def _(s):
            wait(s, "w1q_done", "w2q_done", "kvm_7", "xT_all", "mmdv_7_3")
            for dc in range(0, 8, 2):
                sWT1.inc(s.dma_start_transpose(w1T[:, dc * DFF:(dc + 1) * DFF],
                                               dw1q[:, dc * 128:(dc + 1) * 128]), 16)

        @block.sync
        def _(s):
            for dc in range(1, 8, 2):
                sWT1.inc(s.dma_start_transpose(w1T[:, dc * DFF:(dc + 1) * DFF],
                                               dw1q[:, dc * 128:(dc + 1) * 128]), 16)
        marks["w1T_ready"] = (sWT1.h, sWT1.n)

    esA2.close()
    esAP.close()
    w2T = es.enter_context(nc.sbuf_tensor("w2T", [128, 32 * D], BF16))
    g1tb = es.enter_context(nc.sbuf_tensor("g1tb", [128, D], BF16))
    b1tb = es.enter_context(nc.sbuf_tensor("b1tb", [128, D], BF16))
    g2s = [es.enter_context(nc.sbuf_tensor(f"g2s{i}", [128, 256], BF16, side="right")) for i in range(4)]
    b2s = [es.enter_context(nc.sbuf_tensor(f"b2s{i}", [128, 256], BF16, side="right")) for i in range(4)]
    valsin = es.enter_context(nc.sbuf_tensor("valsin", [128, D], BF16))
    xin = es.enter_context(nc.sbuf_tensor("xin", [128, D], BF16))
    y0t = [es.enter_context(nc.sbuf_tensor(f"y0t{i}", [128, D], BF16)) for i in range(3)]
    utt = [es.enter_context(nc.sbuf_tensor(f"ut{i}", [128, D], BF16)) for i in range(3)]
    dq1Tt = [es.enter_context(nc.sbuf_tensor(f"dq1T{i}", [128, D], BF16, side="right")) for i in range(2)]
    hga = [es.enter_context(nc.sbuf_tensor(f"hga{i}", [128, DFF], BF16)) for i in range(2)]
    hscr1 = es.enter_context(nc.sbuf_tensor("hscr", [128, 512], BF16))
    hscr = [hscr1, hscr1]
    fst = es.enter_context(nc.sbuf_tensor("fst", [128, D], BF16, side="right"))
    dq2Tp = [es.enter_context(nc.sbuf_tensor(f"dq2Tp{i}", [128, 512], BF16)) for i in range(2)]
    qi16one = es.enter_context(nc.sbuf_tensor("qi16c", [128, 512], I16, side="right"))
    qi16t = [qi16one, qi16one]
    outt = es.enter_context(nc.sbuf_tensor("outt", [128, 128], F32, side="right"))
    dq1 = xin

    pA = es.enter_context(nc.psum_tensor("pA", [128, 512], F32))
    pB = es.enter_context(nc.psum_tensor("pB", [128, 512], F32))
    pC = es.enter_context(nc.psum_tensor("pC", [128, 512], F32))
    pD = es.enter_context(nc.psum_tensor("pD", [128, 512], F32))
    pE = es.enter_context(nc.psum_tensor("pE", [128, 512], F32))
    pF = es.enter_context(nc.psum_tensor("pF", [128, 512], F32))
    pG = es.enter_context(nc.psum_tensor("pG", [128, 512], F32))

    A_DONE = ("outf_7", "evval_7_3", "mmval_7_3", "vals_all")

    if True:
        @block.gpsimd
        def _(g):
            wait(g, *A_DONE)
            sGB.inc(g.dma_start(out=g1tb[:, :], in_=dg1[:, :]), 16)
            sGB.inc(g.dma_start(out=b1tb[:, :], in_=db1g[:, :]), 16)
            mark("gb_ready", sGB)

        @block.sync
        def _(s):
            wait(s, "w2q_done", *A_DONE)
            for e in range(8):
                for j in range(4):
                    hc = 4 * e + j
                    sWT2e[e].inc(s.dma_start_transpose(w2T[:, hc * D:(hc + 1) * D],
                                                       dw2q[:, hc * 128:(hc + 1) * 128]), 16)
        for e in range(8):
            marks[f"w2Tg_{e}"] = (sWT2e[e].h, 64)

    C_SUM0, C_M0, C_V0, C_USC, C_R0, C_SIG0, C_UB, C_QA, C_QB = 0, 1, 2, 3, 4, 5, 6, 7, 8
    C_MAX1, C_S1, C_S1S, C_AL1, C_TMP = 9, 10, 11, 12, 13
    C_GS, C_GSQ, C_M2, C_V2, C_R2, C_UB2, C_Q2 = 14, 15, 16, 17, 18, 19, 20
    C_MAX2, C_S2, C_S2S, C_BETA = 21, 22, 23, 24
    C_M3, C_V3, C_R3, C_B3, C_Q3 = 25, 26, 27, 28, 29

    def quake(tag, vin, ncols, vout, qo):
        blk = block
        cs = slice(0, ncols)
        if True:
            @blk.scalar
            def _(sc):
                wait(sc, f"qpre_{tag}")
                sc.copy(qk_vb[:, cs], vin)
                sA.inc(sc.copy(qk_v64[:, cs], vin))
                mark(f"qa1_{tag}", sA)

            @blk.vector
            def _(v):
                wait(v, f"qa1_{tag}")
                sV.inc(v.tensor_single_scalar(qk_bi[:, cs],
                                              qk_vb[:, cs].bitcast(I16), 0.0, ALU.add))
                mark(f"qv1_{tag}", sV)

            @blk.scalar
            def _(sc):
                wait(sc, f"qv1_{tag}")
                sA.inc(sc.activation(qk_sd64[:, cs].bitcast(I16), qk_bi[:, cs],
                                     AF.Copy, scale=-0.5, bias=QC16))
                mark(f"qa2_{tag}", sA)

            @blk.vector
            def _(v):
                wait(v, f"qa2_{tag}")
                w64 = slice(0, 64)
                v.tensor_single_scalar(qk_y64a[:, w64], qk_sd64[:, w64], 0.0, ALU.add)
                cur, nxt = qk_y64a, qk_y64b
                for _i in range(3):
                    v.tensor_tensor(qk_t64a[:, w64], qk_v64[:, w64], cur[:, w64], ALU.mult)
                    v.tensor_tensor(qk_t64b[:, w64], qk_t64a[:, w64], cur[:, w64], ALU.mult)
                    v.tensor_scalar(qk_t64a[:, w64], qk_t64b[:, w64], -0.5, 1.5,
                                    ALU.mult, ALU.add)
                    v.tensor_tensor(nxt[:, w64], cur[:, w64], qk_t64a[:, w64], ALU.mult)
                    cur, nxt = nxt, cur
                sV.inc(v.tensor_single_scalar(vout, cur[:, cs], 0.0, ALU.add))
                v.wait_ge(sV.h, sV.n)
                mark(f"qdone_{tag}", sV)

    sTP1 = [Sem(es, nc, f"sTP1{i}") for i in range(2)]
    sROW1 = [Sem(es, nc, f"sROW1{i}") for i in range(2)]
    sROW2 = [Sem(es, nc, f"sROW2{i}") for i in range(2)]
    sT2e = [[Sem(es, nc, f"sT2e{k}_{i}") for i in range(2)] for k in range(8)]

    def _ctx(tt):
        p3 = tt % 3
        co = 30 * p3
        c2o = 16 * p3
        return (tt % 2,
                lambda c: colar[:, co + c:co + c + 1],
                lambda c: colar2[:, c2o + c:c2o + c + 1],
                256 * (tt % 2), 0,
                y0t[p3], utt[p3], dq1Tt[tt % 2], fst, qi16t[tt % 2])

    def emit_F1(tt):
        p2, CL, C2, rof, rofp, y0, ut, dq1T, fst_, qi16c = _ctx(tt)
        co = 30 * (tt % 3)

        @block.gpsimd
        def _(g, tt=tt):
            if tt == 0:
                wait(g, *A_DONE)
            if tt >= 1:
                wait(g, f"woutmm_{tt - 1}_1")
            sVLD.inc(g.dma_start(
                out=valsin[:, :].rearrange("d (h t) -> d h t", h=8),
                in_=dvals[:, tt * 128:(tt + 1) * 128].rearrange("(h d) t -> d h t", h=8)), 16)
            mark(f"vld_{tt}", sVLD)
            if tt >= 1:
                wait(g, f"tp1_{tt - 1}")
            sXIN.inc(g.dma_start(out=xin[:, :], in_=dx[tt * 128:(tt + 1) * 128, :]), 16)
            mark(f"xin_{tt}", sXIN)

        @block.tensor
        def _(t, tt=tt):
            if tt == 0:
                wait(t, "ld_wout")
            wait(t, f"vld_{tt}")
            if tt >= 1:
                wait(t, f"y0d_{tt - 1}")
            for half, ps in ((0, pA), (1, pB)):
                for dc in range(8):
                    ins = t.matmul(ps[:, :], valsin[:, dc * 128:(dc + 1) * 128],
                                   wout_sb[:, dc * D + half * 512:dc * D + half * 512 + 512],
                                   start=(dc == 0), stop=(dc == 7))
                sMM.inc(ins)
                mark(f"woutmm_{tt}_{half}", sMM)

        @block.vector
        def _(v, tt=tt):
            wait(v, f"woutmm_{tt}_1", f"xin_{tt}")
            if tt >= 3:
                wait(v, f"y2d_{tt - 3}")
            v.scalar_tensor_tensor(y0[:, 0:512], pA[:, :], 1.0, xin[:, 0:512],
                                   ALU.bypass, ALU.add, accum_out=C2(0))
            sV.inc(v.scalar_tensor_tensor(y0[:, 512:1024], pB[:, :], 1.0,
                                          xin[:, 512:1024], ALU.bypass, ALU.add,
                                          accum_out=C2(1)))
            mark(f"y0d_{tt}", sV)

        @block.scalar
        def _(sc, tt=tt):
            wait(sc, f"y0d_{tt}")
            sc.activation(pG[:, :], y0[:, 0:512], AF.Square, accum_out=C2(2))
            sA.inc(sc.activation(pG[:, :], y0[:, 512:1024], AF.Square, accum_out=C2(3)))
            mark(f"sq0_{tt}", sA)

        @block.vector
        def _(v, tt=tt):
            wait(v, f"sq0_{tt}")
            v = TinyV(v, sV)
            v.tensor_tensor(CL(C_SUM0), C2(0), C2(1), ALU.add)
            v.tensor_single_scalar(CL(C_M0), CL(C_SUM0), 1.0 / D, ALU.mult)
            v.tensor_tensor(CL(C_TMP), C2(2), C2(3), ALU.add)
            v.tensor_single_scalar(CL(C_V0), CL(C_TMP), 1.0 / D, ALU.mult)
            v.scalar_tensor_tensor(CL(C_TMP), CL(C_M0), CL(C_M0), CL(C_V0),
                                   ALU.mult, ALU.subtract)
            v.tensor_single_scalar(CL(C_V0), CL(C_TMP), -1.0, ALU.mult)
            v.tensor_scalar(CL(C_QA), CL(C_V0), 1.0 + EPS, EPS * EPS, ALU.mult, ALU.add)
            v.tensor_single_scalar(CL(C_QB), CL(C_V0), EPS, ALU.add)
            mark(f"qpre_q1_{tt}", sV)

        quake(f"q1_{tt}", colar[:, co + C_QA:co + C_QB + 1], 2,
              colar[:, co + C_USC:co + C_R0 + 1], 0)

    def emit_F2(tt):
        p2, CL, C2, rof, rofp, y0, ut, dq1T, fst_, qi16c = _ctx(tt)

        @block.vector
        def _(v, tt=tt):
            wait(v, f"qdone_q1_{tt}", "gb_ready")
            if tt >= 3:
                wait(v, f"outd_{tt - 3}_7")
            v.scalar_tensor_tensor(dq1T[:, :], y0[:, :], CL(C_M0), g1tb[:, :],
                                   ALU.subtract, ALU.mult)
            v.scalar_tensor_tensor(ut[:, :], dq1T[:, :], CL(C_USC), b1tb[:, :],
                                   ALU.mult, ALU.add)
            tv = TinyV(v, sV)
            tv.tensor_tensor(CL(C_SIG0), CL(C_QB), CL(C_R0), ALU.mult)
            tv.tensor_reduce(CL(C_MAX1), ut[:, :], AXX, ALU.max,
                             apply_absolute_value=True)
            tv.tensor_scalar(CL(C_S1), CL(C_MAX1), 1e-5, 1.0 / 127.0, ALU.max, ALU.mult)
            tv.reciprocal(CL(C_S1), CL(C_S1))
            tv.tensor_tensor(CL(C_S1S), CL(C_S1), swcols[:, 0:1], ALU.mult)
            tv.reciprocal(CL(C_AL1), CL(C_S1S))
            tv.tensor_single_scalar(srowcols[:, 2 * p2:2 * p2 + 1],
                                    CL(C_S1S), 0.0, ALU.add)
            mark(f"h1d_{tt}", sV)

        for half in range(2):
            @block.scalar
            def _(sc, tt=tt, half=half):
                if half == 0:
                    wait(sc, f"h1d_{tt}")
                else:
                    wait(sc, f"c1v_{tt}_0")
                hs = slice(half * 512, half * 512 + 512)
                sc.activation(pG[:, :], ut[:, hs], AF.Copy, scale=CL(C_S1), bias=MAGIC)
                sA.inc(sc.activation(qi16c[:, :], pG[:, :], AF.Copy, bias=-MAGIC))
                mark(f"c1a_{tt}_{half}", sA)

            @block.vector
            def _(v, tt=tt, half=half):
                wait(v, f"c1a_{tt}_{half}")
                hs = slice(half * 512, half * 512 + 512)
                sV.inc(v.tensor_single_scalar(dq1[:, hs], qi16c[:, :], 0.0, ALU.add))
                mark(f"c1v_{tt}_{half}", sV)
                if half == 1:
                    mark(f"conv1_{tt}", sV)

        @block.sync
        def _(s, tt=tt):
            wait(s, f"conv1_{tt}")
            sTP1[tt % 2].inc(s.dma_start_transpose(
                dq1T[:, :].rearrange("p (c t) -> p c t", c=8),
                dq1[:, :].rearrange("p (c d) -> p c d", c=8)), 16)
            mark(f"tp1_{tt}", sTP1[tt % 2])

    def emit_Bp1(tt):
        p2, CL, C2, rof, rofp, y0, ut, dq1T, fst_, qi16c = _ctx(tt)
        co = 30 * (tt % 3)

        @block.gpsimd
        def _(g, tt=tt):
            wait(g, f"h1d_{tt}")
            sROW1[p2].inc(g.dma_start(out=rowar[0:1, rof:rof + 128],
                                      in_=srowcols[:, 2 * p2:2 * p2 + 1]), 16)
            mark(f"s1rowev_{tt}", sROW1[p2])

        for e in range(8):
            @block.tensor
            def _(t, tt=tt, e=e):
                if e == 0:
                    wait(t, f"tp1_{tt}", f"s1rowev_{tt}")
                    if tt == 0:
                        wait(t, "w1T_ready")
                ps = pC if e % 2 == 0 else pD
                if tt > 0 or e >= 2:
                    wait(t, f"gelu_{tt}_{e - 2}" if e >= 2 else f"gelu_{tt - 1}_{6 + e}")
                for dc in range(8):
                    t.matmul(ps[:, :], dq1T[:, dc * 128:(dc + 1) * 128],
                             w1T[:, dc * DFF + e * 512:dc * DFF + e * 512 + 512],
                             start=(dc == 0), stop=False)
                sMM.inc(t.matmul(ps[:, :], rowar[0:1, rof:rof + 128],
                                 b1row[0:1, e * 512:e * 512 + 512],
                                 start=False, stop=True))
                mark(f"m1_{tt}_{e}", sMM)

            @block.scalar
            def _(sc, tt=tt, e=e):
                wait(sc, f"m1_{tt}_{e}")
                if tt >= 2:
                    wait(sc, f"tp2_{tt - 2}_{e}")
                ps = pC if e % 2 == 0 else pD
                sc.activation(hga[tt % 2][:, e * 512:e * 512 + 512], ps[:, :],
                              AF.Gelu, scale=CL(C_AL1), accum_out=C2(0 + e))
                sA.inc(sc.activation(pG[:, :], hga[tt % 2][:, e * 512:e * 512 + 512],
                                     AF.Square, accum_out=C2(8 + e)))
                mark(f"gelu_{tt}_{e}", sA)
                mark(f"sq2_{tt}_{e}", sA)

        @block.vector
        def _(v, tt=tt):
            wait(v, f"sq2_{tt}_7")
            v = TinyV(v, sV)
            v.tensor_reduce(CL(C_GS), colar2[:, 16 * (tt % 3) + 0:16 * (tt % 3) + 8], AXX, ALU.add)
            v.tensor_reduce(CL(C_GSQ), colar2[:, 16 * (tt % 3) + 8:16 * (tt % 3) + 16], AXX, ALU.add)
            v.tensor_single_scalar(CL(C_M2), CL(C_GS), 1.0 / DFF, ALU.mult)
            v.tensor_single_scalar(CL(C_V2), CL(C_GSQ), 1.0 / DFF, ALU.mult)
            v.scalar_tensor_tensor(CL(C_TMP), CL(C_M2), CL(C_M2), CL(C_V2),
                                   ALU.mult, ALU.subtract)
            v.tensor_single_scalar(CL(C_V2), CL(C_TMP), -1.0, ALU.mult)
            v.tensor_single_scalar(CL(C_Q2), CL(C_V2), EPS, ALU.add)
            mark(f"qpre_q2_{tt}", sV)

        quake(f"q2_{tt}", colar[:, co + C_Q2:co + C_Q2 + 1], 1,
              colar[:, co + C_R2:co + C_R2 + 1], 0)

    def emit_Bp2(tt):
        p2, CL, C2, rof, rofp, y0, ut, dq1T, fst_, qi16c = _ctx(tt)
        co = 30 * (tt % 3)

        @block.vector
        def _(v, tt=tt):
            wait(v, f"qdone_q2_{tt}")
            for k in range(16):
                n = 16 * tt + k
                v.wait_ge(sG2[k % 4].h, 16 * (n // 4 + 1))
                v.wait_ge(sB2[k % 4].h, 16 * (n // 4 + 1))
                hs = slice(k * 256, k * 256 + 256)
                v.scalar_tensor_tensor(hscr1[:, 0:256], hga[tt % 2][:, hs], CL(C_M2),
                                       g2s[k % 4][:, :], ALU.subtract, ALU.mult)
                v.scalar_tensor_tensor(hga[tt % 2][:, hs], hscr1[:, 0:256], CL(C_R2),
                                       b2s[k % 4][:, :], ALU.mult, ALU.add)
                sV.inc(v.tensor_reduce(C2(k), hga[tt % 2][:, hs], AXX, ALU.max,
                                       apply_absolute_value=True))
                mark(f"h2e_{tt}_{k}", sV)
            v.wait_ge(sV.h, sV.n)
            tv = TinyV(v, sV)
            tv.tensor_reduce(CL(C_MAX2), colar2[:, 16 * (tt % 3) + 0:16 * (tt % 3) + 16],
                             AXX, ALU.max)
            tv.tensor_scalar(CL(C_S2), CL(C_MAX2), 1e-5, 1.0 / 127.0, ALU.max, ALU.mult)
            tv.reciprocal(CL(C_S2), CL(C_S2))
            tv.tensor_tensor(CL(C_S2S), CL(C_S2), swcols[:, 1:2], ALU.mult)
            tv.reciprocal(CL(C_TMP), CL(C_S2S))
            tv.tensor_tensor(CL(C_BETA), CL(C_SIG0), CL(C_TMP), ALU.mult)
            tv.tensor_single_scalar(srowcols[:, 2 * p2 + 1:2 * p2 + 2],
                                    CL(C_S2S), 0.0, ALU.add)
            mark(f"h2d_{tt}", sV)

        @block.gpsimd
        def _(g, tt=tt):
            for k in range(16):
                if k >= 4:
                    wait(g, f"h2e_{tt}_{k - 4}")
                elif tt >= 1:
                    wait(g, f"h2e_{tt - 1}_{12 + k}")
                sl = slice(k * 256, k * 256 + 256)
                sG2[k % 4].inc(g.dma_start(out=g2s[k % 4][:, :], in_=dg2[:, sl]), 16)
                sB2[k % 4].inc(g.dma_start(out=b2s[k % 4][:, :], in_=db2g[:, sl]), 16)

        @block.gpsimd
        def _(g, tt=tt):
            wait(g, f"h2d_{tt}")
            sROW2[p2].inc(g.dma_start(out=rowar[0:1, rof + 128:rof + 256],
                                      in_=srowcols[:, 2 * p2 + 1:2 * p2 + 2]), 16)
            mark(f"s2rowev_{tt}", sROW2[p2])

        for e in range(8):
            @block.scalar
            def _(sc, tt=tt, e=e):
                if e == 0:
                    wait(sc, f"h2d_{tt}")
                if e >= 1:
                    wait(sc, f"c2v_{tt}_{e - 1}")
                hs = slice(e * 512, e * 512 + 512)
                sc.activation(pG[:, :], hga[tt % 2][:, hs], AF.Copy,
                              scale=CL(C_S2), bias=MAGIC)
                sA.inc(sc.activation(qi16c[:, :], pG[:, :], AF.Copy, bias=-MAGIC))
                mark(f"c2a_{tt}_{e}", sA)

            @block.vector
            def _(v, tt=tt, e=e):
                wait(v, f"c2a_{tt}_{e}")
                hs = slice(e * 512, e * 512 + 512)
                sV.inc(v.tensor_single_scalar(hga[tt % 2][:, hs], qi16c[:, :],
                                              0.0, ALU.add))
                mark(f"c2v_{tt}_{e}", sV)

        for e in range(8):
            @block.sync
            def _(s, tt=tt, e=e):
                wait(s, f"c2v_{tt}_{e}")
                if e >= 2:
                    wait(s, f"m2p_{tt}_{e - 2}")
                elif tt >= 1:
                    wait(s, f"m2p_{tt - 1}_{6 + e}")
                sT2e[e][tt % 2].inc(s.dma_start_transpose(
                    dq2Tp[e % 2][:, :].rearrange("p (c t) -> p c t", c=4),
                    hga[tt % 2][:, e * 512:(e + 1) * 512].rearrange("p (c d) -> p c d", c=4)), 16)
                mark(f"tp2_{tt}_{e}", sT2e[e][tt % 2])

            @block.tensor
            def _(t, tt=tt, e=e):
                wait(t, f"tp2_{tt}_{e}")
                if tt == 0:
                    wait(t, f"w2Tg_{e}")
                if e == 0 and tt > 0:
                    wait(t, f"fs_{tt - 1}")
                for half, ps in ((0, pE), (1, pF)):
                    for j in range(4):
                        hc = 4 * e + j
                        ins = t.matmul(ps[:, :], dq2Tp[e % 2][:, j * 128:(j + 1) * 128],
                                       w2T[:, hc * D + half * 512:hc * D + half * 512 + 512],
                                       start=(hc == 0), stop=False)
                if e < 7:
                    sMM.inc(ins)
                    mark(f"m2p_{tt}_{e}", sMM)
                else:
                    wait(t, f"s2rowev_{tt}")
                    for half, ps in ((0, pE), (1, pF)):
                        sMM.inc(t.matmul(ps[:, :], rowar[0:1, rof + 128:rof + 256],
                                         b2row[0:1, half * 512:half * 512 + 512],
                                         start=False, stop=True))
                        mark(f"m2_{tt}_{half}", sMM)
                    marks[f"m2p_{tt}_7"] = marks[f"m2_{tt}_1"]

        @block.scalar
        def _(sc, tt=tt):
            wait(sc, f"m2_{tt}_1")
            if tt >= 1:
                wait(sc, f"y2d_{tt - 1}")
            sc.activation(fst[:, 0:512], pE[:, :], AF.Copy, scale=CL(C_BETA))
            sA.inc(sc.activation(fst[:, 512:1024], pF[:, :], AF.Copy, scale=CL(C_BETA)))
            mark(f"fs_{tt}", sA)

        @block.vector
        def _(v, tt=tt):
            wait(v, f"fs_{tt}")
            sV.inc(v.scalar_tensor_tensor(ut[:, :], fst[:, :], 1.0, y0[:, :],
                                          ALU.bypass, ALU.add, accum_out=C2(8)))
            mark(f"y2d_{tt}", sV)

        @block.scalar
        def _(sc, tt=tt):
            wait(sc, f"y2d_{tt}")
            sc.activation(pG[:, :], ut[:, 0:512], AF.Square, accum_out=C2(9))
            sA.inc(sc.activation(pG[:, :], ut[:, 512:1024], AF.Square, accum_out=C2(10)))
            mark(f"sq3_{tt}", sA)

        @block.vector
        def _(v, tt=tt):
            wait(v, f"sq3_{tt}")
            v = TinyV(v, sV)
            v.tensor_single_scalar(CL(C_M3), C2(8), 1.0 / D, ALU.mult)
            v.tensor_tensor(CL(C_TMP), C2(9), C2(10), ALU.add)
            v.tensor_single_scalar(CL(C_V3), CL(C_TMP), 1.0 / D, ALU.mult)
            v.scalar_tensor_tensor(CL(C_TMP), CL(C_M3), CL(C_M3), CL(C_V3),
                                   ALU.mult, ALU.subtract)
            v.tensor_single_scalar(CL(C_V3), CL(C_TMP), -1.0, ALU.mult)
            v.scalar_tensor_tensor(CL(C_Q3), CL(C_QB), EPS, CL(C_V3),
                                   ALU.mult, ALU.add)
            mark(f"qpre_q3_{tt}", sV)

        quake(f"q3_{tt}", colar[:, co + C_Q3:co + C_Q3 + 1], 1,
              colar[:, co + C_R3:co + C_R3 + 1], 0)

        @block.vector
        def _(v, tt=tt):
            wait(v, f"qdone_q3_{tt}")
            v = TinyV(v, sV)
            v.scalar_tensor_tensor(CL(C_B3), CL(C_M3), -1.0, CL(C_R3),
                                   ALU.mult, ALU.mult)
            mark(f"st3_{tt}", sV)

        for qq in range(8):
            @block.scalar
            def _(sc, tt=tt, qq=qq):
                if qq == 0:
                    wait(sc, f"st3_{tt}")
                    if tt >= 1:
                        wait(sc, f"ost_{tt - 1}_7")
                else:
                    wait(sc, f"ost_{tt}_{qq - 1}")
                hs = slice(qq * 128, qq * 128 + 128)
                sA.inc(sc.activation(outt[:, :], ut[:, hs], AF.Identity,
                                     scale=CL(C_R3), bias=CL(C_B3)))
                mark(f"outd_{tt}_{qq}", sA)

            @block.gpsimd
            def _(g, tt=tt, qq=qq):
                wait(g, f"outd_{tt}_{qq}")
                hs = slice(qq * 128, qq * 128 + 128)
                sOUT.inc(g.dma_start(out=dy[tt * 128:(tt + 1) * 128, hs],
                                     in_=outt[:, :]), 16)
                mark(f"ost_{tt}_{qq}", sOUT)
                if tt == NT - 1 and qq == 7:
                    g.wait_ge(sOUT.h, 128 * NT)

    for tt in range(NT):
        emit_F1(tt)
        if tt >= 1:
            emit_Bp1(tt - 1)
        emit_F2(tt)
        if tt >= 2:
            emit_Bp2(tt - 2)
    emit_Bp1(NT - 1)
    emit_Bp2(NT - 2)
    emit_Bp2(NT - 1)

    es.close()
    return nc


_NC_CACHE = {}


def kernel(**inputs):
    x = np.ascontiguousarray(np.asarray(inputs["L0_attn_x"], dtype=np.float32))
    wqkv = np.ascontiguousarray(np.asarray(inputs["L0_attn_w_qkv"], dtype=np.float32))
    wout = np.ascontiguousarray(np.asarray(inputs["L0_attn_w_out"], dtype=np.float32))
    g1 = np.asarray(inputs["L0_ffn_ln1_g"], dtype=np.float32)
    b1g = np.asarray(inputs["L0_ffn_ln1_b"], dtype=np.float32)
    w1 = np.ascontiguousarray(np.asarray(inputs["L0_ffn_w1"], dtype=np.float32))
    b1 = np.asarray(inputs["L0_ffn_b1"], dtype=np.float32)
    g2 = np.asarray(inputs["L0_ffn_ln2_g"], dtype=np.float32)
    b2g = np.asarray(inputs["L0_ffn_ln2_b"], dtype=np.float32)
    w2 = np.ascontiguousarray(np.asarray(inputs["L0_ffn_w2"], dtype=np.float32))
    b2 = np.asarray(inputs["L0_ffn_b2"], dtype=np.float32)

    consts = _np_consts()
    if "nc" not in _NC_CACHE:
        _NC_CACHE["nc"] = build_nc()
    nc = _NC_CACHE["nc"]

    in_maps = []
    for c in range(8):
        b = c // 4
        q = c % 4
        m = np.zeros(8, np.float32)
        for r in range(8):
            if r // 4 == b and (r % 4) < q:
                m[r] = 1.0
        oh = np.zeros(8, np.float32)
        oh[c] = 1.0
        im = {
            "x": np.ascontiguousarray(x[b, q * T:(q + 1) * T, :]),
            "wqkv": wqkv,
            "wout": wout,
            "w1": w1,
            "w2": w2,
            "b1r": np.ascontiguousarray(b1.reshape(1, DFF)),
            "b2r": np.ascontiguousarray(b2.reshape(1, D)),
            "g1t": np.ascontiguousarray(np.broadcast_to(g1[None, :], (128, D))),
            "b1t": np.ascontiguousarray(np.broadcast_to(b1g[None, :], (128, D))),
            "g2t": np.ascontiguousarray(np.broadcast_to(g2[None, :], (128, DFF))),
            "b2t": np.ascontiguousarray(np.broadcast_to(b2g[None, :], (128, DFF))),
            "maskr": np.ascontiguousarray(np.broadcast_to(m[None, :], (128, 8))),
            "onehot": np.ascontiguousarray(np.broadcast_to(oh[None, :], (128, 8))),
        }
        im.update(consts)
        in_maps.append(im)

    res = run_bass_kernel_spmd(nc, in_maps, core_ids=list(range(8)))
    out = np.zeros((2, 8192, D), np.float32)
    for c in range(8):
        b = c // 4
        q = c % 4
        out[b, q * T:(q + 1) * T, :] = res.results[c]["y"]
    return out


# version 20
# speedup vs baseline: 1.0695x; 1.0695x over previous
import numpy as np
import ml_dtypes
import contextlib
import concourse.bass as bass
import concourse.mybir as mybir
from concourse.bass_utils import run_bass_kernel_spmd

F32 = mybir.dt.float32
BF16 = mybir.dt.bfloat16
I16 = mybir.dt.int16
AF = mybir.ActivationFunctionType
ALU = mybir.AluOpType
AXX = mybir.AxisListType.X

NPBF = ml_dtypes.bfloat16
T = 2048
D = 1024
DFF = 4096
H = 8
NT = 16
MAGIC = float(3 * 2.0 ** 22)
QC16 = float(0x5F37)
EPS = 1e-5


def _np_consts():
    N = 128
    n = np.arange(N)
    j65 = np.arange(65)
    jj = np.arange(1, 64)
    w = 2 * np.pi / N
    Rre = np.cos(w * np.outer(j65, n))
    Rim = np.zeros((64, N)); Rim[1:64] = -np.sin(w * np.outer(jj, n))
    cj = np.ones(65); cj[1:64] = 2.0
    Are = (cj[None, :] / N) * np.cos(w * np.outer(n, j65))
    Aim = np.zeros((N, 64)); Aim[:, 1:64] = (-2.0 / N) * np.sin(w * np.outer(n, jj))
    return {
        "cRreT": np.ascontiguousarray(Rre.T).astype(np.float32),
        "cRimT": np.ascontiguousarray(Rim.T).astype(np.float32),
        "cAreT": np.ascontiguousarray(Are.T).astype(np.float32),
        "cAimT": np.ascontiguousarray(Aim.T).astype(np.float32),
        "ident": np.eye(128, dtype=np.float32),
        "onesrow": np.ones((1, 128), dtype=np.float32),
        "onescol": np.ones((128, 1), dtype=np.float32),
    }


class TinyV:

    def __init__(self, v, sem):
        self._v = v
        self._s = sem

    def __getattr__(self, name):
        f = getattr(self._v, name)
        if not callable(f):
            return f

        def wrapped(*a, **k):
            r = f(*a, **k)
            self._s.inc(r)
            self._v.wait_ge(self._s.h, self._s.n)
            return r
        return wrapped


class Sem:
    def __init__(self, stack, nc, name):
        self.h = stack.enter_context(nc.semaphore(name))
        self.n = 0

    def inc(self, instr, k=1):
        instr.then_inc(self.h, k)
        self.n += k


DEBUG = False


def build_nc():
    nc = bass.Bass("TRN2", target_bir_lowering=False)
    es = contextlib.ExitStack()
    marks = {}

    def mark(name, sem):
        marks[name] = (sem.h, sem.n)

    def wait(eng, *names):
        for name in names:
            h, n = marks[name]
            if n > 0:
                eng.wait_ge(h, n)

    dx = nc.dram_tensor("x", [T, D], F32, kind="ExternalInput").ap()
    dwqkv = nc.dram_tensor("wqkv", [D, 3 * D], F32, kind="ExternalInput").ap()
    dwout = nc.dram_tensor("wout", [D, D], F32, kind="ExternalInput").ap()
    dw1 = nc.dram_tensor("w1", [DFF, D], F32, kind="ExternalInput").ap()
    dw2 = nc.dram_tensor("w2", [D, DFF], F32, kind="ExternalInput").ap()
    db1 = nc.dram_tensor("b1r", [1, DFF], F32, kind="ExternalInput").ap()
    db2 = nc.dram_tensor("b2r", [1, D], F32, kind="ExternalInput").ap()
    dg1 = nc.dram_tensor("g1t", [128, D], F32, kind="ExternalInput").ap()
    db1g = nc.dram_tensor("b1t", [128, D], F32, kind="ExternalInput").ap()
    dg2 = nc.dram_tensor("g2t", [128, DFF], F32, kind="ExternalInput").ap()
    db2g = nc.dram_tensor("b2t", [128, DFF], F32, kind="ExternalInput").ap()
    dmaskr = nc.dram_tensor("maskr", [128, 8], F32, kind="ExternalInput").ap()
    donehot = nc.dram_tensor("onehot", [128, 8], F32, kind="ExternalInput").ap()
    dconst = {k: nc.dram_tensor(k, list(v.shape), F32, kind="ExternalInput").ap()
              for k, v in _np_consts().items()}
    dy = nc.dram_tensor("y", [T, D], F32, kind="ExternalOutput").ap()
    KIND = "ExternalOutput" if DEBUG else "Internal"
    dtbl_in = nc.dram_tensor("tbl_in", [128, 128], F32).ap()
    dtbl_out = nc.dram_tensor("tbl_out", [128, 128], F32).ap()
    dw1q = nc.dram_tensor("w1q", [DFF, D], BF16, kind=KIND).ap()
    dw2q = nc.dram_tensor("w2q", [D, DFF], BF16, kind=KIND).ap()
    dvals = nc.dram_tensor("valsd", [D, T], BF16, kind=KIND).ap()
    dkvre = nc.dram_tensor("dkvre", [H * 65, T], BF16, kind=KIND).ap()
    dkvim = nc.dram_tensor("dkvim", [H * 64, T], BF16, kind=KIND).ap()

    def sb(name, shp, dt):
        return es.enter_context(nc.sbuf_tensor(name, shp, dt))

    wout_sb = sb("wout_sb", [128, 8 * D], BF16)
    swcols = sb("swcols", [128, 2], F32)
    b1row = es.enter_context(nc.sbuf_tensor("b1row", [1, DFF], BF16, side="right"))
    b2row = es.enter_context(nc.sbuf_tensor("b2row", [1, D], BF16, side="right"))
    rowar = sb("rowar", [1, 512], BF16)
    colar = sb("colar", [128, 90], F32)
    colar2 = sb("colar2", [128, 48], F32)
    srowcols = sb("srowcols", [128, 4], BF16)
    qk_vb = sb("qk_vb", [128, 2], BF16)
    qk_bi = sb("qk_bi", [128, 2], F32)
    qk_v64 = sb("qk_v64", [128, 64], F32)
    qk_sd64 = sb("qk_sd64", [128, 64], BF16)
    qk_t64a = sb("qk_t64a", [128, 64], F32)
    qk_y64a = sb("qk_y64a", [128, 64], F32)
    qk_y64b = sb("qk_y64b", [128, 64], F32)


    sG = Sem(es, nc, "sG")
    sXL = Sem(es, nc, "sXL")
    sXT = Sem(es, nc, "sXT")
    sWL = Sem(es, nc, "sWL")
    sWQL = Sem(es, nc, "sWQL")
    sWS1 = Sem(es, nc, "sWS1")
    sWS2 = Sem(es, nc, "sWS2")
    sKVS = Sem(es, nc, "sKVS")
    sKVL = Sem(es, nc, "sKVL")
    sVST = Sem(es, nc, "sVST")
    sTBL = Sem(es, nc, "sTBL")
    sGB = Sem(es, nc, "sGB")
    sVLD = Sem(es, nc, "sVLD")
    sXIN = Sem(es, nc, "sXIN")
    sWT1 = Sem(es, nc, "sWT1")
    sWT2 = Sem(es, nc, "sWT2")
    sTP = Sem(es, nc, "sTP")
    sDBG = Sem(es, nc, "sDBG")
    sMM = Sem(es, nc, "sMM")
    sA = Sem(es, nc, "sA")
    sV = Sem(es, nc, "sV")
    sGP = Sem(es, nc, "sGP")
    sCC = Sem(es, nc, "sCC")
    sOUT = Sem(es, nc, "sOUT")
    sWLs = [Sem(es, nc, f"sWLs{k}") for k in range(4)]
    sWQs = [Sem(es, nc, f"sWQs{k}") for k in range(4)]
    sWT2e = [Sem(es, nc, f"sWT2e{k}") for k in range(8)]
    sG2 = [Sem(es, nc, f"sG2{k}") for k in range(4)]
    sB2 = [Sem(es, nc, f"sB2{k}") for k in range(4)]

    esA1 = contextlib.ExitStack()

    def sbA1(name, shp, dt):
        return esA1.enter_context(nc.sbuf_tensor(name, shp, dt, side="right"))
    wkv = sbA1("wkv", [128, 8 * T], BF16)
    wstg = [sbA1(f"wstg{i}", [128, D], F32) for i in range(4)]
    wqf32 = sbA1("wqf32", [128, D], F32)
    wqi16 = [sbA1(f"wqi16{i}", [128, D], I16) for i in range(2)]
    dqwn = [sbA1(f"dqwn{i}", [128, D], BF16) for i in range(2)]
    kh = sbA1("kh", [128, T], BF16)
    vh = sbA1("vh", [128, T], BF16)
    xtmp = [sbA1(f"xtmp{i}", [128, D], BF16) for i in range(2)]
    fkre = sbA1("fkre", [65, T], BF16)

    esA2 = contextlib.ExitStack()

    def sbA2(name, shp, dt):
        return esA2.enter_context(nc.sbuf_tensor(name, shp, dt))
    xT = sbA2("xT", [128, 8 * T], BF16)
    crre = sbA2("crre", [128, 65], BF16)
    crim = sbA2("crim", [128, 64], BF16)
    care = sbA2("care", [65, 128], BF16)
    caim = sbA2("caim", [64, 128], BF16)
    onesrow_f = sbA2("onesrow_f", [1, 128], F32)
    onescol_f = sbA2("onescol_f", [128, 1], F32)
    maskr = sbA2("maskr_sb", [128, 8], F32)
    onehot = sbA2("onehot_sb", [128, 8], F32)
    Smtx = sbA2("Smtx", [128, 16], F32)
    Sr64 = sbA2("Sr64", [128, 8], F32)
    P_sb = sbA2("P_sb", [128, 16], F32)
    wacc = sbA2("wacc", [128, 64], F32)
    wcolsum = sbA2("wcolsum", [128, 2], F32)
    swv = sbA2("swv", [1, 2], F32)
    cont = sbA2("cont", [128, 128], F32)
    G = sbA2("G_sb", [128, 128], F32)
    wq = sbA2("wq", [128, 8 * D], BF16)
    scrB = sbA2("scrB", [64, T], BF16)
    fqre = [sbA2(f"fqre{i}", [65, T], BF16) for i in range(2)]
    fqim = [sbA2(f"fqim{i}", [64, T], BF16) for i in range(2)]
    valstg = sbA2("valstg", [128, T], BF16)
    qh = sbA2("qh", [128, T], BF16)

    esA3 = contextlib.ExitStack()
    fkim = esA3.enter_context(nc.sbuf_tensor("fkim", [64, T], BF16))
    fvre = esA3.enter_context(nc.sbuf_tensor("fvre", [65, T], BF16))
    fvim = esA3.enter_context(nc.sbuf_tensor("fvim", [64, T], BF16))
    kvre_p1 = esA3.enter_context(nc.sbuf_tensor("kvre_p1", [65, T], BF16))
    kvim_p1 = esA3.enter_context(nc.sbuf_tensor("kvim_p1", [64, T], BF16))
    esA3b = contextlib.ExitStack()
    kvre = kvim = None

    esAP = contextlib.ExitStack()

    def psA(name, shp, dt):
        return esAP.enter_context(nc.psum_tensor(name, shp, dt))
    pk = psA("pk", [128, 512], F32)
    pv = psA("pv", [128, 512], F32)
    pfre = [psA(f"pfre{i}", [65, 512], F32) for i in range(2)]
    pfim = [psA(f"pfim{i}", [64, 512], F32) for i in range(2)]
    pmisc = psA("pmisc", [128, 512], F32)

    block = es.enter_context(nc.Block())
    if True:
        @block.scalar
        def _(sc):
            sc.preload_act_table(AF.Gelu)

        @block.gpsimd
        def _(g):
            g.memset(Smtx[:, :], 0.0)
            g.memset(Sr64[:, :], 0.0)
            for pad in (qk_v64, qk_t64a, qk_y64a, qk_y64b):
                g.memset(pad[:, :], 0.0)
            g.memset(qk_sd64[:, :], 0.0)
            for nm, dst in (("cRreT", crre), ("cRimT", crim), ("cAreT", care),
                            ("cAimT", caim)):
                sG.inc(g.dma_start(out=dst[:, :], in_=dconst[nm][:, :]), 16)
            sG.inc(g.dma_start(out=onesrow_f[:, :], in_=dconst["onesrow"][:, :]), 16)
            sG.inc(g.dma_start(out=onescol_f[:, :], in_=dconst["onescol"][:, :]), 16)
            sG.inc(g.dma_start(out=maskr[:, :], in_=dmaskr[:, :]), 16)
            sG.inc(g.dma_start(out=onehot[:, :], in_=donehot[:, :]), 16)
            sG.inc(g.dma_start(out=b1row[:, :], in_=db1[:, :]), 16)
            sG.inc(g.dma_start(out=b2row[:, :], in_=db2[:, :]), 16)
            mark("ld_consts", sG)
            for dc in range(8):
                sG.inc(g.dma_start(out=wkv[:, dc * T:(dc + 1) * T],
                                   in_=dwqkv[dc * 128:(dc + 1) * 128, D:3 * D]), 16)
            mark("ld_wkv", sG)
            pass

    for tt in range(NT):
        if True:
            @block.gpsimd
            def _(g, tt=tt):
                if tt >= 2:
                    g.wait_ge(sXT.h, 16 * (tt - 1))
                sXL.inc(g.dma_start(out=xtmp[tt % 2][:, :],
                                    in_=dx[tt * 128:(tt + 1) * 128, :]), 16)
                mark(f"xld_{tt}", sXL)

            @block.sync
            def _(s, tt=tt):
                s.wait_ge(sXL.h, 16 * min(tt + 2, NT))
                sXT.inc(s.dma_start_transpose(
                    xT[:, :].rearrange("p (c u t) -> p c u t", c=8, u=NT)[:, :, tt, :],
                    xtmp[tt % 2][:, :].rearrange("p (c d) -> p c d", c=8)), 16)
                mark(f"xT_{tt}", sXT)
                if tt == NT - 1:
                    mark("xT_all", sXT)

    if True:
        @block.gpsimd
        def _(g):
            for dc in range(8):
                sG.inc(g.dma_start(out=wq[:, dc * D:(dc + 1) * D],
                                   in_=dwqkv[dc * 128:(dc + 1) * 128, 0:D]), 16)
            mark("ld_wq", sG)
            for dc in range(8):
                sG.inc(g.dma_start(out=wout_sb[:, dc * D:(dc + 1) * D],
                                   in_=dwout[dc * 128:(dc + 1) * 128, :]), 16)
            mark("ld_wout", sG)
            mark("ld_all", sG)

    def _wsrc(i):
        if i < 32:
            return dw1[i * 128:(i + 1) * 128, :]
        j = i - 32
        return dw2[(j // 4) * 128:(j // 4 + 1) * 128,
                   (j % 4) * 1024:(j % 4 + 1) * 1024]

    def emit_wscan(u):
        @block.gpsimd
        def _(g, u=u):
            i = 2 * u
            if i >= 4:
                wait(g, f"wred_{i - 4}")
            sWLs[i % 4].inc(g.dma_start(out=wstg[i % 4][:, :], in_=_wsrc(i)), 16)

        @block.sync
        def _(s, u=u):
            i = 2 * u + 1
            if i >= 4:
                wait(s, f"wred_{i - 4}")
            sWLs[i % 4].inc(s.dma_start(out=wstg[i % 4][:, :], in_=_wsrc(i)), 16)

        @block.vector
        def _(v, u=u):
            for i in (2 * u, 2 * u + 1):
                v.wait_ge(sWLs[i % 4].h, 16 * (i // 4 + 1))
                sV.inc(v.tensor_reduce(wacc[:, i:i + 1], wstg[i % 4][:, :], AXX,
                                       ALU.add, apply_absolute_value=True))
                mark(f"wred_{i}", sV)
                if i == 63:
                    mark("wred_all", sV)

    def emit_sw_chain():
        @block.vector
        def _(v):
            wait(v, "wred_all")
            v.tensor_reduce(wcolsum[:, 0:1], wacc[:, 0:32], AXX, ALU.add)
            sV.inc(v.tensor_reduce(wcolsum[:, 1:2], wacc[:, 32:64], AXX, ALU.add))
            mark("wcolsum", sV)

        @block.tensor
        def _(t):
            wait(t, "wcolsum", "ld_consts")
            t.matmul(pmisc[0:1, 0:1], wcolsum[:, 0:1], onescol_f[:, :],
                     start=True, stop=True)
            sMM.inc(t.matmul(pmisc[0:1, 1:2], wcolsum[:, 1:2], onescol_f[:, :],
                             start=True, stop=True))
            mark("wtot_mm", sMM)

        @block.scalar
        def _(sc):
            wait(sc, "wtot_mm")
            sA.inc(sc.copy(swv[0:1, 0:2], pmisc[0:1, 0:2]))
            mark("wtot_ev", sA)

        @block.tensor
        def _(t):
            wait(t, "wtot_ev")
            sMM.inc(t.matmul(pmisc[:, 8:10], onesrow_f[0:1, :], swv[0:1, 0:2],
                             start=True, stop=True))
            mark("swb_mm", sMM)

        @block.scalar
        def _(sc):
            wait(sc, "swb_mm")
            sA.inc(sc.copy(swcols[:, 0:2], pmisc[:, 8:10]))
            mark("sw_bc", sA)

        @block.vector
        def _(v):
            wait(v, "sw_bc")
            tv = TinyV(v, sV)
            tv.tensor_scalar(wcolsum[:, 0:2], swcols[:, 0:2], 1.0 / (DFF * D), 1e-5,
                             ALU.mult, ALU.max)
            sV.inc(v.reciprocal(swcols[:, 0:2], wcolsum[:, 0:2]))
            mark("sw_cols", sV)

    def emit_wquant(i):
        if True:
            @block.sync
            def _(s, i=i):
                if i == 0:
                    wait(s, "sw_cols")
                if i >= 4:
                    wait(s, f"wq_rd_{i - 4}")
                if i < 32:
                    src = dw1[i * 128:(i + 1) * 128, :]
                else:
                    j = i - 32
                    src = dw2[(j // 4) * 128:(j // 4 + 1) * 128,
                              (j % 4) * 1024:(j % 4 + 1) * 1024]
                sWQs[i % 4].inc(s.dma_start(out=wstg[i % 4][:, :], in_=src), 16)

            @block.scalar
            def _(sc, i=i):
                sc.wait_ge(sWQs[i % 4].h, 16 * (i // 4 + 1))
                if i >= 2:
                    wait(sc, f"wq_cl_{i - 2}")
                swsl = swcols[:, 0:1] if i < 32 else swcols[:, 1:2]
                sc.activation(wqf32[:, :], wstg[i % 4][:, :], AF.Copy,
                              scale=swsl, bias=MAGIC)
                sA.inc(sc.activation(wqi16[i % 2][:, :], wqf32[:, :], AF.Copy, bias=-MAGIC))
                mark(f"wq_rd_{i}", sA)

            @block.vector
            def _(v, i=i):
                wait(v, f"wq_rd_{i}")
                if i >= 2:
                    v.wait_ge(sWS1.h, 16 * min(i, 32))
                    if i > 32:
                        v.wait_ge(sWS2.h, 16 * (i - 32))
                sV.inc(v.tensor_scalar(dqwn[i % 2][:, :], wqi16[i % 2][:, :], -1.0, 1.0,
                                       ALU.max, ALU.min))
                mark(f"wq_cl_{i}", sV)

            @block.gpsimd
            def _(g, i=i):
                wait(g, f"wq_cl_{i}")
                if i < 32:
                    dst = dw1q[i * 128:(i + 1) * 128, :]
                    sWS1.inc(g.dma_start(out=dst, in_=dqwn[i % 2][:, :]), 16)
                else:
                    j = i - 32
                    dst = dw2q[(j // 4) * 128:(j // 4 + 1) * 128,
                               (j % 4) * 1024:(j % 4 + 1) * 1024]
                    sWS2.inc(g.dma_start(out=dst, in_=dqwn[i % 2][:, :]), 16)
                if i == 31:
                    mark("w1q_done", sWS1)
                if i == 63:
                    mark("w2q_done", sWS2)

    for h in range(H):
        par = 0
        for tb in range(4):
            if True:
                @block.tensor
                def _(t, h=h, tb=tb):
                    if h == 0:
                        wait(t, f"xT_{4 * tb + 3}")
                    if h == 0 and tb == 0:
                        wait(t, "ld_wkv")
                    else:
                        wait(t, f"evkv_{h}_{tb - 1}" if tb > 0 else f"evkv_{h - 1}_3")
                    for tau, ps in (("k", pk), ("v", pv)):
                        col0 = (0 if tau == "k" else D) + h * 128
                        for dc in range(8):
                            ins = t.matmul(ps[:, :],
                                           wkv[:, dc * T + col0:dc * T + col0 + 128],
                                           xT[:, dc * T + tb * 512:dc * T + tb * 512 + 512],
                                           start=(dc == 0), stop=(dc == 7))
                            if tau == "v" and dc == 7:
                                sMM.inc(ins)
                    mark(f"mmkv_{h}_{tb}", sMM)

                @block.scalar
                def _(sc, h=h, tb=tb):
                    wait(sc, f"mmkv_{h}_{tb}")
                    if h > 0 and tb == 0:
                        wait(sc, f"kvm_{h - 1}")
                    tsl = slice(tb * 512, tb * 512 + 512)
                    sc.copy(kh[:, tsl], pk[:, :])
                    sA.inc(sc.copy(vh[:, tsl], pv[:, :]))
                    mark(f"evkv_{h}_{tb}", sA)

                @block.tensor
                def _(t, h=h, tb=tb):
                    wait(t, f"evkv_{h}_{tb}")
                    if h > 0:
                        wait(t, f"evdv_{h - 1}_{tb}")
                    tsl = slice(tb * 512, tb * 512 + 512)
                    t.matmul(pfre[tb % 2][:, :], crre[:, :], kh[:, tsl], start=True, stop=True)
                    sMM.inc(t.matmul(pfim[tb % 2][:, :], crim[:, :], kh[:, tsl],
                                     start=True, stop=True))
                    mark(f"mmdk_{h}_{tb}", sMM)

                @block.scalar
                def _(sc, h=h, tb=tb):
                    tsl = slice(tb * 512, tb * 512 + 512)
                    wait(sc, f"mmdk_{h}_{tb}")
                    if h > 0 and tb == 0:
                        wait(sc, f"kvm_{h - 1}")
                    sc.copy(fkre[:, tsl], pfre[tb % 2][:, :])
                    sA.inc(sc.copy(fkim[:, tsl], pfim[tb % 2][:, :]))
                    mark(f"evdk_{h}_{tb}", sA)

                @block.tensor
                def _(t, h=h, tb=tb):
                    wait(t, f"evdk_{h}_{tb}")
                    tsl = slice(tb * 512, tb * 512 + 512)
                    t.matmul(pfre[tb % 2][:, :], crre[:, :], vh[:, tsl], start=True, stop=True)
                    sMM.inc(t.matmul(pfim[tb % 2][:, :], crim[:, :], vh[:, tsl],
                                     start=True, stop=True))
                    mark(f"mmdv_{h}_{tb}", sMM)

                @block.scalar
                def _(sc, h=h, tb=tb):
                    tsl = slice(tb * 512, tb * 512 + 512)
                    wait(sc, f"mmdv_{h}_{tb}")
                    sc.copy(fvre[:, tsl], pfre[tb % 2][:, :])
                    sA.inc(sc.copy(fvim[:, tsl], pfim[tb % 2][:, :]))
                    mark(f"evdv_{h}_{tb}", sA)

            emit_wscan(h * 4 + tb)

        if True:
            @block.vector
            def _(v, h=h, par=par):
                wait(v, f"evdv_{h}_3")
                if h >= 1:
                    wait(v, f"kvst_{h - 1}")
                v.scalar_tensor_tensor(kvre_p1[:, :], fkre[:, :], 1.0, fvre[:, :],
                                       ALU.bypass, ALU.mult,
                                       accum_out=Sr64[0:65, h:h + 1])
                v.tensor_tensor(scrB[:, :], fkim[:, :], fvim[:, :], ALU.mult)
                v.scalar_tensor_tensor(kvre_p1[0:64, :], scrB[:, :], -1.0,
                                       kvre_p1[0:64, :], ALU.mult, ALU.add,
                                       accum_out=Smtx[0:64, 2 * h:2 * h + 1])
                v.tensor_tensor(scrB[:, :], fkre[0:64, :], fvim[:, :], ALU.mult)
                v.tensor_tensor(kvim_p1[:, :], fkim[:, :], fvre[0:64, :], ALU.mult)
                sV.inc(v.scalar_tensor_tensor(kvim_p1[:, :], scrB[:, :], 1.0,
                                              kvim_p1[:, :], ALU.bypass, ALU.add,
                                              accum_out=Smtx[0:64, 2 * h + 1:2 * h + 2]))
                mark(f"kvm_{h}", sV)

            @block.scalar
            def _(sc, h=h):
                wait(sc, f"kvm_{h}")
                sA.inc(sc.copy(Smtx[64:65, 2 * h:2 * h + 1], Sr64[64:65, h:h + 1]))
                mark(f"sfix_{h}", sA)

            @block.gpsimd
            def _(g, h=h, par=par):
                wait(g, f"kvm_{h}")
                sKVS.inc(g.dma_start(out=dkvre[h * 65:(h + 1) * 65, :], in_=kvre_p1[:, :]), 16)
                sKVS.inc(g.dma_start(out=dkvim[h * 64:(h + 1) * 64, :], in_=kvim_p1[:, :]), 16)
                mark(f"kvst_{h}", sKVS)

    esA3.close()
    kvre = [esA3b.enter_context(nc.sbuf_tensor(f"kvre{i}", [65, T], BF16)) for i in range(2)]
    kvim = [esA3b.enter_context(nc.sbuf_tensor(f"kvim{i}", [64, T], BF16)) for i in range(2)]

    emit_sw_chain()

    if True:
        @block.vector
        def _(v):
            wait(v, "sfix_7")
            last = None
            for r in range(8):
                last = v.tensor_scalar(cont[:, r * 16:(r + 1) * 16], Smtx[:, :],
                                       onehot[:, r:r + 1], None, ALU.mult)
            sV.inc(last)
            mark("cont_ready", sV)

        @block.gpsimd
        def _(g):
            wait(g, "cont_ready")
            sTBL.inc(g.dma_start(out=dtbl_in[:, :], in_=cont[:, :]), 16)
            g.wait_ge(sTBL.h, sTBL.n)
            sCC.inc(g.collective_compute(
                "AllReduce", ALU.add, replica_groups=[list(range(8))],
                ins=[dtbl_in[:, :].opt()], outs=[dtbl_out[:, :].opt()]))
            g.wait_ge(sCC.h, sCC.n)
            sTBL.inc(g.dma_start(out=G[:, :], in_=dtbl_out[:, :]), 16)
            mark("G_ready", sTBL)

        @block.vector
        def _(v):
            wait(v, "G_ready")
            tv = TinyV(v, sV)
            tv.tensor_scalar(P_sb[:, :], G[:, 0:16], maskr[:, 0:1], None, ALU.mult)
            for r in range(1, 8):
                tv.scalar_tensor_tensor(P_sb[:, :], G[:, r * 16:(r + 1) * 16],
                                        maskr[:, r:r + 1], P_sb[:, :],
                                        ALU.mult, ALU.add)
            sV.inc(v.engine_nop())
            mark("P_ready", sV)

    def emit_p2F(h):
        par = h % 2
        for tb in range(4):
            @block.tensor
            def _(t, h=h, tb=tb):
                if h == 0 and tb == 0:
                    wait(t, "ld_wq")
                else:
                    wait(t, f"evq_{h}_{tb - 1}" if tb > 0 else f"evq_{h - 1}_3")
                for dc in range(8):
                    ins = t.matmul(pk[:, :],
                                   wq[:, dc * D + h * 128:dc * D + h * 128 + 128],
                                   xT[:, dc * T + tb * 512:dc * T + tb * 512 + 512],
                                   start=(dc == 0), stop=(dc == 7))
                sMM.inc(ins)
                mark(f"mmq_{h}_{tb}", sMM)

            @block.scalar
            def _(sc, h=h, tb=tb):
                wait(sc, f"mmq_{h}_{tb}")
                if h > 0 and tb == 0:
                    wait(sc, f"mmfq_{h - 1}_3")
                sA.inc(sc.copy(qh[:, tb * 512:tb * 512 + 512], pk[:, :]))
                mark(f"evq_{h}_{tb}", sA)

            @block.tensor
            def _(t, h=h, tb=tb):
                wait(t, f"evq_{h}_{tb}")
                tsl = slice(tb * 512, tb * 512 + 512)
                t.matmul(pfre[tb % 2][:, :], crre[:, :], qh[:, tsl], start=True, stop=True)
                sMM.inc(t.matmul(pfim[tb % 2][:, :], crim[:, :], qh[:, tsl],
                                 start=True, stop=True))
                mark(f"mmfq_{h}_{tb}", sMM)

            @block.scalar
            def _(sc, h=h, tb=tb, par=par):
                wait(sc, f"mmfq_{h}_{tb}")
                if h > 1 and tb == 0:
                    wait(sc, f"outf_{h - 2}")
                tsl = slice(tb * 512, tb * 512 + 512)
                sc.copy(fqre[par][:, tsl], pfre[tb % 2][:, :])
                sA.inc(sc.copy(fqim[par][:, tsl], pfim[tb % 2][:, :]))
                mark(f"evfq_{h}_{tb}", sA)

    def emit_p2B(h):
        par = h % 2
        if True:
            @block.gpsimd
            def _(g, h=h, par=par):
                wait(g, "kvst_7")
                if h >= 2:
                    wait(g, f"outf_{h - 2}")
                sKVL.inc(g.dma_start(out=kvre[par][:, :], in_=dkvre[h * 65:(h + 1) * 65, :]), 16)
                sKVL.inc(g.dma_start(out=kvim[par][:, :], in_=dkvim[h * 64:(h + 1) * 64, :]), 16)
                mark(f"kvld_{h}", sKVL)

            @block.vector
            def _(v, h=h, par=par):
                wait(v, "P_ready")
                wait(v, f"kvld_{h}", f"evfq_{h}_3")
                v.tensor_tensor_scan(kvre[par][:, :], kvre[par][:, :], kvre[par][:, :],
                                     P_sb[0:65, 2 * h:2 * h + 1], ALU.add, ALU.bypass)
                v.tensor_tensor_scan(kvim[par][:, :], kvim[par][:, :], kvim[par][:, :],
                                     P_sb[0:64, 2 * h + 1:2 * h + 2], ALU.add, ALU.bypass)
                v.tensor_tensor(scrB[:, :], kvim[par][:, :], fqre[par][0:64, :], ALU.mult)
                v.tensor_tensor(fqre[par][:, :], kvre[par][:, :], fqre[par][:, :], ALU.mult)
                v.tensor_tensor(kvim[par][:, :], kvim[par][:, :], fqim[par][:, :], ALU.mult)
                v.tensor_tensor(fqre[par][0:64, :], kvim[par][:, :], fqre[par][0:64, :], ALU.add)
                v.tensor_tensor(fqim[par][:, :], kvre[par][0:64, :], fqim[par][:, :], ALU.mult)
                sV.inc(v.tensor_tensor(fqim[par][:, :], scrB[:, :], fqim[par][:, :],
                                       ALU.subtract))
                mark(f"outf_{h}", sV)

        for tb in range(4):
            @block.tensor
            def _(t, h=h, tb=tb, par=par):
                wait(t, f"outf_{h}")
                if not (h == 0 and tb == 0):
                    wait(t, f"evval_{h}_{tb - 1}" if tb > 0 else f"evval_{h - 1}_3")
                tsl = slice(tb * 512, tb * 512 + 512)
                t.matmul(pmisc[:, :], care[:, :], fqre[par][:, tsl], start=True, stop=False)
                sMM.inc(t.matmul(pmisc[:, :], caim[:, :], fqim[par][:, tsl],
                                 start=False, stop=True))
                mark(f"mmval_{h}_{tb}", sMM)

            @block.scalar
            def _(sc, h=h, tb=tb):
                wait(sc, f"mmval_{h}_{tb}")
                if tb == 0 and h > 0:
                    wait(sc, f"valst_{h - 1}")
                sA.inc(sc.copy(valstg[:, tb * 512:tb * 512 + 512], pmisc[:, :]))
                mark(f"evval_{h}_{tb}", sA)

        if True:
            @block.gpsimd
            def _(g, h=h):
                wait(g, f"evval_{h}_3")
                sVST.inc(g.dma_start(out=dvals[h * 128:(h + 1) * 128, :], in_=valstg[:, :]), 16)
                mark(f"valst_{h}", sVST)

        for k in range(8):
            emit_wquant(8 * h + k)

    for h in range(H):
        emit_p2F(h)
        if h >= 1:
            emit_p2B(h - 1)
    emit_p2B(H - 1)

    marks["vals_all"] = marks["valst_7"]

    if DEBUG:
        ddbg = nc.dram_tensor("dbg1", [128, 256], F32, kind="ExternalOutput").ap()
        if True:
            @block.gpsimd
            def _(g):
                wait(g, "sw_cols", "G_ready", "P_ready", "sfix_7", "wred_all")
                sDBG.inc(g.dma_start(out=ddbg[:, 0:64], in_=wacc[:, :]), 16)
                sDBG.inc(g.dma_start(out=ddbg[:, 64:66], in_=wcolsum[:, :]), 16)
                sDBG.inc(g.dma_start(out=ddbg[:, 66:68], in_=swcols[:, :]), 16)
                sDBG.inc(g.dma_start(out=ddbg[:, 68:84], in_=P_sb[:, :]), 16)
                sDBG.inc(g.dma_start(out=ddbg[:, 84:100], in_=Smtx[:, :]), 16)
                sDBG.inc(g.dma_start(out=ddbg[0:1, 100:102], in_=swv[:, :]), 16)
                sDBG.inc(g.dma_start(out=ddbg[:, 102:230], in_=G[:, :]), 16)
                g.wait_ge(sDBG.h, sDBG.n)

    esA3b.close()
    esA1.close()
    w1T = es.enter_context(nc.sbuf_tensor("w1T", [128, 8 * DFF], BF16, side="right"))

    if True:
        @block.sync
        def _(s):
            wait(s, "w1q_done", "w2q_done", "kvm_7", "xT_all", "mmdv_7_3")
            for dc in range(0, 8, 2):
                sWT1.inc(s.dma_start_transpose(w1T[:, dc * DFF:(dc + 1) * DFF],
                                               dw1q[:, dc * 128:(dc + 1) * 128]), 16)

        @block.sync
        def _(s):
            for dc in range(1, 8, 2):
                sWT1.inc(s.dma_start_transpose(w1T[:, dc * DFF:(dc + 1) * DFF],
                                               dw1q[:, dc * 128:(dc + 1) * 128]), 16)
        marks["w1T_ready"] = (sWT1.h, sWT1.n)

    esA2.close()
    esAP.close()
    w2T = es.enter_context(nc.sbuf_tensor("w2T", [128, 32 * D], BF16))
    g1tb = es.enter_context(nc.sbuf_tensor("g1tb", [128, D], BF16))
    b1tb = es.enter_context(nc.sbuf_tensor("b1tb", [128, D], BF16))
    g2s = [es.enter_context(nc.sbuf_tensor(f"g2s{i}", [128, 256], BF16, side="right")) for i in range(4)]
    b2s = [es.enter_context(nc.sbuf_tensor(f"b2s{i}", [128, 256], BF16, side="right")) for i in range(4)]
    valsin = es.enter_context(nc.sbuf_tensor("valsin", [128, D], BF16))
    xin = es.enter_context(nc.sbuf_tensor("xin", [128, D], BF16))
    y0t = [es.enter_context(nc.sbuf_tensor(f"y0t{i}", [128, D], BF16)) for i in range(3)]
    utt = [es.enter_context(nc.sbuf_tensor(f"ut{i}", [128, D], BF16)) for i in range(3)]
    dq1Tt = [es.enter_context(nc.sbuf_tensor(f"dq1T{i}", [128, D], BF16, side="right")) for i in range(2)]
    hga = [es.enter_context(nc.sbuf_tensor(f"hga{i}", [128, DFF], BF16)) for i in range(2)]
    hscr1 = es.enter_context(nc.sbuf_tensor("hscr", [128, 512], BF16))
    hscr = [hscr1, hscr1]
    fst = es.enter_context(nc.sbuf_tensor("fst", [128, D], BF16, side="right"))
    dq2Tp = [es.enter_context(nc.sbuf_tensor(f"dq2Tp{i}", [128, 512], BF16)) for i in range(2)]
    qi16one = es.enter_context(nc.sbuf_tensor("qi16c", [128, 512], I16, side="right"))
    qi16t = [qi16one, qi16one]
    outt = [es.enter_context(nc.sbuf_tensor(f"outt{i}", [128, 128], F32, side="right")) for i in range(2)]
    dq1 = xin

    pA = es.enter_context(nc.psum_tensor("pA", [128, 512], F32))
    pB = es.enter_context(nc.psum_tensor("pB", [128, 512], F32))
    pC = es.enter_context(nc.psum_tensor("pC", [128, 512], F32))
    pD = es.enter_context(nc.psum_tensor("pD", [128, 512], F32))
    pE = es.enter_context(nc.psum_tensor("pE", [128, 512], F32))
    pF = es.enter_context(nc.psum_tensor("pF", [128, 512], F32))
    pG = es.enter_context(nc.psum_tensor("pG", [128, 512], F32))

    A_DONE = ("outf_7", "evval_7_3", "mmval_7_3", "vals_all")

    if True:
        @block.gpsimd
        def _(g):
            wait(g, *A_DONE)
            sGB.inc(g.dma_start(out=g1tb[:, :], in_=dg1[:, :]), 16)
            sGB.inc(g.dma_start(out=b1tb[:, :], in_=db1g[:, :]), 16)
            mark("gb_ready", sGB)

        @block.sync
        def _(s):
            wait(s, "w2q_done", *A_DONE)
            for e in range(8):
                for j in range(4):
                    hc = 4 * e + j
                    sWT2e[e].inc(s.dma_start_transpose(w2T[:, hc * D:(hc + 1) * D],
                                                       dw2q[:, hc * 128:(hc + 1) * 128]), 16)
        for e in range(8):
            marks[f"w2Tg_{e}"] = (sWT2e[e].h, 64)

    C_SUM0, C_M0, C_V0, C_USC, C_R0, C_SIG0, C_UB, C_QA, C_QB = 0, 1, 2, 3, 4, 5, 6, 7, 8
    C_MAX1, C_S1, C_S1S, C_AL1, C_TMP = 9, 10, 11, 12, 13
    C_GS, C_GSQ, C_M2, C_V2, C_R2, C_UB2, C_Q2 = 14, 15, 16, 17, 18, 19, 20
    C_MAX2, C_S2, C_S2S, C_BETA = 21, 22, 23, 24
    C_M3, C_V3, C_R3, C_B3, C_Q3 = 25, 26, 27, 28, 29

    def quake(tag, vin, ncols, vout, qo):
        blk = block
        cs = slice(0, ncols)
        if True:
            @blk.scalar
            def _(sc):
                wait(sc, f"qpre_{tag}")
                sc.copy(qk_vb[:, cs], vin)
                sA.inc(sc.copy(qk_v64[:, cs], vin))
                mark(f"qa1_{tag}", sA)

            @blk.vector
            def _(v):
                wait(v, f"qa1_{tag}")
                sV.inc(v.tensor_single_scalar(qk_bi[:, cs],
                                              qk_vb[:, cs].bitcast(I16), 0.0, ALU.add))
                mark(f"qv1_{tag}", sV)

            @blk.scalar
            def _(sc):
                wait(sc, f"qv1_{tag}")
                sA.inc(sc.activation(qk_sd64[:, cs].bitcast(I16), qk_bi[:, cs],
                                     AF.Copy, scale=-0.5, bias=QC16))
                mark(f"qa2_{tag}", sA)

            @blk.vector
            def _(v):
                wait(v, f"qa2_{tag}")
                w64 = slice(0, 64)
                v.tensor_single_scalar(qk_y64a[:, w64], qk_sd64[:, w64], 0.0, ALU.add)
                cur, nxt = qk_y64a, qk_y64b
                for _i in range(3):
                    v.tensor_tensor(qk_t64a[:, w64], qk_v64[:, w64], cur[:, w64], ALU.mult)
                    v.tensor_tensor(nxt[:, w64], qk_t64a[:, w64], cur[:, w64], ALU.mult)
                    v.tensor_scalar(qk_t64a[:, w64], nxt[:, w64], -0.5, 1.5,
                                    ALU.mult, ALU.add)
                    v.tensor_tensor(nxt[:, w64], cur[:, w64], qk_t64a[:, w64], ALU.mult)
                    cur, nxt = nxt, cur
                sV.inc(v.tensor_single_scalar(vout, cur[:, cs], 0.0, ALU.add))
                v.wait_ge(sV.h, sV.n)
                mark(f"qdone_{tag}", sV)

    sTP1 = [Sem(es, nc, f"sTP1{i}") for i in range(2)]
    sROW1 = [Sem(es, nc, f"sROW1{i}") for i in range(2)]
    sROW2 = [Sem(es, nc, f"sROW2{i}") for i in range(2)]
    sT2e = [[Sem(es, nc, f"sT2e{k}_{i}") for i in range(2)] for k in range(8)]

    def _ctx(tt):
        p3 = tt % 3
        co = 30 * p3
        c2o = 16 * p3
        return (tt % 2,
                lambda c: colar[:, co + c:co + c + 1],
                lambda c: colar2[:, c2o + c:c2o + c + 1],
                256 * (tt % 2), 0,
                y0t[p3], utt[p3], dq1Tt[tt % 2], fst, qi16t[tt % 2])

    def emit_F1(tt):
        p2, CL, C2, rof, rofp, y0, ut, dq1T, fst_, qi16c = _ctx(tt)
        co = 30 * (tt % 3)

        @block.gpsimd
        def _(g, tt=tt):
            if tt == 0:
                wait(g, *A_DONE)
            if tt >= 1:
                wait(g, f"woutmm_{tt - 1}_1")
            sVLD.inc(g.dma_start(
                out=valsin[:, :].rearrange("d (h t) -> d h t", h=8),
                in_=dvals[:, tt * 128:(tt + 1) * 128].rearrange("(h d) t -> d h t", h=8)), 16)
            mark(f"vld_{tt}", sVLD)
            if tt >= 1:
                wait(g, f"tp1_{tt - 1}")
            sXIN.inc(g.dma_start(out=xin[:, :], in_=dx[tt * 128:(tt + 1) * 128, :]), 16)
            mark(f"xin_{tt}", sXIN)

        @block.tensor
        def _(t, tt=tt):
            if tt == 0:
                wait(t, "ld_wout")
            wait(t, f"vld_{tt}")
            if tt >= 1:
                wait(t, f"y0d_{tt - 1}")
            for half, ps in ((0, pA), (1, pB)):
                for dc in range(8):
                    ins = t.matmul(ps[:, :], valsin[:, dc * 128:(dc + 1) * 128],
                                   wout_sb[:, dc * D + half * 512:dc * D + half * 512 + 512],
                                   start=(dc == 0), stop=(dc == 7))
                sMM.inc(ins)
                mark(f"woutmm_{tt}_{half}", sMM)

        @block.vector
        def _(v, tt=tt):
            wait(v, f"woutmm_{tt}_1", f"xin_{tt}")
            if tt >= 3:
                wait(v, f"y2d_{tt - 3}")
            v.scalar_tensor_tensor(y0[:, 0:512], pA[:, :], 1.0, xin[:, 0:512],
                                   ALU.bypass, ALU.add, accum_out=C2(0))
            sV.inc(v.scalar_tensor_tensor(y0[:, 512:1024], pB[:, :], 1.0,
                                          xin[:, 512:1024], ALU.bypass, ALU.add,
                                          accum_out=C2(1)))
            mark(f"y0d_{tt}", sV)

        @block.scalar
        def _(sc, tt=tt):
            wait(sc, f"y0d_{tt}")
            sc.activation(pG[:, :], y0[:, 0:512], AF.Square, accum_out=C2(2))
            sA.inc(sc.activation(pG[:, :], y0[:, 512:1024], AF.Square, accum_out=C2(3)))
            mark(f"sq0_{tt}", sA)

        @block.vector
        def _(v, tt=tt):
            wait(v, f"sq0_{tt}")
            v = TinyV(v, sV)
            v.tensor_tensor(CL(C_SUM0), C2(0), C2(1), ALU.add)
            v.tensor_single_scalar(CL(C_M0), CL(C_SUM0), 1.0 / D, ALU.mult)
            v.tensor_tensor(CL(C_TMP), C2(2), C2(3), ALU.add)
            v.tensor_single_scalar(CL(C_V0), CL(C_TMP), 1.0 / D, ALU.mult)
            v.scalar_tensor_tensor(CL(C_TMP), CL(C_M0), CL(C_M0), CL(C_V0),
                                   ALU.mult, ALU.subtract)
            v.tensor_single_scalar(CL(C_V0), CL(C_TMP), -1.0, ALU.mult)
            v.tensor_scalar(CL(C_QA), CL(C_V0), 1.0 + EPS, EPS * EPS, ALU.mult, ALU.add)
            v.tensor_single_scalar(CL(C_QB), CL(C_V0), EPS, ALU.add)
            mark(f"qpre_q1_{tt}", sV)

        quake(f"q1_{tt}", colar[:, co + C_QA:co + C_QB + 1], 2,
              colar[:, co + C_USC:co + C_R0 + 1], 0)

    def emit_F2(tt):
        p2, CL, C2, rof, rofp, y0, ut, dq1T, fst_, qi16c = _ctx(tt)

        @block.vector
        def _(v, tt=tt):
            wait(v, f"qdone_q1_{tt}", "gb_ready")
            if tt >= 3:
                wait(v, f"outd_{tt - 3}_7")
            v.scalar_tensor_tensor(dq1T[:, :], y0[:, :], CL(C_M0), g1tb[:, :],
                                   ALU.subtract, ALU.mult)
            v.scalar_tensor_tensor(ut[:, :], dq1T[:, :], CL(C_USC), b1tb[:, :],
                                   ALU.mult, ALU.add)
            tv = TinyV(v, sV)
            tv.tensor_tensor(CL(C_SIG0), CL(C_QB), CL(C_R0), ALU.mult)
            tv.tensor_reduce(CL(C_MAX1), ut[:, :], AXX, ALU.max,
                             apply_absolute_value=True)
            tv.tensor_scalar(CL(C_S1), CL(C_MAX1), 1e-5, 1.0 / 127.0, ALU.max, ALU.mult)
            tv.reciprocal(CL(C_S1), CL(C_S1))
            tv.tensor_tensor(CL(C_S1S), CL(C_S1), swcols[:, 0:1], ALU.mult)
            tv.reciprocal(CL(C_AL1), CL(C_S1S))
            tv.tensor_single_scalar(srowcols[:, 2 * p2:2 * p2 + 1],
                                    CL(C_S1S), 0.0, ALU.add)
            mark(f"h1d_{tt}", sV)

        for half in range(2):
            @block.scalar
            def _(sc, tt=tt, half=half):
                if half == 0:
                    wait(sc, f"h1d_{tt}")
                else:
                    wait(sc, f"c1v_{tt}_0")
                hs = slice(half * 512, half * 512 + 512)
                sc.activation(pG[:, :], ut[:, hs], AF.Copy, scale=CL(C_S1), bias=MAGIC)
                sA.inc(sc.activation(qi16c[:, :], pG[:, :], AF.Copy, bias=-MAGIC))
                mark(f"c1a_{tt}_{half}", sA)

            @block.vector
            def _(v, tt=tt, half=half):
                wait(v, f"c1a_{tt}_{half}")
                hs = slice(half * 512, half * 512 + 512)
                sV.inc(v.tensor_single_scalar(dq1[:, hs], qi16c[:, :], 0.0, ALU.add))
                mark(f"c1v_{tt}_{half}", sV)
                if half == 1:
                    mark(f"conv1_{tt}", sV)

        @block.sync
        def _(s, tt=tt):
            wait(s, f"conv1_{tt}")
            sTP1[tt % 2].inc(s.dma_start_transpose(
                dq1T[:, :].rearrange("p (c t) -> p c t", c=8),
                dq1[:, :].rearrange("p (c d) -> p c d", c=8)), 16)
            mark(f"tp1_{tt}", sTP1[tt % 2])

    def emit_Bp1(tt):
        p2, CL, C2, rof, rofp, y0, ut, dq1T, fst_, qi16c = _ctx(tt)
        co = 30 * (tt % 3)

        @block.gpsimd
        def _(g, tt=tt):
            wait(g, f"h1d_{tt}")
            sROW1[p2].inc(g.dma_start(out=rowar[0:1, rof:rof + 128],
                                      in_=srowcols[:, 2 * p2:2 * p2 + 1]), 16)
            mark(f"s1rowev_{tt}", sROW1[p2])

        for e in range(8):
            @block.tensor
            def _(t, tt=tt, e=e):
                if e == 0:
                    wait(t, f"tp1_{tt}", f"s1rowev_{tt}")
                    if tt == 0:
                        wait(t, "w1T_ready")
                ps = pC if e % 2 == 0 else pD
                if tt > 0 or e >= 2:
                    wait(t, f"gelu_{tt}_{e - 2}" if e >= 2 else f"gelu_{tt - 1}_{6 + e}")
                for dc in range(8):
                    t.matmul(ps[:, :], dq1T[:, dc * 128:(dc + 1) * 128],
                             w1T[:, dc * DFF + e * 512:dc * DFF + e * 512 + 512],
                             start=(dc == 0), stop=False)
                sMM.inc(t.matmul(ps[:, :], rowar[0:1, rof:rof + 128],
                                 b1row[0:1, e * 512:e * 512 + 512],
                                 start=False, stop=True))
                mark(f"m1_{tt}_{e}", sMM)

            @block.scalar
            def _(sc, tt=tt, e=e):
                wait(sc, f"m1_{tt}_{e}")
                if tt >= 2:
                    wait(sc, f"tp2_{tt - 2}_{e}")
                ps = pC if e % 2 == 0 else pD
                sc.activation(hga[tt % 2][:, e * 512:e * 512 + 512], ps[:, :],
                              AF.Gelu, scale=CL(C_AL1), accum_out=C2(0 + e))
                sA.inc(sc.activation(pG[:, :], hga[tt % 2][:, e * 512:e * 512 + 512],
                                     AF.Square, accum_out=C2(8 + e)))
                mark(f"gelu_{tt}_{e}", sA)
                mark(f"sq2_{tt}_{e}", sA)

        @block.vector
        def _(v, tt=tt):
            wait(v, f"sq2_{tt}_7")
            v = TinyV(v, sV)
            v.tensor_reduce(CL(C_GS), colar2[:, 16 * (tt % 3) + 0:16 * (tt % 3) + 8], AXX, ALU.add)
            v.tensor_reduce(CL(C_GSQ), colar2[:, 16 * (tt % 3) + 8:16 * (tt % 3) + 16], AXX, ALU.add)
            v.tensor_single_scalar(CL(C_M2), CL(C_GS), 1.0 / DFF, ALU.mult)
            v.tensor_single_scalar(CL(C_V2), CL(C_GSQ), 1.0 / DFF, ALU.mult)
            v.scalar_tensor_tensor(CL(C_TMP), CL(C_M2), CL(C_M2), CL(C_V2),
                                   ALU.mult, ALU.subtract)
            v.tensor_single_scalar(CL(C_V2), CL(C_TMP), -1.0, ALU.mult)
            v.tensor_single_scalar(CL(C_Q2), CL(C_V2), EPS, ALU.add)
            mark(f"qpre_q2_{tt}", sV)

        quake(f"q2_{tt}", colar[:, co + C_Q2:co + C_Q2 + 1], 1,
              colar[:, co + C_R2:co + C_R2 + 1], 0)

    def emit_Bp2(tt):
        p2, CL, C2, rof, rofp, y0, ut, dq1T, fst_, qi16c = _ctx(tt)
        co = 30 * (tt % 3)

        @block.vector
        def _(v, tt=tt):
            wait(v, f"qdone_q2_{tt}")
            for k in range(16):
                n = 16 * tt + k
                v.wait_ge(sG2[k % 4].h, 16 * (n // 4 + 1))
                v.wait_ge(sB2[k % 4].h, 16 * (n // 4 + 1))
                hs = slice(k * 256, k * 256 + 256)
                v.scalar_tensor_tensor(hscr1[:, 0:256], hga[tt % 2][:, hs], CL(C_M2),
                                       g2s[k % 4][:, :], ALU.subtract, ALU.mult)
                v.scalar_tensor_tensor(hga[tt % 2][:, hs], hscr1[:, 0:256], CL(C_R2),
                                       b2s[k % 4][:, :], ALU.mult, ALU.add)
                sV.inc(v.tensor_reduce(C2(k), hga[tt % 2][:, hs], AXX, ALU.max,
                                       apply_absolute_value=True))
                mark(f"h2e_{tt}_{k}", sV)
            v.wait_ge(sV.h, sV.n)
            tv = TinyV(v, sV)
            tv.tensor_reduce(CL(C_MAX2), colar2[:, 16 * (tt % 3) + 0:16 * (tt % 3) + 16],
                             AXX, ALU.max)
            tv.tensor_scalar(CL(C_S2), CL(C_MAX2), 1e-5, 1.0 / 127.0, ALU.max, ALU.mult)
            tv.reciprocal(CL(C_S2), CL(C_S2))
            tv.tensor_tensor(CL(C_S2S), CL(C_S2), swcols[:, 1:2], ALU.mult)
            tv.reciprocal(CL(C_TMP), CL(C_S2S))
            tv.tensor_tensor(CL(C_BETA), CL(C_SIG0), CL(C_TMP), ALU.mult)
            tv.tensor_single_scalar(srowcols[:, 2 * p2 + 1:2 * p2 + 2],
                                    CL(C_S2S), 0.0, ALU.add)
            mark(f"h2d_{tt}", sV)

        @block.gpsimd
        def _(g, tt=tt):
            for k in range(16):
                if k >= 4:
                    wait(g, f"h2e_{tt}_{k - 4}")
                elif tt >= 1:
                    wait(g, f"h2e_{tt - 1}_{12 + k}")
                sl = slice(k * 256, k * 256 + 256)
                sG2[k % 4].inc(g.dma_start(out=g2s[k % 4][:, :], in_=dg2[:, sl]), 16)
                sB2[k % 4].inc(g.dma_start(out=b2s[k % 4][:, :], in_=db2g[:, sl]), 16)

        @block.gpsimd
        def _(g, tt=tt):
            wait(g, f"h2d_{tt}")
            sROW2[p2].inc(g.dma_start(out=rowar[0:1, rof + 128:rof + 256],
                                      in_=srowcols[:, 2 * p2 + 1:2 * p2 + 2]), 16)
            mark(f"s2rowev_{tt}", sROW2[p2])

        for e in range(8):
            @block.scalar
            def _(sc, tt=tt, e=e):
                if e == 0:
                    wait(sc, f"h2d_{tt}")
                if e >= 1:
                    wait(sc, f"c2v_{tt}_{e - 1}")
                hs = slice(e * 512, e * 512 + 512)
                sc.activation(pG[:, :], hga[tt % 2][:, hs], AF.Copy,
                              scale=CL(C_S2), bias=MAGIC)
                sA.inc(sc.activation(qi16c[:, :], pG[:, :], AF.Copy, bias=-MAGIC))
                mark(f"c2a_{tt}_{e}", sA)

            @block.vector
            def _(v, tt=tt, e=e):
                wait(v, f"c2a_{tt}_{e}")
                hs = slice(e * 512, e * 512 + 512)
                sV.inc(v.tensor_single_scalar(hga[tt % 2][:, hs], qi16c[:, :],
                                              0.0, ALU.add))
                mark(f"c2v_{tt}_{e}", sV)

        for e in range(8):
            @block.sync
            def _(s, tt=tt, e=e):
                wait(s, f"c2v_{tt}_{e}")
                if e >= 2:
                    wait(s, f"m2p_{tt}_{e - 2}")
                elif tt >= 1:
                    wait(s, f"m2p_{tt - 1}_{6 + e}")
                sT2e[e][tt % 2].inc(s.dma_start_transpose(
                    dq2Tp[e % 2][:, :].rearrange("p (c t) -> p c t", c=4),
                    hga[tt % 2][:, e * 512:(e + 1) * 512].rearrange("p (c d) -> p c d", c=4)), 16)
                mark(f"tp2_{tt}_{e}", sT2e[e][tt % 2])

            @block.tensor
            def _(t, tt=tt, e=e):
                wait(t, f"tp2_{tt}_{e}")
                if tt == 0:
                    wait(t, f"w2Tg_{e}")
                if e == 0 and tt > 0:
                    wait(t, f"fs_{tt - 1}")
                for half, ps in ((0, pE), (1, pF)):
                    for j in range(4):
                        hc = 4 * e + j
                        ins = t.matmul(ps[:, :], dq2Tp[e % 2][:, j * 128:(j + 1) * 128],
                                       w2T[:, hc * D + half * 512:hc * D + half * 512 + 512],
                                       start=(hc == 0), stop=False)
                if e < 7:
                    sMM.inc(ins)
                    mark(f"m2p_{tt}_{e}", sMM)
                else:
                    wait(t, f"s2rowev_{tt}")
                    for half, ps in ((0, pE), (1, pF)):
                        sMM.inc(t.matmul(ps[:, :], rowar[0:1, rof + 128:rof + 256],
                                         b2row[0:1, half * 512:half * 512 + 512],
                                         start=False, stop=True))
                        mark(f"m2_{tt}_{half}", sMM)
                    marks[f"m2p_{tt}_7"] = marks[f"m2_{tt}_1"]

        @block.scalar
        def _(sc, tt=tt):
            wait(sc, f"m2_{tt}_1")
            if tt >= 1:
                wait(sc, f"y2d_{tt - 1}")
            sc.activation(fst[:, 0:512], pE[:, :], AF.Copy, scale=CL(C_BETA))
            sA.inc(sc.activation(fst[:, 512:1024], pF[:, :], AF.Copy, scale=CL(C_BETA)))
            mark(f"fs_{tt}", sA)

        @block.vector
        def _(v, tt=tt):
            wait(v, f"fs_{tt}")
            sV.inc(v.scalar_tensor_tensor(ut[:, :], fst[:, :], 1.0, y0[:, :],
                                          ALU.bypass, ALU.add, accum_out=C2(8)))
            mark(f"y2d_{tt}", sV)

        @block.scalar
        def _(sc, tt=tt):
            wait(sc, f"y2d_{tt}")
            sc.activation(pG[:, :], ut[:, 0:512], AF.Square, accum_out=C2(9))
            sA.inc(sc.activation(pG[:, :], ut[:, 512:1024], AF.Square, accum_out=C2(10)))
            mark(f"sq3_{tt}", sA)

        @block.vector
        def _(v, tt=tt):
            wait(v, f"sq3_{tt}")
            v = TinyV(v, sV)
            v.tensor_single_scalar(CL(C_M3), C2(8), 1.0 / D, ALU.mult)
            v.tensor_tensor(CL(C_TMP), C2(9), C2(10), ALU.add)
            v.tensor_single_scalar(CL(C_V3), CL(C_TMP), 1.0 / D, ALU.mult)
            v.scalar_tensor_tensor(CL(C_TMP), CL(C_M3), CL(C_M3), CL(C_V3),
                                   ALU.mult, ALU.subtract)
            v.tensor_single_scalar(CL(C_V3), CL(C_TMP), -1.0, ALU.mult)
            v.scalar_tensor_tensor(CL(C_Q3), CL(C_QB), EPS, CL(C_V3),
                                   ALU.mult, ALU.add)
            mark(f"qpre_q3_{tt}", sV)

        quake(f"q3_{tt}", colar[:, co + C_Q3:co + C_Q3 + 1], 1,
              colar[:, co + C_R3:co + C_R3 + 1], 0)

        @block.vector
        def _(v, tt=tt):
            wait(v, f"qdone_q3_{tt}")
            v = TinyV(v, sV)
            v.scalar_tensor_tensor(CL(C_B3), CL(C_M3), -1.0, CL(C_R3),
                                   ALU.mult, ALU.mult)
            mark(f"st3_{tt}", sV)

        for qq in range(8):
            @block.scalar
            def _(sc, tt=tt, qq=qq):
                if qq == 0:
                    wait(sc, f"st3_{tt}")
                if qq >= 2:
                    wait(sc, f"ost_{tt}_{qq - 2}")
                elif tt >= 1:
                    wait(sc, f"ost_{tt - 1}_{6 + qq}")
                hs = slice(qq * 128, qq * 128 + 128)
                sA.inc(sc.activation(outt[qq % 2][:, :], ut[:, hs], AF.Identity,
                                     scale=CL(C_R3), bias=CL(C_B3)))
                mark(f"outd_{tt}_{qq}", sA)

            @block.gpsimd
            def _(g, tt=tt, qq=qq):
                wait(g, f"outd_{tt}_{qq}")
                hs = slice(qq * 128, qq * 128 + 128)
                sOUT.inc(g.dma_start(out=dy[tt * 128:(tt + 1) * 128, hs],
                                     in_=outt[qq % 2][:, :]), 16)
                mark(f"ost_{tt}_{qq}", sOUT)
                if tt == NT - 1 and qq == 7:
                    g.wait_ge(sOUT.h, 128 * NT)

    for tt in range(NT):
        emit_F1(tt)
        if tt >= 1:
            emit_Bp1(tt - 1)
        emit_F2(tt)
        if tt >= 2:
            emit_Bp2(tt - 2)
    emit_Bp1(NT - 1)
    emit_Bp2(NT - 2)
    emit_Bp2(NT - 1)

    es.close()
    return nc


_NC_CACHE = {}


def kernel(**inputs):
    x = np.ascontiguousarray(np.asarray(inputs["L0_attn_x"], dtype=np.float32))
    wqkv = np.ascontiguousarray(np.asarray(inputs["L0_attn_w_qkv"], dtype=np.float32))
    wout = np.ascontiguousarray(np.asarray(inputs["L0_attn_w_out"], dtype=np.float32))
    g1 = np.asarray(inputs["L0_ffn_ln1_g"], dtype=np.float32)
    b1g = np.asarray(inputs["L0_ffn_ln1_b"], dtype=np.float32)
    w1 = np.ascontiguousarray(np.asarray(inputs["L0_ffn_w1"], dtype=np.float32))
    b1 = np.asarray(inputs["L0_ffn_b1"], dtype=np.float32)
    g2 = np.asarray(inputs["L0_ffn_ln2_g"], dtype=np.float32)
    b2g = np.asarray(inputs["L0_ffn_ln2_b"], dtype=np.float32)
    w2 = np.ascontiguousarray(np.asarray(inputs["L0_ffn_w2"], dtype=np.float32))
    b2 = np.asarray(inputs["L0_ffn_b2"], dtype=np.float32)

    consts = _np_consts()
    if "nc" not in _NC_CACHE:
        _NC_CACHE["nc"] = build_nc()
    nc = _NC_CACHE["nc"]

    in_maps = []
    for c in range(8):
        b = c // 4
        q = c % 4
        m = np.zeros(8, np.float32)
        for r in range(8):
            if r // 4 == b and (r % 4) < q:
                m[r] = 1.0
        oh = np.zeros(8, np.float32)
        oh[c] = 1.0
        im = {
            "x": np.ascontiguousarray(x[b, q * T:(q + 1) * T, :]),
            "wqkv": wqkv,
            "wout": wout,
            "w1": w1,
            "w2": w2,
            "b1r": np.ascontiguousarray(b1.reshape(1, DFF)),
            "b2r": np.ascontiguousarray(b2.reshape(1, D)),
            "g1t": np.ascontiguousarray(np.broadcast_to(g1[None, :], (128, D))),
            "b1t": np.ascontiguousarray(np.broadcast_to(b1g[None, :], (128, D))),
            "g2t": np.ascontiguousarray(np.broadcast_to(g2[None, :], (128, DFF))),
            "b2t": np.ascontiguousarray(np.broadcast_to(b2g[None, :], (128, DFF))),
            "maskr": np.ascontiguousarray(np.broadcast_to(m[None, :], (128, 8))),
            "onehot": np.ascontiguousarray(np.broadcast_to(oh[None, :], (128, 8))),
        }
        im.update(consts)
        in_maps.append(im)

    res = run_bass_kernel_spmd(nc, in_maps, core_ids=list(range(8)))
    out = np.zeros((2, 8192, D), np.float32)
    for c in range(8):
        b = c // 4
        q = c % 4
        out[b, q * T:(q + 1) * T, :] = res.results[c]["y"]
    return out


# version 21
# speedup vs baseline: 1.0944x; 1.0233x over previous
import numpy as np
import ml_dtypes
import contextlib
import concourse.bass as bass
import concourse.mybir as mybir
from concourse.bass_utils import run_bass_kernel_spmd

F32 = mybir.dt.float32
BF16 = mybir.dt.bfloat16
I16 = mybir.dt.int16
AF = mybir.ActivationFunctionType
ALU = mybir.AluOpType
AXX = mybir.AxisListType.X

NPBF = ml_dtypes.bfloat16
T = 2048
D = 1024
DFF = 4096
H = 8
NT = 16
MAGIC = float(3 * 2.0 ** 22)
QC16 = float(0x5F37)
EPS = 1e-5


def _np_consts():
    N = 128
    n = np.arange(N)
    j65 = np.arange(65)
    jj = np.arange(1, 64)
    w = 2 * np.pi / N
    Rre = np.cos(w * np.outer(j65, n))
    Rim = np.zeros((64, N)); Rim[1:64] = -np.sin(w * np.outer(jj, n))
    cj = np.ones(65); cj[1:64] = 2.0
    Are = (cj[None, :] / N) * np.cos(w * np.outer(n, j65))
    Aim = np.zeros((N, 64)); Aim[:, 1:64] = (-2.0 / N) * np.sin(w * np.outer(n, jj))
    return {
        "cRreT": np.ascontiguousarray(Rre.T).astype(np.float32),
        "cRimT": np.ascontiguousarray(Rim.T).astype(np.float32),
        "cAreT": np.ascontiguousarray(Are.T).astype(np.float32),
        "cAimT": np.ascontiguousarray(Aim.T).astype(np.float32),
        "ident": np.eye(128, dtype=np.float32),
        "onesrow": np.ones((1, 128), dtype=np.float32),
        "onescol": np.ones((128, 1), dtype=np.float32),
    }


class TinyV:

    def __init__(self, v, sem):
        self._v = v
        self._s = sem

    def __getattr__(self, name):
        f = getattr(self._v, name)
        if not callable(f):
            return f

        def wrapped(*a, **k):
            r = f(*a, **k)
            self._s.inc(r)
            self._v.wait_ge(self._s.h, self._s.n)
            return r
        return wrapped


class Sem:
    def __init__(self, stack, nc, name):
        self.h = stack.enter_context(nc.semaphore(name))
        self.n = 0

    def inc(self, instr, k=1):
        instr.then_inc(self.h, k)
        self.n += k


DEBUG = False


def build_nc():
    nc = bass.Bass("TRN2", target_bir_lowering=False)
    es = contextlib.ExitStack()
    marks = {}

    def mark(name, sem):
        marks[name] = (sem.h, sem.n)

    def wait(eng, *names):
        for name in names:
            h, n = marks[name]
            if n > 0:
                eng.wait_ge(h, n)

    dx = nc.dram_tensor("x", [T, D], F32, kind="ExternalInput").ap()
    dwqkv = nc.dram_tensor("wqkv", [D, 3 * D], F32, kind="ExternalInput").ap()
    dwout = nc.dram_tensor("wout", [D, D], F32, kind="ExternalInput").ap()
    dw1 = nc.dram_tensor("w1", [DFF, D], F32, kind="ExternalInput").ap()
    dw2 = nc.dram_tensor("w2", [D, DFF], F32, kind="ExternalInput").ap()
    db1 = nc.dram_tensor("b1r", [1, DFF], F32, kind="ExternalInput").ap()
    db2 = nc.dram_tensor("b2r", [1, D], F32, kind="ExternalInput").ap()
    dg1 = nc.dram_tensor("g1t", [128, D], F32, kind="ExternalInput").ap()
    db1g = nc.dram_tensor("b1t", [128, D], F32, kind="ExternalInput").ap()
    dg2 = nc.dram_tensor("g2t", [128, DFF], F32, kind="ExternalInput").ap()
    db2g = nc.dram_tensor("b2t", [128, DFF], F32, kind="ExternalInput").ap()
    dmaskr = nc.dram_tensor("maskr", [128, 8], F32, kind="ExternalInput").ap()
    donehot = nc.dram_tensor("onehot", [128, 8], F32, kind="ExternalInput").ap()
    dconst = {k: nc.dram_tensor(k, list(v.shape), F32, kind="ExternalInput").ap()
              for k, v in _np_consts().items()}
    dy = nc.dram_tensor("y", [T, D], F32, kind="ExternalOutput").ap()
    KIND = "ExternalOutput" if DEBUG else "Internal"
    dtbl_in = nc.dram_tensor("tbl_in", [128, 128], F32).ap()
    dtbl_out = nc.dram_tensor("tbl_out", [128, 128], F32).ap()
    dw1q = nc.dram_tensor("w1q", [DFF, D], BF16, kind=KIND).ap()
    dw2q = nc.dram_tensor("w2q", [D, DFF], BF16, kind=KIND).ap()
    dvals = nc.dram_tensor("valsd", [D, T], BF16, kind=KIND).ap()
    dkvre = nc.dram_tensor("dkvre", [H * 65, T], BF16, kind=KIND).ap()
    dkvim = nc.dram_tensor("dkvim", [H * 64, T], BF16, kind=KIND).ap()

    def sb(name, shp, dt):
        return es.enter_context(nc.sbuf_tensor(name, shp, dt))

    wout_sb = sb("wout_sb", [128, 8 * D], BF16)
    swcols = sb("swcols", [128, 2], F32)
    b1row = es.enter_context(nc.sbuf_tensor("b1row", [1, DFF], BF16, side="right"))
    b2row = es.enter_context(nc.sbuf_tensor("b2row", [1, D], BF16, side="right"))
    rowar = sb("rowar", [1, 512], BF16)
    colar = sb("colar", [128, 90], F32)
    colar2 = sb("colar2", [128, 48], F32)
    srowcols = sb("srowcols", [128, 4], BF16)
    qk_vb = sb("qk_vb", [128, 2], BF16)
    qk_bi = sb("qk_bi", [128, 2], F32)
    qk_v64 = sb("qk_v64", [128, 64], F32)
    qk_sd64 = sb("qk_sd64", [128, 64], BF16)
    qk_t64a = sb("qk_t64a", [128, 64], F32)
    qk_y64a = sb("qk_y64a", [128, 64], F32)
    qk_y64b = sb("qk_y64b", [128, 64], F32)


    sG = Sem(es, nc, "sG")
    sXL = Sem(es, nc, "sXL")
    sXT = Sem(es, nc, "sXT")
    sWL = Sem(es, nc, "sWL")
    sWQL = Sem(es, nc, "sWQL")
    sWS1 = Sem(es, nc, "sWS1")
    sWS2 = Sem(es, nc, "sWS2")
    sKVS = Sem(es, nc, "sKVS")
    sKVL = Sem(es, nc, "sKVL")
    sVST = Sem(es, nc, "sVST")
    sTBL = Sem(es, nc, "sTBL")
    sGB = Sem(es, nc, "sGB")
    sVLD = Sem(es, nc, "sVLD")
    sXIN = Sem(es, nc, "sXIN")
    sWT1 = Sem(es, nc, "sWT1")
    sWT2 = Sem(es, nc, "sWT2")
    sTP = Sem(es, nc, "sTP")
    sDBG = Sem(es, nc, "sDBG")
    sMM = Sem(es, nc, "sMM")
    sA = Sem(es, nc, "sA")
    sV = Sem(es, nc, "sV")
    sGP = Sem(es, nc, "sGP")
    sCC = Sem(es, nc, "sCC")
    sOUT = Sem(es, nc, "sOUT")
    sWLs = [Sem(es, nc, f"sWLs{k}") for k in range(4)]
    sWQs = [Sem(es, nc, f"sWQs{k}") for k in range(4)]
    sWT2e = [Sem(es, nc, f"sWT2e{k}") for k in range(8)]
    sG2 = [Sem(es, nc, f"sG2{k}") for k in range(4)]
    sOUT2 = Sem(es, nc, "sOUT2")
    sB2 = [Sem(es, nc, f"sB2{k}") for k in range(4)]

    esA1 = contextlib.ExitStack()

    def sbA1(name, shp, dt):
        return esA1.enter_context(nc.sbuf_tensor(name, shp, dt, side="right"))
    wkv = sbA1("wkv", [128, 8 * T], BF16)
    wstg = [sbA1(f"wstg{i}", [128, D], F32) for i in range(4)]
    wqf32 = sbA1("wqf32", [128, D], F32)
    wqi16 = [sbA1(f"wqi16{i}", [128, D], I16) for i in range(2)]
    dqwn = [sbA1(f"dqwn{i}", [128, D], BF16) for i in range(2)]
    kh = sbA1("kh", [128, T], BF16)
    vh = sbA1("vh", [128, T], BF16)
    xtmp = [sbA1(f"xtmp{i}", [128, D], BF16) for i in range(2)]
    fkre = sbA1("fkre", [65, T], BF16)

    esA2 = contextlib.ExitStack()

    def sbA2(name, shp, dt):
        return esA2.enter_context(nc.sbuf_tensor(name, shp, dt))
    xT = sbA2("xT", [128, 8 * T], BF16)
    crre = sbA2("crre", [128, 65], BF16)
    crim = sbA2("crim", [128, 64], BF16)
    care = sbA2("care", [65, 128], BF16)
    caim = sbA2("caim", [64, 128], BF16)
    onesrow_f = sbA2("onesrow_f", [1, 128], F32)
    onescol_f = sbA2("onescol_f", [128, 1], F32)
    maskr = sbA2("maskr_sb", [128, 8], F32)
    onehot = sbA2("onehot_sb", [128, 8], F32)
    Smtx = sbA2("Smtx", [128, 16], F32)
    Sr64 = sbA2("Sr64", [128, 8], F32)
    P_sb = sbA2("P_sb", [128, 16], F32)
    wacc = sbA2("wacc", [128, 64], F32)
    wcolsum = sbA2("wcolsum", [128, 2], F32)
    swv = sbA2("swv", [1, 2], F32)
    cont = sbA2("cont", [128, 128], F32)
    G = sbA2("G_sb", [128, 128], F32)
    wq = sbA2("wq", [128, 8 * D], BF16)
    scrB = sbA2("scrB", [64, T], BF16)
    fqre = [sbA2(f"fqre{i}", [65, T], BF16) for i in range(2)]
    fqim = [sbA2(f"fqim{i}", [64, T], BF16) for i in range(2)]
    valstg = sbA2("valstg", [128, T], BF16)
    qh = sbA2("qh", [128, T], BF16)

    esA3 = contextlib.ExitStack()
    fkim = esA3.enter_context(nc.sbuf_tensor("fkim", [64, T], BF16))
    fvre = esA3.enter_context(nc.sbuf_tensor("fvre", [65, T], BF16))
    fvim = esA3.enter_context(nc.sbuf_tensor("fvim", [64, T], BF16))
    kvre_p1 = esA3.enter_context(nc.sbuf_tensor("kvre_p1", [65, T], BF16))
    kvim_p1 = esA3.enter_context(nc.sbuf_tensor("kvim_p1", [64, T], BF16))
    esA3b = contextlib.ExitStack()
    kvre = kvim = None

    esAP = contextlib.ExitStack()

    def psA(name, shp, dt):
        return esAP.enter_context(nc.psum_tensor(name, shp, dt))
    pk = psA("pk", [128, 512], F32)
    pv = psA("pv", [128, 512], F32)
    pfre = [psA(f"pfre{i}", [65, 512], F32) for i in range(2)]
    pfim = [psA(f"pfim{i}", [64, 512], F32) for i in range(2)]
    pmisc = psA("pmisc", [128, 512], F32)

    block = es.enter_context(nc.Block())
    if True:
        @block.scalar
        def _(sc):
            sc.preload_act_table(AF.Gelu)

        @block.gpsimd
        def _(g):
            g.memset(Smtx[:, :], 0.0)
            g.memset(Sr64[:, :], 0.0)
            for pad in (qk_v64, qk_t64a, qk_y64a, qk_y64b):
                g.memset(pad[:, :], 0.0)
            g.memset(qk_sd64[:, :], 0.0)
            for nm, dst in (("cRreT", crre), ("cRimT", crim), ("cAreT", care),
                            ("cAimT", caim)):
                sG.inc(g.dma_start(out=dst[:, :], in_=dconst[nm][:, :]), 16)
            sG.inc(g.dma_start(out=onesrow_f[:, :], in_=dconst["onesrow"][:, :]), 16)
            sG.inc(g.dma_start(out=onescol_f[:, :], in_=dconst["onescol"][:, :]), 16)
            sG.inc(g.dma_start(out=maskr[:, :], in_=dmaskr[:, :]), 16)
            sG.inc(g.dma_start(out=onehot[:, :], in_=donehot[:, :]), 16)
            sG.inc(g.dma_start(out=b1row[:, :], in_=db1[:, :]), 16)
            sG.inc(g.dma_start(out=b2row[:, :], in_=db2[:, :]), 16)
            mark("ld_consts", sG)
            for dc in range(8):
                sG.inc(g.dma_start(out=wkv[:, dc * T:(dc + 1) * T],
                                   in_=dwqkv[dc * 128:(dc + 1) * 128, D:3 * D]), 16)
            mark("ld_wkv", sG)
            pass

    for tt in range(NT):
        if True:
            @block.gpsimd
            def _(g, tt=tt):
                if tt >= 2:
                    g.wait_ge(sXT.h, 16 * (tt - 1))
                sXL.inc(g.dma_start(out=xtmp[tt % 2][:, :],
                                    in_=dx[tt * 128:(tt + 1) * 128, :]), 16)
                mark(f"xld_{tt}", sXL)

            @block.sync
            def _(s, tt=tt):
                s.wait_ge(sXL.h, 16 * min(tt + 2, NT))
                sXT.inc(s.dma_start_transpose(
                    xT[:, :].rearrange("p (c u t) -> p c u t", c=8, u=NT)[:, :, tt, :],
                    xtmp[tt % 2][:, :].rearrange("p (c d) -> p c d", c=8)), 16)
                mark(f"xT_{tt}", sXT)
                if tt == NT - 1:
                    mark("xT_all", sXT)

    if True:
        @block.gpsimd
        def _(g):
            for dc in range(8):
                sG.inc(g.dma_start(out=wq[:, dc * D:(dc + 1) * D],
                                   in_=dwqkv[dc * 128:(dc + 1) * 128, 0:D]), 16)
            mark("ld_wq", sG)
            for dc in range(8):
                sG.inc(g.dma_start(out=wout_sb[:, dc * D:(dc + 1) * D],
                                   in_=dwout[dc * 128:(dc + 1) * 128, :]), 16)
            mark("ld_wout", sG)
            mark("ld_all", sG)

    def _wsrc(i):
        if i < 32:
            return dw1[i * 128:(i + 1) * 128, :]
        j = i - 32
        return dw2[(j // 4) * 128:(j // 4 + 1) * 128,
                   (j % 4) * 1024:(j % 4 + 1) * 1024]

    def emit_wscan(u):
        @block.gpsimd
        def _(g, u=u):
            i = 2 * u
            if i >= 4:
                wait(g, f"wred_{i - 4}")
            sWLs[i % 4].inc(g.dma_start(out=wstg[i % 4][:, :], in_=_wsrc(i)), 16)

        @block.sync
        def _(s, u=u):
            i = 2 * u + 1
            if i >= 4:
                wait(s, f"wred_{i - 4}")
            sWLs[i % 4].inc(s.dma_start(out=wstg[i % 4][:, :], in_=_wsrc(i)), 16)

        @block.vector
        def _(v, u=u):
            for i in (2 * u, 2 * u + 1):
                v.wait_ge(sWLs[i % 4].h, 16 * (i // 4 + 1))
                sV.inc(v.tensor_reduce(wacc[:, i:i + 1], wstg[i % 4][:, :], AXX,
                                       ALU.add, apply_absolute_value=True))
                mark(f"wred_{i}", sV)
                if i == 63:
                    mark("wred_all", sV)

    def emit_sw_chain():
        @block.vector
        def _(v):
            wait(v, "wred_all")
            v.tensor_reduce(wcolsum[:, 0:1], wacc[:, 0:32], AXX, ALU.add)
            sV.inc(v.tensor_reduce(wcolsum[:, 1:2], wacc[:, 32:64], AXX, ALU.add))
            mark("wcolsum", sV)

        @block.tensor
        def _(t):
            wait(t, "wcolsum", "ld_consts")
            t.matmul(pmisc[0:1, 0:1], wcolsum[:, 0:1], onescol_f[:, :],
                     start=True, stop=True)
            sMM.inc(t.matmul(pmisc[0:1, 1:2], wcolsum[:, 1:2], onescol_f[:, :],
                             start=True, stop=True))
            mark("wtot_mm", sMM)

        @block.scalar
        def _(sc):
            wait(sc, "wtot_mm")
            sA.inc(sc.copy(swv[0:1, 0:2], pmisc[0:1, 0:2]))
            mark("wtot_ev", sA)

        @block.tensor
        def _(t):
            wait(t, "wtot_ev")
            sMM.inc(t.matmul(pmisc[:, 8:10], onesrow_f[0:1, :], swv[0:1, 0:2],
                             start=True, stop=True))
            mark("swb_mm", sMM)

        @block.scalar
        def _(sc):
            wait(sc, "swb_mm")
            sA.inc(sc.copy(swcols[:, 0:2], pmisc[:, 8:10]))
            mark("sw_bc", sA)

        @block.vector
        def _(v):
            wait(v, "sw_bc")
            tv = TinyV(v, sV)
            tv.tensor_scalar(wcolsum[:, 0:2], swcols[:, 0:2], 1.0 / (DFF * D), 1e-5,
                             ALU.mult, ALU.max)
            sV.inc(v.reciprocal(swcols[:, 0:2], wcolsum[:, 0:2]))
            mark("sw_cols", sV)

    def emit_wquant(i):
        if True:
            @block.sync
            def _(s, i=i):
                if i == 0:
                    wait(s, "sw_cols")
                if i >= 4:
                    wait(s, f"wq_rd_{i - 4}")
                if i < 32:
                    src = dw1[i * 128:(i + 1) * 128, :]
                else:
                    j = i - 32
                    src = dw2[(j // 4) * 128:(j // 4 + 1) * 128,
                              (j % 4) * 1024:(j % 4 + 1) * 1024]
                sWQs[i % 4].inc(s.dma_start(out=wstg[i % 4][:, :], in_=src), 16)

            @block.scalar
            def _(sc, i=i):
                sc.wait_ge(sWQs[i % 4].h, 16 * (i // 4 + 1))
                if i >= 2:
                    wait(sc, f"wq_cl_{i - 2}")
                swsl = swcols[:, 0:1] if i < 32 else swcols[:, 1:2]
                sc.activation(wqf32[:, :], wstg[i % 4][:, :], AF.Copy,
                              scale=swsl, bias=MAGIC)
                sA.inc(sc.activation(wqi16[i % 2][:, :], wqf32[:, :], AF.Copy, bias=-MAGIC))
                mark(f"wq_rd_{i}", sA)

            @block.vector
            def _(v, i=i):
                wait(v, f"wq_rd_{i}")
                if i >= 2:
                    v.wait_ge(sWS1.h, 16 * min(i, 32))
                    if i > 32:
                        v.wait_ge(sWS2.h, 16 * (i - 32))
                sV.inc(v.tensor_scalar(dqwn[i % 2][:, :], wqi16[i % 2][:, :], -1.0, 1.0,
                                       ALU.max, ALU.min))
                mark(f"wq_cl_{i}", sV)

            @block.gpsimd
            def _(g, i=i):
                wait(g, f"wq_cl_{i}")
                if i < 32:
                    dst = dw1q[i * 128:(i + 1) * 128, :]
                    sWS1.inc(g.dma_start(out=dst, in_=dqwn[i % 2][:, :]), 16)
                else:
                    j = i - 32
                    dst = dw2q[(j // 4) * 128:(j // 4 + 1) * 128,
                               (j % 4) * 1024:(j % 4 + 1) * 1024]
                    sWS2.inc(g.dma_start(out=dst, in_=dqwn[i % 2][:, :]), 16)
                if i == 31:
                    mark("w1q_done", sWS1)
                if i == 63:
                    mark("w2q_done", sWS2)

    for h in range(H):
        par = 0
        for tb in range(4):
            if True:
                @block.tensor
                def _(t, h=h, tb=tb):
                    if h == 0:
                        wait(t, f"xT_{4 * tb + 3}")
                    if h == 0 and tb == 0:
                        wait(t, "ld_wkv")
                    else:
                        wait(t, f"evkv_{h}_{tb - 1}" if tb > 0 else f"evkv_{h - 1}_3")
                    for tau, ps in (("k", pk), ("v", pv)):
                        col0 = (0 if tau == "k" else D) + h * 128
                        for dc in range(8):
                            ins = t.matmul(ps[:, :],
                                           wkv[:, dc * T + col0:dc * T + col0 + 128],
                                           xT[:, dc * T + tb * 512:dc * T + tb * 512 + 512],
                                           start=(dc == 0), stop=(dc == 7))
                            if tau == "v" and dc == 7:
                                sMM.inc(ins)
                    mark(f"mmkv_{h}_{tb}", sMM)

                @block.scalar
                def _(sc, h=h, tb=tb):
                    wait(sc, f"mmkv_{h}_{tb}")
                    if h > 0 and tb == 0:
                        wait(sc, f"kvm_{h - 1}")
                    tsl = slice(tb * 512, tb * 512 + 512)
                    sc.copy(kh[:, tsl], pk[:, :])
                    sA.inc(sc.copy(vh[:, tsl], pv[:, :]))
                    mark(f"evkv_{h}_{tb}", sA)

                @block.tensor
                def _(t, h=h, tb=tb):
                    wait(t, f"evkv_{h}_{tb}")
                    if h > 0:
                        wait(t, f"evdv_{h - 1}_{tb}")
                    tsl = slice(tb * 512, tb * 512 + 512)
                    t.matmul(pfre[tb % 2][:, :], crre[:, :], kh[:, tsl], start=True, stop=True)
                    sMM.inc(t.matmul(pfim[tb % 2][:, :], crim[:, :], kh[:, tsl],
                                     start=True, stop=True))
                    mark(f"mmdk_{h}_{tb}", sMM)

                @block.scalar
                def _(sc, h=h, tb=tb):
                    tsl = slice(tb * 512, tb * 512 + 512)
                    wait(sc, f"mmdk_{h}_{tb}")
                    if h > 0 and tb == 0:
                        wait(sc, f"kvm_{h - 1}")
                    sc.copy(fkre[:, tsl], pfre[tb % 2][:, :])
                    sA.inc(sc.copy(fkim[:, tsl], pfim[tb % 2][:, :]))
                    mark(f"evdk_{h}_{tb}", sA)

                @block.tensor
                def _(t, h=h, tb=tb):
                    wait(t, f"evdk_{h}_{tb}")
                    tsl = slice(tb * 512, tb * 512 + 512)
                    t.matmul(pfre[tb % 2][:, :], crre[:, :], vh[:, tsl], start=True, stop=True)
                    sMM.inc(t.matmul(pfim[tb % 2][:, :], crim[:, :], vh[:, tsl],
                                     start=True, stop=True))
                    mark(f"mmdv_{h}_{tb}", sMM)

                @block.scalar
                def _(sc, h=h, tb=tb):
                    tsl = slice(tb * 512, tb * 512 + 512)
                    wait(sc, f"mmdv_{h}_{tb}")
                    sc.copy(fvre[:, tsl], pfre[tb % 2][:, :])
                    sA.inc(sc.copy(fvim[:, tsl], pfim[tb % 2][:, :]))
                    mark(f"evdv_{h}_{tb}", sA)

            emit_wscan(h * 4 + tb)

        if True:
            @block.vector
            def _(v, h=h, par=par):
                wait(v, f"evdv_{h}_3")
                if h >= 1:
                    wait(v, f"kvst_{h - 1}")
                v.scalar_tensor_tensor(kvre_p1[:, :], fkre[:, :], 1.0, fvre[:, :],
                                       ALU.bypass, ALU.mult,
                                       accum_out=Sr64[0:65, h:h + 1])
                v.tensor_tensor(scrB[:, :], fkim[:, :], fvim[:, :], ALU.mult)
                v.scalar_tensor_tensor(kvre_p1[0:64, :], scrB[:, :], -1.0,
                                       kvre_p1[0:64, :], ALU.mult, ALU.add,
                                       accum_out=Smtx[0:64, 2 * h:2 * h + 1])
                v.tensor_tensor(scrB[:, :], fkre[0:64, :], fvim[:, :], ALU.mult)
                v.tensor_tensor(kvim_p1[:, :], fkim[:, :], fvre[0:64, :], ALU.mult)
                sV.inc(v.scalar_tensor_tensor(kvim_p1[:, :], scrB[:, :], 1.0,
                                              kvim_p1[:, :], ALU.bypass, ALU.add,
                                              accum_out=Smtx[0:64, 2 * h + 1:2 * h + 2]))
                mark(f"kvm_{h}", sV)

            @block.scalar
            def _(sc, h=h):
                wait(sc, f"kvm_{h}")
                sA.inc(sc.copy(Smtx[64:65, 2 * h:2 * h + 1], Sr64[64:65, h:h + 1]))
                mark(f"sfix_{h}", sA)

            @block.gpsimd
            def _(g, h=h, par=par):
                wait(g, f"kvm_{h}")
                sKVS.inc(g.dma_start(out=dkvre[h * 65:(h + 1) * 65, :], in_=kvre_p1[:, :]), 16)
                sKVS.inc(g.dma_start(out=dkvim[h * 64:(h + 1) * 64, :], in_=kvim_p1[:, :]), 16)
                mark(f"kvst_{h}", sKVS)

    esA3.close()
    kvre = [esA3b.enter_context(nc.sbuf_tensor(f"kvre{i}", [65, T], BF16)) for i in range(2)]
    kvim = [esA3b.enter_context(nc.sbuf_tensor(f"kvim{i}", [64, T], BF16)) for i in range(2)]

    emit_sw_chain()

    if True:
        @block.vector
        def _(v):
            wait(v, "sfix_7")
            last = None
            for r in range(8):
                last = v.tensor_scalar(cont[:, r * 16:(r + 1) * 16], Smtx[:, :],
                                       onehot[:, r:r + 1], None, ALU.mult)
            sV.inc(last)
            mark("cont_ready", sV)

        @block.gpsimd
        def _(g):
            wait(g, "cont_ready")
            sTBL.inc(g.dma_start(out=dtbl_in[:, :], in_=cont[:, :]), 16)
            g.wait_ge(sTBL.h, sTBL.n)
            sCC.inc(g.collective_compute(
                "AllReduce", ALU.add, replica_groups=[list(range(8))],
                ins=[dtbl_in[:, :].opt()], outs=[dtbl_out[:, :].opt()]))
            g.wait_ge(sCC.h, sCC.n)
            sTBL.inc(g.dma_start(out=G[:, :], in_=dtbl_out[:, :]), 16)
            mark("G_ready", sTBL)

        @block.vector
        def _(v):
            wait(v, "G_ready")
            tv = TinyV(v, sV)
            tv.tensor_scalar(P_sb[:, :], G[:, 0:16], maskr[:, 0:1], None, ALU.mult)
            for r in range(1, 8):
                tv.scalar_tensor_tensor(P_sb[:, :], G[:, r * 16:(r + 1) * 16],
                                        maskr[:, r:r + 1], P_sb[:, :],
                                        ALU.mult, ALU.add)
            sV.inc(v.engine_nop())
            mark("P_ready", sV)

    def emit_p2F(h):
        par = h % 2
        for tb in range(4):
            @block.tensor
            def _(t, h=h, tb=tb):
                if h == 0 and tb == 0:
                    wait(t, "ld_wq")
                else:
                    wait(t, f"evq_{h}_{tb - 1}" if tb > 0 else f"evq_{h - 1}_3")
                for dc in range(8):
                    ins = t.matmul(pk[:, :],
                                   wq[:, dc * D + h * 128:dc * D + h * 128 + 128],
                                   xT[:, dc * T + tb * 512:dc * T + tb * 512 + 512],
                                   start=(dc == 0), stop=(dc == 7))
                sMM.inc(ins)
                mark(f"mmq_{h}_{tb}", sMM)

            @block.scalar
            def _(sc, h=h, tb=tb):
                wait(sc, f"mmq_{h}_{tb}")
                if h > 0 and tb == 0:
                    wait(sc, f"mmfq_{h - 1}_3")
                sA.inc(sc.copy(qh[:, tb * 512:tb * 512 + 512], pk[:, :]))
                mark(f"evq_{h}_{tb}", sA)

            @block.tensor
            def _(t, h=h, tb=tb):
                wait(t, f"evq_{h}_{tb}")
                tsl = slice(tb * 512, tb * 512 + 512)
                t.matmul(pfre[tb % 2][:, :], crre[:, :], qh[:, tsl], start=True, stop=True)
                sMM.inc(t.matmul(pfim[tb % 2][:, :], crim[:, :], qh[:, tsl],
                                 start=True, stop=True))
                mark(f"mmfq_{h}_{tb}", sMM)

            @block.scalar
            def _(sc, h=h, tb=tb, par=par):
                wait(sc, f"mmfq_{h}_{tb}")
                if h > 1 and tb == 0:
                    wait(sc, f"outf_{h - 2}")
                tsl = slice(tb * 512, tb * 512 + 512)
                sc.copy(fqre[par][:, tsl], pfre[tb % 2][:, :])
                sA.inc(sc.copy(fqim[par][:, tsl], pfim[tb % 2][:, :]))
                mark(f"evfq_{h}_{tb}", sA)

    def emit_p2B(h):
        par = h % 2
        if True:
            @block.gpsimd
            def _(g, h=h, par=par):
                wait(g, "kvst_7")
                if h >= 2:
                    wait(g, f"outf_{h - 2}")
                sKVL.inc(g.dma_start(out=kvre[par][:, :], in_=dkvre[h * 65:(h + 1) * 65, :]), 16)
                sKVL.inc(g.dma_start(out=kvim[par][:, :], in_=dkvim[h * 64:(h + 1) * 64, :]), 16)
                mark(f"kvld_{h}", sKVL)

            @block.vector
            def _(v, h=h, par=par):
                wait(v, "P_ready")
                wait(v, f"kvld_{h}", f"evfq_{h}_3")
                v.tensor_tensor_scan(kvre[par][:, :], kvre[par][:, :], kvre[par][:, :],
                                     P_sb[0:65, 2 * h:2 * h + 1], ALU.add, ALU.bypass)
                v.tensor_tensor_scan(kvim[par][:, :], kvim[par][:, :], kvim[par][:, :],
                                     P_sb[0:64, 2 * h + 1:2 * h + 2], ALU.add, ALU.bypass)
                v.tensor_tensor(scrB[:, :], kvim[par][:, :], fqre[par][0:64, :], ALU.mult)
                v.tensor_tensor(fqre[par][:, :], kvre[par][:, :], fqre[par][:, :], ALU.mult)
                v.tensor_tensor(kvim[par][:, :], kvim[par][:, :], fqim[par][:, :], ALU.mult)
                v.tensor_tensor(fqre[par][0:64, :], kvim[par][:, :], fqre[par][0:64, :], ALU.add)
                v.tensor_tensor(fqim[par][:, :], kvre[par][0:64, :], fqim[par][:, :], ALU.mult)
                sV.inc(v.tensor_tensor(fqim[par][:, :], scrB[:, :], fqim[par][:, :],
                                       ALU.subtract))
                mark(f"outf_{h}", sV)

        for tb in range(4):
            @block.tensor
            def _(t, h=h, tb=tb, par=par):
                wait(t, f"outf_{h}")
                if not (h == 0 and tb == 0):
                    wait(t, f"evval_{h}_{tb - 1}" if tb > 0 else f"evval_{h - 1}_3")
                tsl = slice(tb * 512, tb * 512 + 512)
                t.matmul(pmisc[:, :], care[:, :], fqre[par][:, tsl], start=True, stop=False)
                sMM.inc(t.matmul(pmisc[:, :], caim[:, :], fqim[par][:, tsl],
                                 start=False, stop=True))
                mark(f"mmval_{h}_{tb}", sMM)

            @block.scalar
            def _(sc, h=h, tb=tb):
                wait(sc, f"mmval_{h}_{tb}")
                if tb == 0 and h > 0:
                    wait(sc, f"valst_{h - 1}")
                sA.inc(sc.copy(valstg[:, tb * 512:tb * 512 + 512], pmisc[:, :]))
                mark(f"evval_{h}_{tb}", sA)

        if True:
            @block.gpsimd
            def _(g, h=h):
                wait(g, f"evval_{h}_3")
                sVST.inc(g.dma_start(out=dvals[h * 128:(h + 1) * 128, :], in_=valstg[:, :]), 16)
                mark(f"valst_{h}", sVST)

        for k in range(8):
            emit_wquant(8 * h + k)

    for h in range(H):
        emit_p2F(h)
        if h >= 1:
            emit_p2B(h - 1)
    emit_p2B(H - 1)

    marks["vals_all"] = marks["valst_7"]

    if DEBUG:
        ddbg = nc.dram_tensor("dbg1", [128, 256], F32, kind="ExternalOutput").ap()
        if True:
            @block.gpsimd
            def _(g):
                wait(g, "sw_cols", "G_ready", "P_ready", "sfix_7", "wred_all")
                sDBG.inc(g.dma_start(out=ddbg[:, 0:64], in_=wacc[:, :]), 16)
                sDBG.inc(g.dma_start(out=ddbg[:, 64:66], in_=wcolsum[:, :]), 16)
                sDBG.inc(g.dma_start(out=ddbg[:, 66:68], in_=swcols[:, :]), 16)
                sDBG.inc(g.dma_start(out=ddbg[:, 68:84], in_=P_sb[:, :]), 16)
                sDBG.inc(g.dma_start(out=ddbg[:, 84:100], in_=Smtx[:, :]), 16)
                sDBG.inc(g.dma_start(out=ddbg[0:1, 100:102], in_=swv[:, :]), 16)
                sDBG.inc(g.dma_start(out=ddbg[:, 102:230], in_=G[:, :]), 16)
                g.wait_ge(sDBG.h, sDBG.n)

    esA3b.close()
    esA1.close()
    w1T = es.enter_context(nc.sbuf_tensor("w1T", [128, 8 * DFF], BF16, side="right"))

    if True:
        @block.sync
        def _(s):
            wait(s, "w1q_done", "w2q_done", "kvm_7", "xT_all", "mmdv_7_3")
            for dc in range(0, 8, 2):
                sWT1.inc(s.dma_start_transpose(w1T[:, dc * DFF:(dc + 1) * DFF],
                                               dw1q[:, dc * 128:(dc + 1) * 128]), 16)

        @block.sync
        def _(s):
            for dc in range(1, 8, 2):
                sWT1.inc(s.dma_start_transpose(w1T[:, dc * DFF:(dc + 1) * DFF],
                                               dw1q[:, dc * 128:(dc + 1) * 128]), 16)
        marks["w1T_ready"] = (sWT1.h, sWT1.n)

    esA2.close()
    esAP.close()
    w2T = es.enter_context(nc.sbuf_tensor("w2T", [128, 32 * D], BF16))
    g1tb = es.enter_context(nc.sbuf_tensor("g1tb", [128, D], BF16))
    b1tb = es.enter_context(nc.sbuf_tensor("b1tb", [128, D], BF16))
    g2s = [es.enter_context(nc.sbuf_tensor(f"g2s{i}", [128, 256], BF16, side="right")) for i in range(4)]
    b2s = [es.enter_context(nc.sbuf_tensor(f"b2s{i}", [128, 256], BF16, side="right")) for i in range(4)]
    valsin = es.enter_context(nc.sbuf_tensor("valsin", [128, D], BF16))
    xin = es.enter_context(nc.sbuf_tensor("xin", [128, D], BF16))
    y0t = [es.enter_context(nc.sbuf_tensor(f"y0t{i}", [128, D], BF16)) for i in range(3)]
    utt = [es.enter_context(nc.sbuf_tensor(f"ut{i}", [128, D], BF16)) for i in range(3)]
    dq1Tt = [es.enter_context(nc.sbuf_tensor(f"dq1T{i}", [128, D], BF16, side="right")) for i in range(2)]
    hga = [es.enter_context(nc.sbuf_tensor(f"hga{i}", [128, DFF], BF16)) for i in range(2)]
    hscr1 = es.enter_context(nc.sbuf_tensor("hscr", [128, 512], BF16))
    hscr = [hscr1, hscr1]
    fst = es.enter_context(nc.sbuf_tensor("fst", [128, D], BF16, side="right"))
    dq2Tp = [es.enter_context(nc.sbuf_tensor(f"dq2Tp{i}", [128, 512], BF16)) for i in range(2)]
    qi16one = es.enter_context(nc.sbuf_tensor("qi16c", [128, 512], I16, side="right"))
    qi16t = [qi16one, qi16one]
    outt = [es.enter_context(nc.sbuf_tensor(f"outt{i}", [128, 128], F32, side="right")) for i in range(2)]
    dq1 = xin

    pA = es.enter_context(nc.psum_tensor("pA", [128, 512], F32))
    pB = es.enter_context(nc.psum_tensor("pB", [128, 512], F32))
    pC = es.enter_context(nc.psum_tensor("pC", [128, 512], F32))
    pD = es.enter_context(nc.psum_tensor("pD", [128, 512], F32))
    pE = es.enter_context(nc.psum_tensor("pE", [128, 512], F32))
    pF = es.enter_context(nc.psum_tensor("pF", [128, 512], F32))
    pG = es.enter_context(nc.psum_tensor("pG", [128, 512], F32))

    A_DONE = ("outf_7", "evval_7_3", "mmval_7_3", "vals_all")

    if True:
        @block.gpsimd
        def _(g):
            wait(g, *A_DONE)
            sGB.inc(g.dma_start(out=g1tb[:, :], in_=dg1[:, :]), 16)
            sGB.inc(g.dma_start(out=b1tb[:, :], in_=db1g[:, :]), 16)
            mark("gb_ready", sGB)

        @block.sync
        def _(s):
            wait(s, "w2q_done", *A_DONE)
            for e in range(8):
                for j in range(4):
                    hc = 4 * e + j
                    sWT2e[e].inc(s.dma_start_transpose(w2T[:, hc * D:(hc + 1) * D],
                                                       dw2q[:, hc * 128:(hc + 1) * 128]), 16)
        for e in range(8):
            marks[f"w2Tg_{e}"] = (sWT2e[e].h, 64)

    C_SUM0, C_M0, C_V0, C_USC, C_R0, C_SIG0, C_UB, C_QA, C_QB = 0, 1, 2, 3, 4, 5, 6, 7, 8
    C_MAX1, C_S1, C_S1S, C_AL1, C_TMP = 9, 10, 11, 12, 13
    C_GS, C_GSQ, C_M2, C_V2, C_R2, C_UB2, C_Q2 = 14, 15, 16, 17, 18, 19, 20
    C_MAX2, C_S2, C_S2S, C_BETA = 21, 22, 23, 24
    C_M3, C_V3, C_R3, C_B3, C_Q3 = 25, 26, 27, 28, 29

    def quake(tag, vin, ncols, vout, qo):
        blk = block
        cs = slice(0, ncols)
        if True:
            @blk.scalar
            def _(sc):
                wait(sc, f"qpre_{tag}")
                sc.copy(qk_vb[:, cs], vin)
                sA.inc(sc.copy(qk_v64[:, cs], vin))
                mark(f"qa1_{tag}", sA)

            @blk.vector
            def _(v):
                wait(v, f"qa1_{tag}")
                sV.inc(v.tensor_single_scalar(qk_bi[:, cs],
                                              qk_vb[:, cs].bitcast(I16), 0.0, ALU.add))
                mark(f"qv1_{tag}", sV)

            @blk.scalar
            def _(sc):
                wait(sc, f"qv1_{tag}")
                sA.inc(sc.activation(qk_sd64[:, cs].bitcast(I16), qk_bi[:, cs],
                                     AF.Copy, scale=-0.5, bias=QC16))
                mark(f"qa2_{tag}", sA)

            @blk.vector
            def _(v):
                wait(v, f"qa2_{tag}")
                w64 = slice(0, 64)
                v.tensor_single_scalar(qk_y64a[:, w64], qk_sd64[:, w64], 0.0, ALU.add)
                cur, nxt = qk_y64a, qk_y64b
                for _i in range(3):
                    v.tensor_tensor(qk_t64a[:, w64], qk_v64[:, w64], cur[:, w64], ALU.mult)
                    v.tensor_tensor(nxt[:, w64], qk_t64a[:, w64], cur[:, w64], ALU.mult)
                    v.tensor_scalar(qk_t64a[:, w64], nxt[:, w64], -0.5, 1.5,
                                    ALU.mult, ALU.add)
                    v.tensor_tensor(nxt[:, w64], cur[:, w64], qk_t64a[:, w64], ALU.mult)
                    cur, nxt = nxt, cur
                sV.inc(v.tensor_single_scalar(vout, cur[:, cs], 0.0, ALU.add))
                v.wait_ge(sV.h, sV.n)
                mark(f"qdone_{tag}", sV)

    sTP1 = [Sem(es, nc, f"sTP1{i}") for i in range(2)]
    sROW1 = [Sem(es, nc, f"sROW1{i}") for i in range(2)]
    sROW2 = [Sem(es, nc, f"sROW2{i}") for i in range(2)]
    sT2e = [[Sem(es, nc, f"sT2e{k}_{i}") for i in range(2)] for k in range(8)]

    def _ctx(tt):
        p3 = tt % 3
        co = 30 * p3
        c2o = 16 * p3
        return (tt % 2,
                lambda c: colar[:, co + c:co + c + 1],
                lambda c: colar2[:, c2o + c:c2o + c + 1],
                256 * (tt % 2), 0,
                y0t[p3], utt[p3], dq1Tt[tt % 2], fst, qi16t[tt % 2])

    def emit_F1(tt):
        p2, CL, C2, rof, rofp, y0, ut, dq1T, fst_, qi16c = _ctx(tt)
        co = 30 * (tt % 3)

        @block.gpsimd
        def _(g, tt=tt):
            if tt == 0:
                wait(g, *A_DONE)
            if tt >= 1:
                wait(g, f"woutmm_{tt - 1}_1")
            sVLD.inc(g.dma_start(
                out=valsin[:, :].rearrange("d (h t) -> d h t", h=8),
                in_=dvals[:, tt * 128:(tt + 1) * 128].rearrange("(h d) t -> d h t", h=8)), 16)
            mark(f"vld_{tt}", sVLD)
            if tt >= 1:
                wait(g, f"tp1_{tt - 1}")
            sXIN.inc(g.dma_start(out=xin[:, :], in_=dx[tt * 128:(tt + 1) * 128, :]), 16)
            mark(f"xin_{tt}", sXIN)

        @block.tensor
        def _(t, tt=tt):
            if tt == 0:
                wait(t, "ld_wout")
            wait(t, f"vld_{tt}")
            if tt >= 1:
                wait(t, f"y0d_{tt - 1}")
            for half, ps in ((0, pA), (1, pB)):
                for dc in range(8):
                    ins = t.matmul(ps[:, :], valsin[:, dc * 128:(dc + 1) * 128],
                                   wout_sb[:, dc * D + half * 512:dc * D + half * 512 + 512],
                                   start=(dc == 0), stop=(dc == 7))
                sMM.inc(ins)
                mark(f"woutmm_{tt}_{half}", sMM)

        @block.vector
        def _(v, tt=tt):
            wait(v, f"woutmm_{tt}_1", f"xin_{tt}")
            if tt >= 3:
                wait(v, f"y2d_{tt - 3}")
            v.scalar_tensor_tensor(y0[:, 0:512], pA[:, :], 1.0, xin[:, 0:512],
                                   ALU.bypass, ALU.add, accum_out=C2(0))
            sV.inc(v.scalar_tensor_tensor(y0[:, 512:1024], pB[:, :], 1.0,
                                          xin[:, 512:1024], ALU.bypass, ALU.add,
                                          accum_out=C2(1)))
            mark(f"y0d_{tt}", sV)

        @block.scalar
        def _(sc, tt=tt):
            wait(sc, f"y0d_{tt}")
            sc.activation(pG[:, :], y0[:, 0:512], AF.Square, accum_out=C2(2))
            sA.inc(sc.activation(pG[:, :], y0[:, 512:1024], AF.Square, accum_out=C2(3)))
            mark(f"sq0_{tt}", sA)

        @block.vector
        def _(v, tt=tt):
            wait(v, f"sq0_{tt}")
            v = TinyV(v, sV)
            v.tensor_tensor(CL(C_SUM0), C2(0), C2(1), ALU.add)
            v.tensor_single_scalar(CL(C_M0), CL(C_SUM0), 1.0 / D, ALU.mult)
            v.tensor_tensor(CL(C_TMP), C2(2), C2(3), ALU.add)
            v.tensor_single_scalar(CL(C_V0), CL(C_TMP), 1.0 / D, ALU.mult)
            v.scalar_tensor_tensor(CL(C_TMP), CL(C_M0), CL(C_M0), CL(C_V0),
                                   ALU.mult, ALU.subtract)
            v.tensor_single_scalar(CL(C_V0), CL(C_TMP), -1.0, ALU.mult)
            v.tensor_scalar(CL(C_QA), CL(C_V0), 1.0 + EPS, EPS * EPS, ALU.mult, ALU.add)
            v.tensor_single_scalar(CL(C_QB), CL(C_V0), EPS, ALU.add)
            mark(f"qpre_q1_{tt}", sV)

        quake(f"q1_{tt}", colar[:, co + C_QA:co + C_QB + 1], 2,
              colar[:, co + C_USC:co + C_R0 + 1], 0)

    def emit_F2(tt):
        p2, CL, C2, rof, rofp, y0, ut, dq1T, fst_, qi16c = _ctx(tt)

        @block.vector
        def _(v, tt=tt):
            wait(v, f"qdone_q1_{tt}", "gb_ready")
            if tt >= 3:
                wait(v, f"outd_{tt - 3}_7")
            v.scalar_tensor_tensor(dq1T[:, :], y0[:, :], CL(C_M0), g1tb[:, :],
                                   ALU.subtract, ALU.mult)
            v.scalar_tensor_tensor(ut[:, :], dq1T[:, :], CL(C_USC), b1tb[:, :],
                                   ALU.mult, ALU.add)
            tv = TinyV(v, sV)
            tv.tensor_tensor(CL(C_SIG0), CL(C_QB), CL(C_R0), ALU.mult)
            tv.tensor_reduce(CL(C_MAX1), ut[:, :], AXX, ALU.max,
                             apply_absolute_value=True)
            tv.tensor_scalar(CL(C_S1), CL(C_MAX1), 1e-5, 1.0 / 127.0, ALU.max, ALU.mult)
            tv.reciprocal(CL(C_S1), CL(C_S1))
            tv.tensor_tensor(CL(C_S1S), CL(C_S1), swcols[:, 0:1], ALU.mult)
            tv.reciprocal(CL(C_AL1), CL(C_S1S))
            tv.tensor_single_scalar(srowcols[:, 2 * p2:2 * p2 + 1],
                                    CL(C_S1S), 0.0, ALU.add)
            mark(f"h1d_{tt}", sV)

        for half in range(2):
            @block.scalar
            def _(sc, tt=tt, half=half):
                if half == 0:
                    wait(sc, f"h1d_{tt}")
                else:
                    wait(sc, f"c1v_{tt}_0")
                hs = slice(half * 512, half * 512 + 512)
                sc.activation(pG[:, :], ut[:, hs], AF.Copy, scale=CL(C_S1), bias=MAGIC)
                sA.inc(sc.activation(qi16c[:, :], pG[:, :], AF.Copy, bias=-MAGIC))
                mark(f"c1a_{tt}_{half}", sA)

            @block.vector
            def _(v, tt=tt, half=half):
                wait(v, f"c1a_{tt}_{half}")
                hs = slice(half * 512, half * 512 + 512)
                sV.inc(v.tensor_single_scalar(dq1[:, hs], qi16c[:, :], 0.0, ALU.add))
                mark(f"c1v_{tt}_{half}", sV)
                if half == 1:
                    mark(f"conv1_{tt}", sV)

        @block.sync
        def _(s, tt=tt):
            wait(s, f"conv1_{tt}")
            sTP1[tt % 2].inc(s.dma_start_transpose(
                dq1T[:, :].rearrange("p (c t) -> p c t", c=8),
                dq1[:, :].rearrange("p (c d) -> p c d", c=8)), 16)
            mark(f"tp1_{tt}", sTP1[tt % 2])

    def emit_Bp1(tt):
        p2, CL, C2, rof, rofp, y0, ut, dq1T, fst_, qi16c = _ctx(tt)
        co = 30 * (tt % 3)

        @block.gpsimd
        def _(g, tt=tt):
            wait(g, f"h1d_{tt}")
            sROW1[p2].inc(g.dma_start(out=rowar[0:1, rof:rof + 128],
                                      in_=srowcols[:, 2 * p2:2 * p2 + 1]), 16)
            mark(f"s1rowev_{tt}", sROW1[p2])

        for e in range(8):
            @block.tensor
            def _(t, tt=tt, e=e):
                if e == 0:
                    wait(t, f"tp1_{tt}", f"s1rowev_{tt}")
                    if tt == 0:
                        wait(t, "w1T_ready")
                ps = pC if e % 2 == 0 else pD
                if tt > 0 or e >= 2:
                    wait(t, f"gelu_{tt}_{e - 2}" if e >= 2 else f"gelu_{tt - 1}_{6 + e}")
                for dc in range(8):
                    t.matmul(ps[:, :], dq1T[:, dc * 128:(dc + 1) * 128],
                             w1T[:, dc * DFF + e * 512:dc * DFF + e * 512 + 512],
                             start=(dc == 0), stop=False)
                sMM.inc(t.matmul(ps[:, :], rowar[0:1, rof:rof + 128],
                                 b1row[0:1, e * 512:e * 512 + 512],
                                 start=False, stop=True))
                mark(f"m1_{tt}_{e}", sMM)

            @block.scalar
            def _(sc, tt=tt, e=e):
                wait(sc, f"m1_{tt}_{e}")
                if tt >= 2:
                    wait(sc, f"tp2_{tt - 2}_{e}")
                ps = pC if e % 2 == 0 else pD
                sc.activation(hga[tt % 2][:, e * 512:e * 512 + 512], ps[:, :],
                              AF.Gelu, scale=CL(C_AL1), accum_out=C2(0 + e))
                sA.inc(sc.activation(pG[:, :], hga[tt % 2][:, e * 512:e * 512 + 512],
                                     AF.Square, accum_out=C2(8 + e)))
                mark(f"gelu_{tt}_{e}", sA)
                mark(f"sq2_{tt}_{e}", sA)

        @block.vector
        def _(v, tt=tt):
            wait(v, f"sq2_{tt}_7")
            v = TinyV(v, sV)
            v.tensor_reduce(CL(C_GS), colar2[:, 16 * (tt % 3) + 0:16 * (tt % 3) + 8], AXX, ALU.add)
            v.tensor_reduce(CL(C_GSQ), colar2[:, 16 * (tt % 3) + 8:16 * (tt % 3) + 16], AXX, ALU.add)
            v.tensor_single_scalar(CL(C_M2), CL(C_GS), 1.0 / DFF, ALU.mult)
            v.tensor_single_scalar(CL(C_V2), CL(C_GSQ), 1.0 / DFF, ALU.mult)
            v.scalar_tensor_tensor(CL(C_TMP), CL(C_M2), CL(C_M2), CL(C_V2),
                                   ALU.mult, ALU.subtract)
            v.tensor_single_scalar(CL(C_V2), CL(C_TMP), -1.0, ALU.mult)
            v.tensor_single_scalar(CL(C_Q2), CL(C_V2), EPS, ALU.add)
            mark(f"qpre_q2_{tt}", sV)

        quake(f"q2_{tt}", colar[:, co + C_Q2:co + C_Q2 + 1], 1,
              colar[:, co + C_R2:co + C_R2 + 1], 0)

    def emit_Bp2_h2(tt):
        p2, CL, C2, rof, rofp, y0, ut, dq1T, fst_, qi16c = _ctx(tt)
        co = 30 * (tt % 3)

        @block.vector
        def _(v, tt=tt):
            wait(v, f"qdone_q2_{tt}")
            for k in range(16):
                n = 16 * tt + k
                v.wait_ge(sG2[k % 4].h, 16 * (n // 4 + 1))
                v.wait_ge(sB2[k % 4].h, 16 * (n // 4 + 1))
                hs = slice(k * 256, k * 256 + 256)
                v.scalar_tensor_tensor(hscr1[:, 0:256], hga[tt % 2][:, hs], CL(C_M2),
                                       g2s[k % 4][:, :], ALU.subtract, ALU.mult)
                v.scalar_tensor_tensor(hga[tt % 2][:, hs], hscr1[:, 0:256], CL(C_R2),
                                       b2s[k % 4][:, :], ALU.mult, ALU.add)
                sV.inc(v.tensor_reduce(C2(k), hga[tt % 2][:, hs], AXX, ALU.max,
                                       apply_absolute_value=True))
                mark(f"h2e_{tt}_{k}", sV)
            v.wait_ge(sV.h, sV.n)
            tv = TinyV(v, sV)
            tv.tensor_reduce(CL(C_MAX2), colar2[:, 16 * (tt % 3) + 0:16 * (tt % 3) + 16],
                             AXX, ALU.max)
            tv.tensor_scalar(CL(C_S2), CL(C_MAX2), 1e-5, 1.0 / 127.0, ALU.max, ALU.mult)
            tv.reciprocal(CL(C_S2), CL(C_S2))
            tv.tensor_tensor(CL(C_S2S), CL(C_S2), swcols[:, 1:2], ALU.mult)
            tv.reciprocal(CL(C_TMP), CL(C_S2S))
            tv.tensor_tensor(CL(C_BETA), CL(C_SIG0), CL(C_TMP), ALU.mult)
            tv.tensor_single_scalar(srowcols[:, 2 * p2 + 1:2 * p2 + 2],
                                    CL(C_S2S), 0.0, ALU.add)
            mark(f"h2d_{tt}", sV)

        @block.gpsimd
        def _(g, tt=tt):
            for k in range(16):
                if k >= 4:
                    wait(g, f"h2e_{tt}_{k - 4}")
                elif tt >= 1:
                    wait(g, f"h2e_{tt - 1}_{12 + k}")
                sl = slice(k * 256, k * 256 + 256)
                sG2[k % 4].inc(g.dma_start(out=g2s[k % 4][:, :], in_=dg2[:, sl]), 16)
                sB2[k % 4].inc(g.dma_start(out=b2s[k % 4][:, :], in_=db2g[:, sl]), 16)

    def emit_Bp2_rest(tt):
        p2, CL, C2, rof, rofp, y0, ut, dq1T, fst_, qi16c = _ctx(tt)
        co = 30 * (tt % 3)

        @block.gpsimd
        def _(g, tt=tt):
            wait(g, f"h2d_{tt}")
            sROW2[p2].inc(g.dma_start(out=rowar[0:1, rof + 128:rof + 256],
                                      in_=srowcols[:, 2 * p2 + 1:2 * p2 + 2]), 16)
            mark(f"s2rowev_{tt}", sROW2[p2])

        for e in range(8):
            @block.scalar
            def _(sc, tt=tt, e=e):
                if e == 0:
                    wait(sc, f"h2d_{tt}")
                if e >= 1:
                    wait(sc, f"c2v_{tt}_{e - 1}")
                hs = slice(e * 512, e * 512 + 512)
                sc.activation(pG[:, :], hga[tt % 2][:, hs], AF.Copy,
                              scale=CL(C_S2), bias=MAGIC)
                sA.inc(sc.activation(qi16c[:, :], pG[:, :], AF.Copy, bias=-MAGIC))
                mark(f"c2a_{tt}_{e}", sA)

            @block.vector
            def _(v, tt=tt, e=e):
                wait(v, f"c2a_{tt}_{e}")
                hs = slice(e * 512, e * 512 + 512)
                sV.inc(v.tensor_single_scalar(hga[tt % 2][:, hs], qi16c[:, :],
                                              0.0, ALU.add))
                mark(f"c2v_{tt}_{e}", sV)

        for e in range(8):
            @block.sync
            def _(s, tt=tt, e=e):
                wait(s, f"c2v_{tt}_{e}")
                if e >= 2:
                    wait(s, f"m2p_{tt}_{e - 2}")
                elif tt >= 1:
                    wait(s, f"m2p_{tt - 1}_{6 + e}")
                sT2e[e][tt % 2].inc(s.dma_start_transpose(
                    dq2Tp[e % 2][:, :].rearrange("p (c t) -> p c t", c=4),
                    hga[tt % 2][:, e * 512:(e + 1) * 512].rearrange("p (c d) -> p c d", c=4)), 16)
                mark(f"tp2_{tt}_{e}", sT2e[e][tt % 2])

            @block.tensor
            def _(t, tt=tt, e=e):
                wait(t, f"tp2_{tt}_{e}")
                if tt == 0:
                    wait(t, f"w2Tg_{e}")
                if e == 0 and tt > 0:
                    wait(t, f"fs_{tt - 1}")
                for half, ps in ((0, pE), (1, pF)):
                    for j in range(4):
                        hc = 4 * e + j
                        ins = t.matmul(ps[:, :], dq2Tp[e % 2][:, j * 128:(j + 1) * 128],
                                       w2T[:, hc * D + half * 512:hc * D + half * 512 + 512],
                                       start=(hc == 0), stop=False)
                if e < 7:
                    sMM.inc(ins)
                    mark(f"m2p_{tt}_{e}", sMM)
                else:
                    wait(t, f"s2rowev_{tt}")
                    for half, ps in ((0, pE), (1, pF)):
                        sMM.inc(t.matmul(ps[:, :], rowar[0:1, rof + 128:rof + 256],
                                         b2row[0:1, half * 512:half * 512 + 512],
                                         start=False, stop=True))
                        mark(f"m2_{tt}_{half}", sMM)
                    marks[f"m2p_{tt}_7"] = marks[f"m2_{tt}_1"]

        @block.scalar
        def _(sc, tt=tt):
            wait(sc, f"m2_{tt}_1")
            if tt >= 1:
                wait(sc, f"y2d_{tt - 1}")
            sc.activation(fst[:, 0:512], pE[:, :], AF.Copy, scale=CL(C_BETA))
            sA.inc(sc.activation(fst[:, 512:1024], pF[:, :], AF.Copy, scale=CL(C_BETA)))
            mark(f"fs_{tt}", sA)

        @block.vector
        def _(v, tt=tt):
            wait(v, f"fs_{tt}")
            sV.inc(v.scalar_tensor_tensor(ut[:, :], fst[:, :], 1.0, y0[:, :],
                                          ALU.bypass, ALU.add, accum_out=C2(8)))
            mark(f"y2d_{tt}", sV)

        @block.scalar
        def _(sc, tt=tt):
            wait(sc, f"y2d_{tt}")
            sc.activation(pG[:, :], ut[:, 0:512], AF.Square, accum_out=C2(9))
            sA.inc(sc.activation(pG[:, :], ut[:, 512:1024], AF.Square, accum_out=C2(10)))
            mark(f"sq3_{tt}", sA)

        @block.vector
        def _(v, tt=tt):
            wait(v, f"sq3_{tt}")
            v = TinyV(v, sV)
            v.tensor_single_scalar(CL(C_M3), C2(8), 1.0 / D, ALU.mult)
            v.tensor_tensor(CL(C_TMP), C2(9), C2(10), ALU.add)
            v.tensor_single_scalar(CL(C_V3), CL(C_TMP), 1.0 / D, ALU.mult)
            v.scalar_tensor_tensor(CL(C_TMP), CL(C_M3), CL(C_M3), CL(C_V3),
                                   ALU.mult, ALU.subtract)
            v.tensor_single_scalar(CL(C_V3), CL(C_TMP), -1.0, ALU.mult)
            v.scalar_tensor_tensor(CL(C_Q3), CL(C_QB), EPS, CL(C_V3),
                                   ALU.mult, ALU.add)
            mark(f"qpre_q3_{tt}", sV)

        quake(f"q3_{tt}", colar[:, co + C_Q3:co + C_Q3 + 1], 1,
              colar[:, co + C_R3:co + C_R3 + 1], 0)

        @block.vector
        def _(v, tt=tt):
            wait(v, f"qdone_q3_{tt}")
            v = TinyV(v, sV)
            v.scalar_tensor_tensor(CL(C_B3), CL(C_M3), -1.0, CL(C_R3),
                                   ALU.mult, ALU.mult)
            mark(f"st3_{tt}", sV)

        for qq in range(8):
            @block.scalar
            def _(sc, tt=tt, qq=qq):
                if qq == 0:
                    wait(sc, f"st3_{tt}")
                if qq >= 2:
                    wait(sc, f"ost_{tt}_{qq - 2}")
                elif tt >= 1:
                    wait(sc, f"ost_{tt - 1}_{6 + qq}")
                hs = slice(qq * 128, qq * 128 + 128)
                sA.inc(sc.activation(outt[qq % 2][:, :], ut[:, hs], AF.Identity,
                                     scale=CL(C_R3), bias=CL(C_B3)))
                mark(f"outd_{tt}_{qq}", sA)

            if qq % 2 == 0:
                @block.gpsimd
                def _(g, tt=tt, qq=qq):
                    wait(g, f"outd_{tt}_{qq}")
                    hs = slice(qq * 128, qq * 128 + 128)
                    sOUT.inc(g.dma_start(out=dy[tt * 128:(tt + 1) * 128, hs],
                                         in_=outt[qq % 2][:, :]), 16)
                    mark(f"ost_{tt}_{qq}", sOUT)
                    if tt == NT - 1 and qq == 6:
                        g.wait_ge(sOUT.h, 64 * NT)
            else:
                @block.sync
                def _(s, tt=tt, qq=qq):
                    wait(s, f"outd_{tt}_{qq}")
                    hs = slice(qq * 128, qq * 128 + 128)
                    sOUT2.inc(s.dma_start(out=dy[tt * 128:(tt + 1) * 128, hs],
                                          in_=outt[qq % 2][:, :]), 16)
                    mark(f"ost_{tt}_{qq}", sOUT2)
                    if tt == NT - 1 and qq == 7:
                        s.wait_ge(sOUT2.h, 64 * NT)

    for tt in range(NT):
        emit_F1(tt)
        if tt >= 2:
            emit_Bp2_h2(tt - 2)
        if tt >= 1:
            emit_Bp1(tt - 1)
        if tt >= 2:
            emit_Bp2_rest(tt - 2)
        emit_F2(tt)
    emit_Bp2_h2(NT - 2)
    emit_Bp1(NT - 1)
    emit_Bp2_rest(NT - 2)
    emit_Bp2_h2(NT - 1)
    emit_Bp2_rest(NT - 1)

    es.close()
    return nc


_NC_CACHE = {}


def kernel(**inputs):
    x = np.ascontiguousarray(np.asarray(inputs["L0_attn_x"], dtype=np.float32))
    wqkv = np.ascontiguousarray(np.asarray(inputs["L0_attn_w_qkv"], dtype=np.float32))
    wout = np.ascontiguousarray(np.asarray(inputs["L0_attn_w_out"], dtype=np.float32))
    g1 = np.asarray(inputs["L0_ffn_ln1_g"], dtype=np.float32)
    b1g = np.asarray(inputs["L0_ffn_ln1_b"], dtype=np.float32)
    w1 = np.ascontiguousarray(np.asarray(inputs["L0_ffn_w1"], dtype=np.float32))
    b1 = np.asarray(inputs["L0_ffn_b1"], dtype=np.float32)
    g2 = np.asarray(inputs["L0_ffn_ln2_g"], dtype=np.float32)
    b2g = np.asarray(inputs["L0_ffn_ln2_b"], dtype=np.float32)
    w2 = np.ascontiguousarray(np.asarray(inputs["L0_ffn_w2"], dtype=np.float32))
    b2 = np.asarray(inputs["L0_ffn_b2"], dtype=np.float32)

    consts = _np_consts()
    if "nc" not in _NC_CACHE:
        _NC_CACHE["nc"] = build_nc()
    nc = _NC_CACHE["nc"]

    in_maps = []
    for c in range(8):
        b = c // 4
        q = c % 4
        m = np.zeros(8, np.float32)
        for r in range(8):
            if r // 4 == b and (r % 4) < q:
                m[r] = 1.0
        oh = np.zeros(8, np.float32)
        oh[c] = 1.0
        im = {
            "x": np.ascontiguousarray(x[b, q * T:(q + 1) * T, :]),
            "wqkv": wqkv,
            "wout": wout,
            "w1": w1,
            "w2": w2,
            "b1r": np.ascontiguousarray(b1.reshape(1, DFF)),
            "b2r": np.ascontiguousarray(b2.reshape(1, D)),
            "g1t": np.ascontiguousarray(np.broadcast_to(g1[None, :], (128, D))),
            "b1t": np.ascontiguousarray(np.broadcast_to(b1g[None, :], (128, D))),
            "g2t": np.ascontiguousarray(np.broadcast_to(g2[None, :], (128, DFF))),
            "b2t": np.ascontiguousarray(np.broadcast_to(b2g[None, :], (128, DFF))),
            "maskr": np.ascontiguousarray(np.broadcast_to(m[None, :], (128, 8))),
            "onehot": np.ascontiguousarray(np.broadcast_to(oh[None, :], (128, 8))),
        }
        im.update(consts)
        in_maps.append(im)

    res = run_bass_kernel_spmd(nc, in_maps, core_ids=list(range(8)))
    out = np.zeros((2, 8192, D), np.float32)
    for c in range(8):
        b = c // 4
        q = c % 4
        out[b, q * T:(q + 1) * T, :] = res.results[c]["y"]
    return out


# version 23
# speedup vs baseline: 1.1543x; 1.0547x over previous
import numpy as np
import ml_dtypes
import contextlib
import concourse.bass as bass
import concourse.mybir as mybir
from concourse.bass_utils import run_bass_kernel_spmd

F32 = mybir.dt.float32
BF16 = mybir.dt.bfloat16
I16 = mybir.dt.int16
AF = mybir.ActivationFunctionType
ALU = mybir.AluOpType
AXX = mybir.AxisListType.X

NPBF = ml_dtypes.bfloat16
T = 2048
D = 1024
DFF = 4096
H = 8
NT = 16
MAGIC = float(3 * 2.0 ** 22)
QC16 = float(0x5F37)
EPS = 1e-5


def _np_consts():
    N = 128
    n = np.arange(N)
    j65 = np.arange(65)
    jj = np.arange(1, 64)
    w = 2 * np.pi / N
    Rre = np.cos(w * np.outer(j65, n))
    Rim = np.zeros((64, N)); Rim[1:64] = -np.sin(w * np.outer(jj, n))
    cj = np.ones(65); cj[1:64] = 2.0
    Are = (cj[None, :] / N) * np.cos(w * np.outer(n, j65))
    Aim = np.zeros((N, 64)); Aim[:, 1:64] = (-2.0 / N) * np.sin(w * np.outer(n, jj))
    return {
        "cRreT": np.ascontiguousarray(Rre.T).astype(np.float32),
        "cRimT": np.ascontiguousarray(Rim.T).astype(np.float32),
        "cAreT": np.ascontiguousarray(Are.T).astype(np.float32),
        "cAimT": np.ascontiguousarray(Aim.T).astype(np.float32),
        "ident": np.eye(128, dtype=np.float32),
        "onesrow": np.ones((1, 128), dtype=np.float32),
        "onescol": np.ones((128, 1), dtype=np.float32),
    }


class TinyV:

    def __init__(self, v, sem):
        self._v = v
        self._s = sem

    def __getattr__(self, name):
        f = getattr(self._v, name)
        if not callable(f):
            return f

        def wrapped(*a, **k):
            r = f(*a, **k)
            self._s.inc(r)
            self._v.wait_ge(self._s.h, self._s.n)
            return r
        return wrapped


class Sem:
    def __init__(self, stack, nc, name):
        self.h = stack.enter_context(nc.semaphore(name))
        self.n = 0

    def inc(self, instr, k=1):
        instr.then_inc(self.h, k)
        self.n += k


DEBUG = False


def build_nc():
    nc = bass.Bass("TRN2", target_bir_lowering=False)
    es = contextlib.ExitStack()
    marks = {}

    def mark(name, sem):
        marks[name] = (sem.h, sem.n)

    def wait(eng, *names):
        for name in names:
            h, n = marks[name]
            if n > 0:
                eng.wait_ge(h, n)

    dx = nc.dram_tensor("x", [T, D], F32, kind="ExternalInput").ap()
    dwqkv = nc.dram_tensor("wqkv", [D, 3 * D], F32, kind="ExternalInput").ap()
    dwout = nc.dram_tensor("wout", [D, D], F32, kind="ExternalInput").ap()
    dw1 = nc.dram_tensor("w1", [DFF, D], F32, kind="ExternalInput").ap()
    dw2 = nc.dram_tensor("w2", [D, DFF], F32, kind="ExternalInput").ap()
    db1 = nc.dram_tensor("b1r", [1, DFF], F32, kind="ExternalInput").ap()
    db2 = nc.dram_tensor("b2r", [1, D], F32, kind="ExternalInput").ap()
    dg1 = nc.dram_tensor("g1t", [128, D], F32, kind="ExternalInput").ap()
    db1g = nc.dram_tensor("b1t", [128, D], F32, kind="ExternalInput").ap()
    dg2 = nc.dram_tensor("g2t", [128, DFF], F32, kind="ExternalInput").ap()
    db2g = nc.dram_tensor("b2t", [128, DFF], F32, kind="ExternalInput").ap()
    dmaskr = nc.dram_tensor("maskr", [128, 8], F32, kind="ExternalInput").ap()
    donehot = nc.dram_tensor("onehot", [128, 8], F32, kind="ExternalInput").ap()
    dconst = {k: nc.dram_tensor(k, list(v.shape), F32, kind="ExternalInput").ap()
              for k, v in _np_consts().items()}
    dy = nc.dram_tensor("y", [T, D], F32, kind="ExternalOutput").ap()
    KIND = "ExternalOutput" if DEBUG else "Internal"
    dtbl_in = nc.dram_tensor("tbl_in", [128, 128], F32).ap()
    dtbl_out = nc.dram_tensor("tbl_out", [128, 128], F32).ap()
    dw1q = nc.dram_tensor("w1q", [DFF, D], BF16, kind=KIND).ap()
    dw2q = nc.dram_tensor("w2q", [D, DFF], BF16, kind=KIND).ap()
    dvals = nc.dram_tensor("valsd", [D, T], BF16, kind=KIND).ap()
    dkvre = nc.dram_tensor("dkvre", [H * 65, T], BF16, kind=KIND).ap()
    dkvim = nc.dram_tensor("dkvim", [H * 64, T], BF16, kind=KIND).ap()

    def sb(name, shp, dt):
        return es.enter_context(nc.sbuf_tensor(name, shp, dt))

    wout_sb = sb("wout_sb", [128, 8 * D], BF16)
    swcols = sb("swcols", [128, 2], F32)
    b1row = es.enter_context(nc.sbuf_tensor("b1row", [1, DFF], BF16, side="right"))
    b2row = es.enter_context(nc.sbuf_tensor("b2row", [1, D], BF16, side="right"))
    rowar = sb("rowar", [1, 512], BF16)
    colar = sb("colar", [128, 90], F32)
    colar2 = sb("colar2", [128, 48], F32)
    srowcols = sb("srowcols", [128, 4], BF16)
    qk_vb = sb("qk_vb", [128, 2], BF16)
    qk_bi = sb("qk_bi", [128, 2], F32)
    qk_v64 = sb("qk_v64", [128, 64], F32)
    qk_sd64 = sb("qk_sd64", [128, 64], BF16)
    qk_t64a = sb("qk_t64a", [128, 64], F32)
    qk_y64a = sb("qk_y64a", [128, 64], F32)
    qk_y64b = sb("qk_y64b", [128, 64], F32)


    sG = Sem(es, nc, "sG")
    sXL = Sem(es, nc, "sXL")
    sXT = Sem(es, nc, "sXT")
    sWL = Sem(es, nc, "sWL")
    sWQL = Sem(es, nc, "sWQL")
    sWS1 = Sem(es, nc, "sWS1")
    sWS2 = Sem(es, nc, "sWS2")
    sKVS = Sem(es, nc, "sKVS")
    sKVL = Sem(es, nc, "sKVL")
    sVST = Sem(es, nc, "sVST")
    sTBL = Sem(es, nc, "sTBL")
    sGB = Sem(es, nc, "sGB")
    sVLD = Sem(es, nc, "sVLD")
    sXIN = Sem(es, nc, "sXIN")
    sWT1 = Sem(es, nc, "sWT1")
    sWT2 = Sem(es, nc, "sWT2")
    sTP = Sem(es, nc, "sTP")
    sDBG = Sem(es, nc, "sDBG")
    sMM = Sem(es, nc, "sMM")
    sA = Sem(es, nc, "sA")
    sV = Sem(es, nc, "sV")
    sGP = Sem(es, nc, "sGP")
    sCC = Sem(es, nc, "sCC")
    sOUT = Sem(es, nc, "sOUT")
    sWLs = [Sem(es, nc, f"sWLs{k}") for k in range(4)]
    sWQs = [Sem(es, nc, f"sWQs{k}") for k in range(4)]
    sWT2e = [Sem(es, nc, f"sWT2e{k}") for k in range(8)]
    sG2 = [Sem(es, nc, f"sG2{k}") for k in range(4)]
    sOUT2 = Sem(es, nc, "sOUT2")
    sB2 = [Sem(es, nc, f"sB2{k}") for k in range(4)]

    esA1 = contextlib.ExitStack()

    def sbA1(name, shp, dt):
        return esA1.enter_context(nc.sbuf_tensor(name, shp, dt, side="right"))
    wkv = sbA1("wkv", [128, 8 * T], BF16)
    wstg = [sbA1(f"wstg{i}", [128, D], F32) for i in range(4)]
    wqf32 = sbA1("wqf32", [128, D], F32)
    wqi16 = [sbA1(f"wqi16{i}", [128, D], I16) for i in range(2)]
    dqwn = [sbA1(f"dqwn{i}", [128, D], BF16) for i in range(2)]
    kh = sbA1("kh", [128, T], BF16)
    vh = sbA1("vh", [128, T], BF16)
    xtmp = [sbA1(f"xtmp{i}", [128, D], BF16) for i in range(2)]
    fkre = sbA1("fkre", [65, T], BF16)

    esA2 = contextlib.ExitStack()

    def sbA2(name, shp, dt):
        return esA2.enter_context(nc.sbuf_tensor(name, shp, dt))
    xT = sbA2("xT", [128, 8 * T], BF16)
    crre = sbA2("crre", [128, 65], BF16)
    crim = sbA2("crim", [128, 64], BF16)
    care = sbA2("care", [65, 128], BF16)
    caim = sbA2("caim", [64, 128], BF16)
    onesrow_f = sbA2("onesrow_f", [1, 128], F32)
    onescol_f = sbA2("onescol_f", [128, 1], F32)
    maskr = sbA2("maskr_sb", [128, 8], F32)
    onehot = sbA2("onehot_sb", [128, 8], F32)
    Smtx = sbA2("Smtx", [128, 16], F32)
    Sr64 = sbA2("Sr64", [128, 8], F32)
    P_sb = sbA2("P_sb", [128, 16], F32)
    wacc = sbA2("wacc", [128, 64], F32)
    wcolsum = sbA2("wcolsum", [128, 2], F32)
    swv = sbA2("swv", [1, 2], F32)
    cont = sbA2("cont", [128, 128], F32)
    G = sbA2("G_sb", [128, 128], F32)
    wq = sbA2("wq", [128, 8 * D], BF16)
    scrB = sbA2("scrB", [64, T], BF16)
    fqre = [sbA2(f"fqre{i}", [65, T], BF16) for i in range(2)]
    fqim = [sbA2(f"fqim{i}", [64, T], BF16) for i in range(2)]
    valstg = sbA2("valstg", [128, T], BF16)
    qh = sbA2("qh", [128, T], BF16)

    esA3 = contextlib.ExitStack()
    fkim = esA3.enter_context(nc.sbuf_tensor("fkim", [64, T], BF16))
    fvre = esA3.enter_context(nc.sbuf_tensor("fvre", [65, T], BF16))
    fvim = esA3.enter_context(nc.sbuf_tensor("fvim", [64, T], BF16))
    kvre_p1 = esA3.enter_context(nc.sbuf_tensor("kvre_p1", [65, T], BF16))
    kvim_p1 = esA3.enter_context(nc.sbuf_tensor("kvim_p1", [64, T], BF16))
    esA3b = contextlib.ExitStack()
    kvre = kvim = None

    esAP = contextlib.ExitStack()

    def psA(name, shp, dt):
        return esAP.enter_context(nc.psum_tensor(name, shp, dt))
    pk = psA("pk", [128, 512], F32)
    pv = psA("pv", [128, 512], F32)
    pfre = [psA(f"pfre{i}", [65, 512], F32) for i in range(2)]
    pfim = [psA(f"pfim{i}", [64, 512], F32) for i in range(2)]
    pmisc = psA("pmisc", [128, 512], F32)

    block = es.enter_context(nc.Block())
    if True:
        @block.scalar
        def _(sc):
            sc.preload_act_table(AF.Gelu)

        @block.gpsimd
        def _(g):
            g.memset(Smtx[:, :], 0.0)
            g.memset(Sr64[:, :], 0.0)
            for pad in (qk_v64, qk_t64a, qk_y64a, qk_y64b):
                g.memset(pad[:, :], 0.0)
            g.memset(qk_sd64[:, :], 0.0)
            for nm, dst in (("cRreT", crre), ("cRimT", crim), ("cAreT", care),
                            ("cAimT", caim)):
                sG.inc(g.dma_start(out=dst[:, :], in_=dconst[nm][:, :]), 16)
            sG.inc(g.dma_start(out=onesrow_f[:, :], in_=dconst["onesrow"][:, :]), 16)
            sG.inc(g.dma_start(out=onescol_f[:, :], in_=dconst["onescol"][:, :]), 16)
            sG.inc(g.dma_start(out=maskr[:, :], in_=dmaskr[:, :]), 16)
            sG.inc(g.dma_start(out=onehot[:, :], in_=donehot[:, :]), 16)
            sG.inc(g.dma_start(out=b1row[:, :], in_=db1[:, :]), 16)
            sG.inc(g.dma_start(out=b2row[:, :], in_=db2[:, :]), 16)
            mark("ld_consts", sG)
            for dc in range(8):
                sG.inc(g.dma_start(out=wkv[:, dc * T:(dc + 1) * T],
                                   in_=dwqkv[dc * 128:(dc + 1) * 128, D:3 * D]), 16)
            mark("ld_wkv", sG)
            pass

    for tt in range(NT):
        if True:
            @block.gpsimd
            def _(g, tt=tt):
                if tt >= 2:
                    g.wait_ge(sXT.h, 16 * (tt - 1))
                sXL.inc(g.dma_start(out=xtmp[tt % 2][:, :],
                                    in_=dx[tt * 128:(tt + 1) * 128, :]), 16)
                mark(f"xld_{tt}", sXL)

            @block.sync
            def _(s, tt=tt):
                s.wait_ge(sXL.h, 16 * min(tt + 2, NT))
                sXT.inc(s.dma_start_transpose(
                    xT[:, :].rearrange("p (c u t) -> p c u t", c=8, u=NT)[:, :, tt, :],
                    xtmp[tt % 2][:, :].rearrange("p (c d) -> p c d", c=8)), 16)
                mark(f"xT_{tt}", sXT)
                if tt == NT - 1:
                    mark("xT_all", sXT)

    if True:
        @block.gpsimd
        def _(g):
            for dc in range(8):
                sG.inc(g.dma_start(out=wq[:, dc * D:(dc + 1) * D],
                                   in_=dwqkv[dc * 128:(dc + 1) * 128, 0:D]), 16)
            mark("ld_wq", sG)
            for dc in range(8):
                sG.inc(g.dma_start(out=wout_sb[:, dc * D:(dc + 1) * D],
                                   in_=dwout[dc * 128:(dc + 1) * 128, :]), 16)
            mark("ld_wout", sG)
            mark("ld_all", sG)

    def _wsrc(i):
        if i < 32:
            return dw1[i * 128:(i + 1) * 128, :]
        j = i - 32
        return dw2[(j // 4) * 128:(j // 4 + 1) * 128,
                   (j % 4) * 1024:(j % 4 + 1) * 1024]

    def emit_wscan(u):
        @block.gpsimd
        def _(g, u=u):
            i = 2 * u
            if i >= 4:
                wait(g, f"wred_{i - 4}")
            sWLs[i % 4].inc(g.dma_start(out=wstg[i % 4][:, :], in_=_wsrc(i)), 16)

        @block.sync
        def _(s, u=u):
            i = 2 * u + 1
            if i >= 4:
                wait(s, f"wred_{i - 4}")
            sWLs[i % 4].inc(s.dma_start(out=wstg[i % 4][:, :], in_=_wsrc(i)), 16)

        @block.vector
        def _(v, u=u):
            for i in (2 * u, 2 * u + 1):
                v.wait_ge(sWLs[i % 4].h, 16 * (i // 4 + 1))
                sV.inc(v.tensor_reduce(wacc[:, i:i + 1], wstg[i % 4][:, :], AXX,
                                       ALU.add, apply_absolute_value=True))
                mark(f"wred_{i}", sV)
                if i == 63:
                    mark("wred_all", sV)

    def emit_sw_chain():
        @block.vector
        def _(v):
            wait(v, "wred_all")
            v.tensor_reduce(wcolsum[:, 0:1], wacc[:, 0:32], AXX, ALU.add)
            sV.inc(v.tensor_reduce(wcolsum[:, 1:2], wacc[:, 32:64], AXX, ALU.add))
            mark("wcolsum", sV)

        @block.tensor
        def _(t):
            wait(t, "wcolsum", "ld_consts")
            t.matmul(pmisc[0:1, 0:1], wcolsum[:, 0:1], onescol_f[:, :],
                     start=True, stop=True)
            sMM.inc(t.matmul(pmisc[0:1, 1:2], wcolsum[:, 1:2], onescol_f[:, :],
                             start=True, stop=True))
            mark("wtot_mm", sMM)

        @block.scalar
        def _(sc):
            wait(sc, "wtot_mm")
            sA.inc(sc.copy(swv[0:1, 0:2], pmisc[0:1, 0:2]))
            mark("wtot_ev", sA)

        @block.tensor
        def _(t):
            wait(t, "wtot_ev")
            sMM.inc(t.matmul(pmisc[:, 8:10], onesrow_f[0:1, :], swv[0:1, 0:2],
                             start=True, stop=True))
            mark("swb_mm", sMM)

        @block.scalar
        def _(sc):
            wait(sc, "swb_mm")
            sA.inc(sc.copy(swcols[:, 0:2], pmisc[:, 8:10]))
            mark("sw_bc", sA)

        @block.vector
        def _(v):
            wait(v, "sw_bc")
            tv = TinyV(v, sV)
            tv.tensor_scalar(wcolsum[:, 0:2], swcols[:, 0:2], 1.0 / (DFF * D), 1e-5,
                             ALU.mult, ALU.max)
            sV.inc(v.reciprocal(swcols[:, 0:2], wcolsum[:, 0:2]))
            mark("sw_cols", sV)

    def emit_wquant(i):
        if True:
            @block.sync
            def _(s, i=i):
                if i == 0:
                    wait(s, "sw_cols")
                if i >= 4:
                    wait(s, f"wq_rd_{i - 4}")
                if i < 32:
                    src = dw1[i * 128:(i + 1) * 128, :]
                else:
                    j = i - 32
                    src = dw2[(j // 4) * 128:(j // 4 + 1) * 128,
                              (j % 4) * 1024:(j % 4 + 1) * 1024]
                sWQs[i % 4].inc(s.dma_start(out=wstg[i % 4][:, :], in_=src), 16)

            @block.scalar
            def _(sc, i=i):
                sc.wait_ge(sWQs[i % 4].h, 16 * (i // 4 + 1))
                if i >= 2:
                    wait(sc, f"wq_cl_{i - 2}")
                swsl = swcols[:, 0:1] if i < 32 else swcols[:, 1:2]
                sc.activation(wqf32[:, :], wstg[i % 4][:, :], AF.Copy,
                              scale=swsl, bias=MAGIC)
                sA.inc(sc.activation(wqi16[i % 2][:, :], wqf32[:, :], AF.Copy, bias=-MAGIC))
                mark(f"wq_rd_{i}", sA)

            @block.vector
            def _(v, i=i):
                wait(v, f"wq_rd_{i}")
                if i >= 2:
                    v.wait_ge(sWS1.h, 16 * min(i, 32))
                    if i > 32:
                        v.wait_ge(sWS2.h, 16 * (i - 32))
                sV.inc(v.tensor_scalar(dqwn[i % 2][:, :], wqi16[i % 2][:, :], -1.0, 1.0,
                                       ALU.max, ALU.min))
                mark(f"wq_cl_{i}", sV)

            @block.gpsimd
            def _(g, i=i):
                wait(g, f"wq_cl_{i}")
                if i < 32:
                    dst = dw1q[i * 128:(i + 1) * 128, :]
                    sWS1.inc(g.dma_start(out=dst, in_=dqwn[i % 2][:, :]), 16)
                else:
                    j = i - 32
                    dst = dw2q[(j // 4) * 128:(j // 4 + 1) * 128,
                               (j % 4) * 1024:(j % 4 + 1) * 1024]
                    sWS2.inc(g.dma_start(out=dst, in_=dqwn[i % 2][:, :]), 16)
                if i == 31:
                    mark("w1q_done", sWS1)
                if i == 63:
                    mark("w2q_done", sWS2)

    for h in range(H):
        par = 0
        for tb in range(4):
            if True:
                @block.tensor
                def _(t, h=h, tb=tb):
                    if h == 0:
                        wait(t, f"xT_{4 * tb + 3}")
                    if h == 0 and tb == 0:
                        wait(t, "ld_wkv")
                    else:
                        wait(t, f"evkv_{h}_{tb - 1}" if tb > 0 else f"evkv_{h - 1}_3")
                    for tau, ps in (("k", pk), ("v", pv)):
                        col0 = (0 if tau == "k" else D) + h * 128
                        for dc in range(8):
                            ins = t.matmul(ps[:, :],
                                           wkv[:, dc * T + col0:dc * T + col0 + 128],
                                           xT[:, dc * T + tb * 512:dc * T + tb * 512 + 512],
                                           start=(dc == 0), stop=(dc == 7))
                            if tau == "v" and dc == 7:
                                sMM.inc(ins)
                    mark(f"mmkv_{h}_{tb}", sMM)

                @block.scalar
                def _(sc, h=h, tb=tb):
                    wait(sc, f"mmkv_{h}_{tb}")
                    if h > 0 and tb == 0:
                        wait(sc, f"kvm_{h - 1}")
                    tsl = slice(tb * 512, tb * 512 + 512)
                    sc.copy(kh[:, tsl], pk[:, :])
                    sA.inc(sc.copy(vh[:, tsl], pv[:, :]))
                    mark(f"evkv_{h}_{tb}", sA)

                @block.tensor
                def _(t, h=h, tb=tb):
                    wait(t, f"evkv_{h}_{tb}")
                    if h > 0:
                        wait(t, f"evdv_{h - 1}_{tb}")
                    tsl = slice(tb * 512, tb * 512 + 512)
                    t.matmul(pfre[tb % 2][:, :], crre[:, :], kh[:, tsl], start=True, stop=True)
                    sMM.inc(t.matmul(pfim[tb % 2][:, :], crim[:, :], kh[:, tsl],
                                     start=True, stop=True))
                    mark(f"mmdk_{h}_{tb}", sMM)

                @block.scalar
                def _(sc, h=h, tb=tb):
                    tsl = slice(tb * 512, tb * 512 + 512)
                    wait(sc, f"mmdk_{h}_{tb}")
                    if h > 0 and tb == 0:
                        wait(sc, f"kvm_{h - 1}")
                    sc.copy(fkre[:, tsl], pfre[tb % 2][:, :])
                    sA.inc(sc.copy(fkim[:, tsl], pfim[tb % 2][:, :]))
                    mark(f"evdk_{h}_{tb}", sA)

                @block.tensor
                def _(t, h=h, tb=tb):
                    wait(t, f"evdk_{h}_{tb}")
                    tsl = slice(tb * 512, tb * 512 + 512)
                    t.matmul(pfre[tb % 2][:, :], crre[:, :], vh[:, tsl], start=True, stop=True)
                    sMM.inc(t.matmul(pfim[tb % 2][:, :], crim[:, :], vh[:, tsl],
                                     start=True, stop=True))
                    mark(f"mmdv_{h}_{tb}", sMM)

                @block.scalar
                def _(sc, h=h, tb=tb):
                    tsl = slice(tb * 512, tb * 512 + 512)
                    wait(sc, f"mmdv_{h}_{tb}")
                    sc.copy(fvre[:, tsl], pfre[tb % 2][:, :])
                    sA.inc(sc.copy(fvim[:, tsl], pfim[tb % 2][:, :]))
                    mark(f"evdv_{h}_{tb}", sA)

            emit_wscan(h * 4 + tb)

        if True:
            @block.vector
            def _(v, h=h, par=par):
                wait(v, f"evdv_{h}_3")
                if h >= 1:
                    wait(v, f"kvst_{h - 1}")
                v.scalar_tensor_tensor(kvre_p1[:, :], fkre[:, :], 1.0, fvre[:, :],
                                       ALU.bypass, ALU.mult,
                                       accum_out=Sr64[0:65, h:h + 1])
                v.tensor_tensor(scrB[:, :], fkim[:, :], fvim[:, :], ALU.mult)
                v.scalar_tensor_tensor(kvre_p1[0:64, :], scrB[:, :], -1.0,
                                       kvre_p1[0:64, :], ALU.mult, ALU.add,
                                       accum_out=Smtx[0:64, 2 * h:2 * h + 1])
                v.tensor_tensor(scrB[:, :], fkre[0:64, :], fvim[:, :], ALU.mult)
                v.tensor_tensor(kvim_p1[:, :], fkim[:, :], fvre[0:64, :], ALU.mult)
                sV.inc(v.scalar_tensor_tensor(kvim_p1[:, :], scrB[:, :], 1.0,
                                              kvim_p1[:, :], ALU.bypass, ALU.add,
                                              accum_out=Smtx[0:64, 2 * h + 1:2 * h + 2]))
                mark(f"kvm_{h}", sV)

            @block.scalar
            def _(sc, h=h):
                wait(sc, f"kvm_{h}")
                sA.inc(sc.copy(Smtx[64:65, 2 * h:2 * h + 1], Sr64[64:65, h:h + 1]))
                mark(f"sfix_{h}", sA)

            @block.gpsimd
            def _(g, h=h, par=par):
                wait(g, f"kvm_{h}")
                sKVS.inc(g.dma_start(out=dkvre[h * 65:(h + 1) * 65, :], in_=kvre_p1[:, :]), 16)
                sKVS.inc(g.dma_start(out=dkvim[h * 64:(h + 1) * 64, :], in_=kvim_p1[:, :]), 16)
                mark(f"kvst_{h}", sKVS)

    esA3.close()
    kvre = [esA3b.enter_context(nc.sbuf_tensor(f"kvre{i}", [65, T], BF16)) for i in range(2)]
    kvim = [esA3b.enter_context(nc.sbuf_tensor(f"kvim{i}", [64, T], BF16)) for i in range(2)]

    emit_sw_chain()

    if True:
        @block.vector
        def _(v):
            wait(v, "sfix_7")
            last = None
            for r in range(8):
                last = v.tensor_scalar(cont[:, r * 16:(r + 1) * 16], Smtx[:, :],
                                       onehot[:, r:r + 1], None, ALU.mult)
            sV.inc(last)
            mark("cont_ready", sV)

        @block.gpsimd
        def _(g):
            wait(g, "cont_ready")
            sTBL.inc(g.dma_start(out=dtbl_in[:, :], in_=cont[:, :]), 16)
            g.wait_ge(sTBL.h, sTBL.n)
            sCC.inc(g.collective_compute(
                "AllReduce", ALU.add, replica_groups=[list(range(8))],
                ins=[dtbl_in[:, :].opt()], outs=[dtbl_out[:, :].opt()]))
            g.wait_ge(sCC.h, sCC.n)
            sTBL.inc(g.dma_start(out=G[:, :], in_=dtbl_out[:, :]), 16)
            mark("G_ready", sTBL)

        @block.vector
        def _(v):
            wait(v, "G_ready")
            tv = TinyV(v, sV)
            tv.tensor_scalar(P_sb[:, :], G[:, 0:16], maskr[:, 0:1], None, ALU.mult)
            for r in range(1, 8):
                tv.scalar_tensor_tensor(P_sb[:, :], G[:, r * 16:(r + 1) * 16],
                                        maskr[:, r:r + 1], P_sb[:, :],
                                        ALU.mult, ALU.add)
            sV.inc(v.engine_nop())
            mark("P_ready", sV)

    def emit_p2F(h):
        par = h % 2
        for tb in range(4):
            @block.tensor
            def _(t, h=h, tb=tb):
                if h == 0 and tb == 0:
                    wait(t, "ld_wq")
                else:
                    wait(t, f"evq_{h}_{tb - 1}" if tb > 0 else f"evq_{h - 1}_3")
                for dc in range(8):
                    ins = t.matmul(pk[:, :],
                                   wq[:, dc * D + h * 128:dc * D + h * 128 + 128],
                                   xT[:, dc * T + tb * 512:dc * T + tb * 512 + 512],
                                   start=(dc == 0), stop=(dc == 7))
                sMM.inc(ins)
                mark(f"mmq_{h}_{tb}", sMM)

            @block.scalar
            def _(sc, h=h, tb=tb):
                wait(sc, f"mmq_{h}_{tb}")
                if h > 0 and tb == 0:
                    wait(sc, f"mmfq_{h - 1}_3")
                sA.inc(sc.copy(qh[:, tb * 512:tb * 512 + 512], pk[:, :]))
                mark(f"evq_{h}_{tb}", sA)

            @block.tensor
            def _(t, h=h, tb=tb):
                wait(t, f"evq_{h}_{tb}")
                tsl = slice(tb * 512, tb * 512 + 512)
                t.matmul(pfre[tb % 2][:, :], crre[:, :], qh[:, tsl], start=True, stop=True)
                sMM.inc(t.matmul(pfim[tb % 2][:, :], crim[:, :], qh[:, tsl],
                                 start=True, stop=True))
                mark(f"mmfq_{h}_{tb}", sMM)

            @block.scalar
            def _(sc, h=h, tb=tb, par=par):
                wait(sc, f"mmfq_{h}_{tb}")
                if h > 1 and tb == 0:
                    wait(sc, f"outf_{h - 2}")
                tsl = slice(tb * 512, tb * 512 + 512)
                sc.copy(fqre[par][:, tsl], pfre[tb % 2][:, :])
                sA.inc(sc.copy(fqim[par][:, tsl], pfim[tb % 2][:, :]))
                mark(f"evfq_{h}_{tb}", sA)

    def emit_p2B(h):
        par = h % 2
        if True:
            @block.gpsimd
            def _(g, h=h, par=par):
                wait(g, "kvst_7")
                if h >= 2:
                    wait(g, f"outf_{h - 2}")
                sKVL.inc(g.dma_start(out=kvre[par][:, :], in_=dkvre[h * 65:(h + 1) * 65, :]), 16)
                sKVL.inc(g.dma_start(out=kvim[par][:, :], in_=dkvim[h * 64:(h + 1) * 64, :]), 16)
                mark(f"kvld_{h}", sKVL)

            @block.vector
            def _(v, h=h, par=par):
                wait(v, "P_ready")
                wait(v, f"kvld_{h}", f"evfq_{h}_3")
                v.tensor_tensor_scan(kvre[par][:, :], kvre[par][:, :], kvre[par][:, :],
                                     P_sb[0:65, 2 * h:2 * h + 1], ALU.add, ALU.bypass)
                v.tensor_tensor_scan(kvim[par][:, :], kvim[par][:, :], kvim[par][:, :],
                                     P_sb[0:64, 2 * h + 1:2 * h + 2], ALU.add, ALU.bypass)
                v.tensor_tensor(scrB[:, :], kvim[par][:, :], fqre[par][0:64, :], ALU.mult)
                v.tensor_tensor(fqre[par][:, :], kvre[par][:, :], fqre[par][:, :], ALU.mult)
                v.tensor_tensor(kvim[par][:, :], kvim[par][:, :], fqim[par][:, :], ALU.mult)
                v.tensor_tensor(fqre[par][0:64, :], kvim[par][:, :], fqre[par][0:64, :], ALU.add)
                v.tensor_tensor(fqim[par][:, :], kvre[par][0:64, :], fqim[par][:, :], ALU.mult)
                sV.inc(v.tensor_tensor(fqim[par][:, :], scrB[:, :], fqim[par][:, :],
                                       ALU.subtract))
                mark(f"outf_{h}", sV)

        for tb in range(4):
            @block.tensor
            def _(t, h=h, tb=tb, par=par):
                wait(t, f"outf_{h}")
                if not (h == 0 and tb == 0):
                    wait(t, f"evval_{h}_{tb - 1}" if tb > 0 else f"evval_{h - 1}_3")
                tsl = slice(tb * 512, tb * 512 + 512)
                t.matmul(pmisc[:, :], care[:, :], fqre[par][:, tsl], start=True, stop=False)
                sMM.inc(t.matmul(pmisc[:, :], caim[:, :], fqim[par][:, tsl],
                                 start=False, stop=True))
                mark(f"mmval_{h}_{tb}", sMM)

            @block.scalar
            def _(sc, h=h, tb=tb):
                wait(sc, f"mmval_{h}_{tb}")
                if tb == 0 and h > 0:
                    wait(sc, f"valst_{h - 1}")
                sA.inc(sc.copy(valstg[:, tb * 512:tb * 512 + 512], pmisc[:, :]))
                mark(f"evval_{h}_{tb}", sA)

        if True:
            @block.gpsimd
            def _(g, h=h):
                wait(g, f"evval_{h}_3")
                sVST.inc(g.dma_start(out=dvals[h * 128:(h + 1) * 128, :], in_=valstg[:, :]), 16)
                mark(f"valst_{h}", sVST)

        for k in range(8):
            emit_wquant(8 * h + k)

    for h in range(H):
        emit_p2F(h)
        if h >= 1:
            emit_p2B(h - 1)
    emit_p2B(H - 1)

    marks["vals_all"] = marks["valst_7"]

    if DEBUG:
        ddbg = nc.dram_tensor("dbg1", [128, 256], F32, kind="ExternalOutput").ap()
        if True:
            @block.gpsimd
            def _(g):
                wait(g, "sw_cols", "G_ready", "P_ready", "sfix_7", "wred_all")
                sDBG.inc(g.dma_start(out=ddbg[:, 0:64], in_=wacc[:, :]), 16)
                sDBG.inc(g.dma_start(out=ddbg[:, 64:66], in_=wcolsum[:, :]), 16)
                sDBG.inc(g.dma_start(out=ddbg[:, 66:68], in_=swcols[:, :]), 16)
                sDBG.inc(g.dma_start(out=ddbg[:, 68:84], in_=P_sb[:, :]), 16)
                sDBG.inc(g.dma_start(out=ddbg[:, 84:100], in_=Smtx[:, :]), 16)
                sDBG.inc(g.dma_start(out=ddbg[0:1, 100:102], in_=swv[:, :]), 16)
                sDBG.inc(g.dma_start(out=ddbg[:, 102:230], in_=G[:, :]), 16)
                g.wait_ge(sDBG.h, sDBG.n)

    esA3b.close()
    esA1.close()
    w1T = es.enter_context(nc.sbuf_tensor("w1T", [128, 8 * DFF], BF16, side="right"))

    if True:
        @block.sync
        def _(s):
            wait(s, "w1q_done", "w2q_done", "kvm_7", "xT_all", "mmdv_7_3")
            for dc in range(0, 8, 2):
                sWT1.inc(s.dma_start_transpose(w1T[:, dc * DFF:(dc + 1) * DFF],
                                               dw1q[:, dc * 128:(dc + 1) * 128]), 16)

        @block.sync
        def _(s):
            for dc in range(1, 8, 2):
                sWT1.inc(s.dma_start_transpose(w1T[:, dc * DFF:(dc + 1) * DFF],
                                               dw1q[:, dc * 128:(dc + 1) * 128]), 16)
        marks["w1T_ready"] = (sWT1.h, sWT1.n)

    esA2.close()
    esAP.close()
    w2T = es.enter_context(nc.sbuf_tensor("w2T", [128, 32 * D], BF16))
    g1tb = es.enter_context(nc.sbuf_tensor("g1tb", [128, D], BF16))
    b1tb = es.enter_context(nc.sbuf_tensor("b1tb", [128, D], BF16))
    g2s = [es.enter_context(nc.sbuf_tensor(f"g2s{i}", [128, 256], BF16, side="right")) for i in range(4)]
    b2s = [es.enter_context(nc.sbuf_tensor(f"b2s{i}", [128, 256], BF16, side="right")) for i in range(4)]
    valsin = es.enter_context(nc.sbuf_tensor("valsin", [128, D], BF16))
    xin = es.enter_context(nc.sbuf_tensor("xin", [128, D], BF16))
    y0t = [es.enter_context(nc.sbuf_tensor(f"y0t{i}", [128, D], BF16)) for i in range(3)]
    utt = [es.enter_context(nc.sbuf_tensor(f"ut{i}", [128, D], BF16)) for i in range(3)]
    dq1Tt = [es.enter_context(nc.sbuf_tensor(f"dq1T{i}", [128, D], BF16, side="right")) for i in range(2)]
    hga = [es.enter_context(nc.sbuf_tensor(f"hga{i}", [128, DFF], BF16)) for i in range(2)]
    hscr1 = es.enter_context(nc.sbuf_tensor("hscr", [128, 512], BF16))
    hscr = [hscr1, hscr1]
    fst = es.enter_context(nc.sbuf_tensor("fst", [128, D], BF16, side="right"))
    dq2Tp = [es.enter_context(nc.sbuf_tensor(f"dq2Tp{i}", [128, 512], BF16)) for i in range(2)]
    qi16one = es.enter_context(nc.sbuf_tensor("qi16c", [128, 512], I16, side="right"))
    qi16t = [qi16one, qi16one]
    outt = [es.enter_context(nc.sbuf_tensor(f"outt{i}", [128, 128], F32, side="right")) for i in range(2)]
    dq1 = xin

    pA = es.enter_context(nc.psum_tensor("pA", [128, 512], F32))
    pB = es.enter_context(nc.psum_tensor("pB", [128, 512], F32))
    pC = es.enter_context(nc.psum_tensor("pC", [128, 512], F32))
    pD = es.enter_context(nc.psum_tensor("pD", [128, 512], F32))
    pE = es.enter_context(nc.psum_tensor("pE", [128, 512], F32))
    pF = es.enter_context(nc.psum_tensor("pF", [128, 512], F32))
    pG = es.enter_context(nc.psum_tensor("pG", [128, 512], F32))

    A_DONE = ("outf_7", "evval_7_3", "mmval_7_3", "vals_all")

    if True:
        @block.gpsimd
        def _(g):
            wait(g, *A_DONE)
            sGB.inc(g.dma_start(out=g1tb[:, :], in_=dg1[:, :]), 16)
            sGB.inc(g.dma_start(out=b1tb[:, :], in_=db1g[:, :]), 16)
            mark("gb_ready", sGB)

        @block.sync
        def _(s):
            wait(s, "w2q_done", *A_DONE)
            for e in range(8):
                for j in range(4):
                    hc = 4 * e + j
                    sWT2e[e].inc(s.dma_start_transpose(w2T[:, hc * D:(hc + 1) * D],
                                                       dw2q[:, hc * 128:(hc + 1) * 128]), 16)
        for e in range(8):
            marks[f"w2Tg_{e}"] = (sWT2e[e].h, 64)

    C_SUM0, C_M0, C_V0, C_USC, C_R0, C_SIG0, C_UB, C_QA, C_QB = 0, 1, 2, 3, 4, 5, 6, 7, 8
    C_MAX1, C_S1, C_S1S, C_AL1, C_TMP = 9, 10, 11, 12, 13
    C_GS, C_GSQ, C_M2, C_V2, C_R2, C_UB2, C_Q2 = 14, 15, 16, 17, 18, 19, 20
    C_MAX2, C_S2, C_S2S, C_BETA = 21, 22, 23, 24
    C_M3, C_V3, C_R3, C_B3, C_Q3 = 25, 26, 27, 28, 29

    def quake(tag, vin, ncols, vout, qo):
        blk = block
        cs = slice(0, ncols)
        if True:
            @blk.scalar
            def _(sc):
                wait(sc, f"qpre_{tag}")
                sc.copy(qk_vb[:, cs], vin)
                sA.inc(sc.copy(qk_v64[:, cs], vin))
                mark(f"qa1_{tag}", sA)

            @blk.vector
            def _(v):
                wait(v, f"qa1_{tag}")
                sV.inc(v.tensor_single_scalar(qk_bi[:, cs],
                                              qk_vb[:, cs].bitcast(I16), 0.0, ALU.add))
                mark(f"qv1_{tag}", sV)

            @blk.scalar
            def _(sc):
                wait(sc, f"qv1_{tag}")
                sA.inc(sc.activation(qk_sd64[:, cs].bitcast(I16), qk_bi[:, cs],
                                     AF.Copy, scale=-0.5, bias=QC16))
                mark(f"qa2_{tag}", sA)

            @blk.vector
            def _(v):
                wait(v, f"qa2_{tag}")
                w64 = slice(0, 64)
                v.tensor_single_scalar(qk_y64a[:, w64], qk_sd64[:, w64], 0.0, ALU.add)
                cur, nxt = qk_y64a, qk_y64b
                for _i in range(3):
                    v.tensor_tensor(qk_t64a[:, w64], qk_v64[:, w64], cur[:, w64], ALU.mult)
                    v.tensor_tensor(nxt[:, w64], qk_t64a[:, w64], cur[:, w64], ALU.mult)
                    v.tensor_scalar(qk_t64a[:, w64], nxt[:, w64], -0.5, 1.5,
                                    ALU.mult, ALU.add)
                    v.tensor_tensor(nxt[:, w64], cur[:, w64], qk_t64a[:, w64], ALU.mult)
                    cur, nxt = nxt, cur
                sV.inc(v.tensor_single_scalar(vout, cur[:, cs], 0.0, ALU.add))
                v.wait_ge(sV.h, sV.n)
                mark(f"qdone_{tag}", sV)

    sTP1 = [Sem(es, nc, f"sTP1{i}") for i in range(2)]
    sROW1 = [Sem(es, nc, f"sROW1{i}") for i in range(2)]
    sROW2 = [Sem(es, nc, f"sROW2{i}") for i in range(2)]
    sT2e = [[Sem(es, nc, f"sT2e{k}_{i}") for i in range(2)] for k in range(8)]

    def _ctx(tt):
        p3 = tt % 3
        co = 30 * p3
        c2o = 16 * p3
        return (tt % 2,
                lambda c: colar[:, co + c:co + c + 1],
                lambda c: colar2[:, c2o + c:c2o + c + 1],
                256 * (tt % 2), 0,
                y0t[p3], utt[p3], dq1Tt[tt % 2], fst, qi16t[tt % 2])

    def emit_F1(tt):
        p2, CL, C2, rof, rofp, y0, ut, dq1T, fst_, qi16c = _ctx(tt)
        co = 30 * (tt % 3)

        @block.gpsimd
        def _(g, tt=tt):
            if tt == 0:
                wait(g, *A_DONE)
            if tt >= 1:
                wait(g, f"woutmm_{tt - 1}_1")
            sVLD.inc(g.dma_start(
                out=valsin[:, :].rearrange("d (h t) -> d h t", h=8),
                in_=dvals[:, tt * 128:(tt + 1) * 128].rearrange("(h d) t -> d h t", h=8)), 16)
            mark(f"vld_{tt}", sVLD)
            if tt >= 1:
                wait(g, f"tp1_{tt - 1}")
            sXIN.inc(g.dma_start(out=xin[:, :], in_=dx[tt * 128:(tt + 1) * 128, :]), 16)
            mark(f"xin_{tt}", sXIN)

        @block.tensor
        def _(t, tt=tt):
            if tt == 0:
                wait(t, "ld_wout")
            wait(t, f"vld_{tt}")
            if tt >= 1:
                wait(t, f"y0d_{tt - 1}")
            for half, ps in ((0, pA), (1, pB)):
                for dc in range(8):
                    ins = t.matmul(ps[:, :], valsin[:, dc * 128:(dc + 1) * 128],
                                   wout_sb[:, dc * D + half * 512:dc * D + half * 512 + 512],
                                   start=(dc == 0), stop=(dc == 7))
                sMM.inc(ins)
                mark(f"woutmm_{tt}_{half}", sMM)

        @block.vector
        def _(v, tt=tt):
            wait(v, f"woutmm_{tt}_1", f"xin_{tt}")
            if tt >= 3:
                wait(v, f"y2d_{tt - 3}")
            v.scalar_tensor_tensor(y0[:, 0:512], pA[:, :], 1.0, xin[:, 0:512],
                                   ALU.bypass, ALU.add, accum_out=C2(0))
            sV.inc(v.scalar_tensor_tensor(y0[:, 512:1024], pB[:, :], 1.0,
                                          xin[:, 512:1024], ALU.bypass, ALU.add,
                                          accum_out=C2(1)))
            mark(f"y0d_{tt}", sV)

        @block.scalar
        def _(sc, tt=tt):
            wait(sc, f"y0d_{tt}")
            sc.activation(pG[:, :], y0[:, 0:512], AF.Square, accum_out=C2(2))
            sA.inc(sc.activation(pG[:, :], y0[:, 512:1024], AF.Square, accum_out=C2(3)))
            mark(f"sq0_{tt}", sA)

        @block.vector
        def _(v, tt=tt):
            wait(v, f"sq0_{tt}")
            v = TinyV(v, sV)
            v.tensor_tensor(CL(C_SUM0), C2(0), C2(1), ALU.add)
            v.tensor_single_scalar(CL(C_M0), CL(C_SUM0), 1.0 / D, ALU.mult)
            v.tensor_tensor(CL(C_TMP), C2(2), C2(3), ALU.add)
            v.tensor_single_scalar(CL(C_V0), CL(C_TMP), 1.0 / D, ALU.mult)
            v.scalar_tensor_tensor(CL(C_TMP), CL(C_M0), CL(C_M0), CL(C_V0),
                                   ALU.mult, ALU.subtract)
            v.tensor_single_scalar(CL(C_V0), CL(C_TMP), -1.0, ALU.mult)
            v.tensor_scalar(CL(C_QA), CL(C_V0), 1.0 + EPS, EPS * EPS, ALU.mult, ALU.add)
            v.tensor_single_scalar(CL(C_QB), CL(C_V0), EPS, ALU.add)
            mark(f"qpre_q1_{tt}", sV)

        quake(f"q1_{tt}", colar[:, co + C_QA:co + C_QB + 1], 2,
              colar[:, co + C_USC:co + C_R0 + 1], 0)

    def emit_F2(tt):
        p2, CL, C2, rof, rofp, y0, ut, dq1T, fst_, qi16c = _ctx(tt)

        @block.vector
        def _(v, tt=tt):
            wait(v, f"qdone_q1_{tt}", "gb_ready")
            if tt >= 3:
                wait(v, f"outd_{tt - 3}_7")
            v.scalar_tensor_tensor(dq1T[:, :], y0[:, :], CL(C_M0), g1tb[:, :],
                                   ALU.subtract, ALU.mult)
            v.scalar_tensor_tensor(ut[:, :], dq1T[:, :], CL(C_USC), b1tb[:, :],
                                   ALU.mult, ALU.add)
            tv = TinyV(v, sV)
            tv.tensor_tensor(CL(C_SIG0), CL(C_QB), CL(C_R0), ALU.mult)
            tv.tensor_reduce(CL(C_MAX1), ut[:, :], AXX, ALU.max,
                             apply_absolute_value=True)
            tv.tensor_scalar(CL(C_S1), CL(C_MAX1), 1e-5, 1.0 / 127.0, ALU.max, ALU.mult)
            tv.reciprocal(CL(C_S1), CL(C_S1))
            tv.tensor_tensor(CL(C_S1S), CL(C_S1), swcols[:, 0:1], ALU.mult)
            tv.reciprocal(CL(C_AL1), CL(C_S1S))
            tv.tensor_single_scalar(srowcols[:, 2 * p2:2 * p2 + 1],
                                    CL(C_S1S), 0.0, ALU.add)
            mark(f"h1d_{tt}", sV)

        for half in range(2):
            @block.scalar
            def _(sc, tt=tt, half=half):
                if half == 0:
                    wait(sc, f"h1d_{tt}")
                else:
                    wait(sc, f"c1v_{tt}_0")
                hs = slice(half * 512, half * 512 + 512)
                sc.activation(pG[:, :], ut[:, hs], AF.Copy, scale=CL(C_S1), bias=MAGIC)
                sA.inc(sc.activation(qi16c[:, :], pG[:, :], AF.Copy, bias=-MAGIC))
                mark(f"c1a_{tt}_{half}", sA)

            @block.vector
            def _(v, tt=tt, half=half):
                wait(v, f"c1a_{tt}_{half}")
                hs = slice(half * 512, half * 512 + 512)
                sV.inc(v.tensor_single_scalar(dq1[:, hs], qi16c[:, :], 0.0, ALU.add))
                mark(f"c1v_{tt}_{half}", sV)
                if half == 1:
                    mark(f"conv1_{tt}", sV)

        @block.sync
        def _(s, tt=tt):
            wait(s, f"conv1_{tt}")
            sTP1[tt % 2].inc(s.dma_start_transpose(
                dq1T[:, :].rearrange("p (c t) -> p c t", c=8),
                dq1[:, :].rearrange("p (c d) -> p c d", c=8)), 16)
            mark(f"tp1_{tt}", sTP1[tt % 2])

    def emit_Bp1(tt):
        p2, CL, C2, rof, rofp, y0, ut, dq1T, fst_, qi16c = _ctx(tt)
        co = 30 * (tt % 3)

        @block.sync
        def _(s, tt=tt):
            wait(s, f"h1d_{tt}")
            sROW1[p2].inc(s.dma_start(out=rowar[0:1, rof:rof + 128],
                                      in_=srowcols[:, 2 * p2:2 * p2 + 1]), 16)
            mark(f"s1rowev_{tt}", sROW1[p2])

        for e in range(8):
            @block.tensor
            def _(t, tt=tt, e=e):
                if e == 0:
                    wait(t, f"tp1_{tt}", f"s1rowev_{tt}")
                    if tt == 0:
                        wait(t, "w1T_ready")
                ps = pC if e % 2 == 0 else pD
                if tt > 0 or e >= 2:
                    wait(t, f"gelu_{tt}_{e - 2}" if e >= 2 else f"gelu_{tt - 1}_{6 + e}")
                for dc in range(8):
                    t.matmul(ps[:, :], dq1T[:, dc * 128:(dc + 1) * 128],
                             w1T[:, dc * DFF + e * 512:dc * DFF + e * 512 + 512],
                             start=(dc == 0), stop=False)
                sMM.inc(t.matmul(ps[:, :], rowar[0:1, rof:rof + 128],
                                 b1row[0:1, e * 512:e * 512 + 512],
                                 start=False, stop=True))
                mark(f"m1_{tt}_{e}", sMM)

            @block.scalar
            def _(sc, tt=tt, e=e):
                wait(sc, f"m1_{tt}_{e}")
                if tt >= 2:
                    wait(sc, f"tp2_{tt - 2}_{e}")
                ps = pC if e % 2 == 0 else pD
                sc.activation(hga[tt % 2][:, e * 512:e * 512 + 512], ps[:, :],
                              AF.Gelu, scale=CL(C_AL1), accum_out=C2(0 + e))
                sA.inc(sc.activation(pG[:, :], hga[tt % 2][:, e * 512:e * 512 + 512],
                                     AF.Square, accum_out=C2(8 + e)))
                mark(f"gelu_{tt}_{e}", sA)
                mark(f"sq2_{tt}_{e}", sA)

        @block.vector
        def _(v, tt=tt):
            wait(v, f"sq2_{tt}_7")
            v = TinyV(v, sV)
            v.tensor_reduce(CL(C_GS), colar2[:, 16 * (tt % 3) + 0:16 * (tt % 3) + 8], AXX, ALU.add)
            v.tensor_reduce(CL(C_GSQ), colar2[:, 16 * (tt % 3) + 8:16 * (tt % 3) + 16], AXX, ALU.add)
            v.tensor_single_scalar(CL(C_M2), CL(C_GS), 1.0 / DFF, ALU.mult)
            v.tensor_single_scalar(CL(C_V2), CL(C_GSQ), 1.0 / DFF, ALU.mult)
            v.scalar_tensor_tensor(CL(C_TMP), CL(C_M2), CL(C_M2), CL(C_V2),
                                   ALU.mult, ALU.subtract)
            v.tensor_single_scalar(CL(C_V2), CL(C_TMP), -1.0, ALU.mult)
            v.tensor_single_scalar(CL(C_Q2), CL(C_V2), EPS, ALU.add)
            mark(f"qpre_q2_{tt}", sV)

        quake(f"q2_{tt}", colar[:, co + C_Q2:co + C_Q2 + 1], 1,
              colar[:, co + C_R2:co + C_R2 + 1], 0)

    def emit_Bp2_h2(tt):
        p2, CL, C2, rof, rofp, y0, ut, dq1T, fst_, qi16c = _ctx(tt)
        co = 30 * (tt % 3)

        @block.vector
        def _(v, tt=tt):
            wait(v, f"qdone_q2_{tt}")
            for k in range(16):
                n = 16 * tt + k
                v.wait_ge(sG2[k % 4].h, 16 * (n // 4 + 1))
                v.wait_ge(sB2[k % 4].h, 16 * (n // 4 + 1))
                hs = slice(k * 256, k * 256 + 256)
                v.scalar_tensor_tensor(hscr1[:, 0:256], hga[tt % 2][:, hs], CL(C_M2),
                                       g2s[k % 4][:, :], ALU.subtract, ALU.mult)
                v.scalar_tensor_tensor(hga[tt % 2][:, hs], hscr1[:, 0:256], CL(C_R2),
                                       b2s[k % 4][:, :], ALU.mult, ALU.add)
                sV.inc(v.tensor_reduce(C2(k), hga[tt % 2][:, hs], AXX, ALU.max,
                                       apply_absolute_value=True))
                mark(f"h2e_{tt}_{k}", sV)
            v.wait_ge(sV.h, sV.n)
            tv = TinyV(v, sV)
            tv.tensor_reduce(CL(C_MAX2), colar2[:, 16 * (tt % 3) + 0:16 * (tt % 3) + 16],
                             AXX, ALU.max)
            tv.tensor_scalar(CL(C_S2), CL(C_MAX2), 1e-5, 1.0 / 127.0, ALU.max, ALU.mult)
            tv.reciprocal(CL(C_S2), CL(C_S2))
            tv.tensor_tensor(CL(C_S2S), CL(C_S2), swcols[:, 1:2], ALU.mult)
            tv.reciprocal(CL(C_TMP), CL(C_S2S))
            tv.tensor_tensor(CL(C_BETA), CL(C_SIG0), CL(C_TMP), ALU.mult)
            tv.tensor_single_scalar(srowcols[:, 2 * p2 + 1:2 * p2 + 2],
                                    CL(C_S2S), 0.0, ALU.add)
            mark(f"h2d_{tt}", sV)

        @block.gpsimd
        def _(g, tt=tt):
            for k in range(16):
                if k >= 4:
                    wait(g, f"h2e_{tt}_{k - 4}")
                elif tt >= 1:
                    wait(g, f"h2e_{tt - 1}_{12 + k}")
                sl = slice(k * 256, k * 256 + 256)
                sG2[k % 4].inc(g.dma_start(out=g2s[k % 4][:, :], in_=dg2[:, sl]), 16)
                sB2[k % 4].inc(g.dma_start(out=b2s[k % 4][:, :], in_=db2g[:, sl]), 16)

    def emit_Bp2_rest(tt):
        p2, CL, C2, rof, rofp, y0, ut, dq1T, fst_, qi16c = _ctx(tt)
        co = 30 * (tt % 3)

        @block.sync
        def _(s, tt=tt):
            wait(s, f"h2d_{tt}")
            sROW2[p2].inc(s.dma_start(out=rowar[0:1, rof + 128:rof + 256],
                                      in_=srowcols[:, 2 * p2 + 1:2 * p2 + 2]), 16)
            mark(f"s2rowev_{tt}", sROW2[p2])

        for e in range(8):
            @block.scalar
            def _(sc, tt=tt, e=e):
                if e == 0:
                    wait(sc, f"h2d_{tt}")
                    if tt + 2 < NT:
                        wait(sc, f"conv1_{tt + 2}")
                if e >= 1:
                    wait(sc, f"c2v_{tt}_{e - 1}")
                hs = slice(e * 512, e * 512 + 512)
                sc.activation(pG[:, :], hga[tt % 2][:, hs], AF.Copy,
                              scale=CL(C_S2), bias=MAGIC)
                sA.inc(sc.activation(qi16c[:, :], pG[:, :], AF.Copy, bias=-MAGIC))
                mark(f"c2a_{tt}_{e}", sA)

            @block.vector
            def _(v, tt=tt, e=e):
                wait(v, f"c2a_{tt}_{e}")
                hs = slice(e * 512, e * 512 + 512)
                sV.inc(v.tensor_single_scalar(hga[tt % 2][:, hs], qi16c[:, :],
                                              0.0, ALU.add))
                mark(f"c2v_{tt}_{e}", sV)

        for e in range(8):
            @block.sync
            def _(s, tt=tt, e=e):
                wait(s, f"c2v_{tt}_{e}")
                if e >= 2:
                    wait(s, f"m2p_{tt}_{e - 2}")
                elif tt >= 1:
                    wait(s, f"m2p_{tt - 1}_{6 + e}")
                sT2e[e][tt % 2].inc(s.dma_start_transpose(
                    dq2Tp[e % 2][:, :].rearrange("p (c t) -> p c t", c=4),
                    hga[tt % 2][:, e * 512:(e + 1) * 512].rearrange("p (c d) -> p c d", c=4)), 16)
                mark(f"tp2_{tt}_{e}", sT2e[e][tt % 2])

            @block.tensor
            def _(t, tt=tt, e=e):
                wait(t, f"tp2_{tt}_{e}")
                if tt == 0:
                    wait(t, f"w2Tg_{e}")
                if e == 0 and tt > 0:
                    wait(t, f"fs_{tt - 1}")
                for half, ps in ((0, pE), (1, pF)):
                    for j in range(4):
                        hc = 4 * e + j
                        ins = t.matmul(ps[:, :], dq2Tp[e % 2][:, j * 128:(j + 1) * 128],
                                       w2T[:, hc * D + half * 512:hc * D + half * 512 + 512],
                                       start=(hc == 0), stop=False)
                if e < 7:
                    sMM.inc(ins)
                    mark(f"m2p_{tt}_{e}", sMM)
                else:
                    wait(t, f"s2rowev_{tt}")
                    for half, ps in ((0, pE), (1, pF)):
                        sMM.inc(t.matmul(ps[:, :], rowar[0:1, rof + 128:rof + 256],
                                         b2row[0:1, half * 512:half * 512 + 512],
                                         start=False, stop=True))
                        mark(f"m2_{tt}_{half}", sMM)
                    marks[f"m2p_{tt}_7"] = marks[f"m2_{tt}_1"]

        @block.scalar
        def _(sc, tt=tt):
            wait(sc, f"m2_{tt}_1")
            if tt >= 1:
                wait(sc, f"y2d_{tt - 1}")
            sc.activation(fst[:, 0:512], pE[:, :], AF.Copy, scale=CL(C_BETA))
            sA.inc(sc.activation(fst[:, 512:1024], pF[:, :], AF.Copy, scale=CL(C_BETA)))
            mark(f"fs_{tt}", sA)

        @block.vector
        def _(v, tt=tt):
            wait(v, f"fs_{tt}")
            sV.inc(v.scalar_tensor_tensor(ut[:, :], fst[:, :], 1.0, y0[:, :],
                                          ALU.bypass, ALU.add, accum_out=C2(8)))
            mark(f"y2d_{tt}", sV)

        @block.scalar
        def _(sc, tt=tt):
            wait(sc, f"y2d_{tt}")
            sc.activation(pG[:, :], ut[:, 0:512], AF.Square, accum_out=C2(9))
            sA.inc(sc.activation(pG[:, :], ut[:, 512:1024], AF.Square, accum_out=C2(10)))
            mark(f"sq3_{tt}", sA)

        @block.vector
        def _(v, tt=tt):
            wait(v, f"sq3_{tt}")
            v = TinyV(v, sV)
            v.tensor_single_scalar(CL(C_M3), C2(8), 1.0 / D, ALU.mult)
            v.tensor_tensor(CL(C_TMP), C2(9), C2(10), ALU.add)
            v.tensor_single_scalar(CL(C_V3), CL(C_TMP), 1.0 / D, ALU.mult)
            v.scalar_tensor_tensor(CL(C_TMP), CL(C_M3), CL(C_M3), CL(C_V3),
                                   ALU.mult, ALU.subtract)
            v.tensor_single_scalar(CL(C_V3), CL(C_TMP), -1.0, ALU.mult)
            v.scalar_tensor_tensor(CL(C_Q3), CL(C_QB), EPS, CL(C_V3),
                                   ALU.mult, ALU.add)
            mark(f"qpre_q3_{tt}", sV)

        quake(f"q3_{tt}", colar[:, co + C_Q3:co + C_Q3 + 1], 1,
              colar[:, co + C_R3:co + C_R3 + 1], 0)

        @block.vector
        def _(v, tt=tt):
            wait(v, f"qdone_q3_{tt}")
            v = TinyV(v, sV)
            v.scalar_tensor_tensor(CL(C_B3), CL(C_M3), -1.0, CL(C_R3),
                                   ALU.mult, ALU.mult)
            mark(f"st3_{tt}", sV)

        for qq in range(8):
            @block.scalar
            def _(sc, tt=tt, qq=qq):
                if qq == 0:
                    wait(sc, f"st3_{tt}")
                if qq >= 2:
                    wait(sc, f"ost_{tt}_{qq - 2}")
                elif tt >= 1:
                    wait(sc, f"ost_{tt - 1}_{6 + qq}")
                hs = slice(qq * 128, qq * 128 + 128)
                sA.inc(sc.activation(outt[qq % 2][:, :], ut[:, hs], AF.Identity,
                                     scale=CL(C_R3), bias=CL(C_B3)))
                mark(f"outd_{tt}_{qq}", sA)

            if qq % 2 == 0:
                @block.gpsimd
                def _(g, tt=tt, qq=qq):
                    wait(g, f"outd_{tt}_{qq}")
                    hs = slice(qq * 128, qq * 128 + 128)
                    sOUT.inc(g.dma_start(out=dy[tt * 128:(tt + 1) * 128, hs],
                                         in_=outt[qq % 2][:, :]), 16)
                    mark(f"ost_{tt}_{qq}", sOUT)
                    if tt == NT - 1 and qq == 6:
                        g.wait_ge(sOUT.h, 64 * NT)
            else:
                @block.sync
                def _(s, tt=tt, qq=qq):
                    wait(s, f"outd_{tt}_{qq}")
                    hs = slice(qq * 128, qq * 128 + 128)
                    sOUT2.inc(s.dma_start(out=dy[tt * 128:(tt + 1) * 128, hs],
                                          in_=outt[qq % 2][:, :]), 16)
                    mark(f"ost_{tt}_{qq}", sOUT2)
                    if tt == NT - 1 and qq == 7:
                        s.wait_ge(sOUT2.h, 64 * NT)

    for tt in range(NT):
        emit_F1(tt)
        if tt >= 2:
            emit_Bp2_h2(tt - 2)
        if tt >= 1:
            emit_Bp1(tt - 1)
        emit_F2(tt)
        if tt >= 2:
            emit_Bp2_rest(tt - 2)
    emit_Bp2_h2(NT - 2)
    emit_Bp1(NT - 1)
    emit_Bp2_rest(NT - 2)
    emit_Bp2_h2(NT - 1)
    emit_Bp2_rest(NT - 1)

    es.close()
    return nc


_NC_CACHE = {}


def kernel(**inputs):
    x = np.ascontiguousarray(np.asarray(inputs["L0_attn_x"], dtype=np.float32))
    wqkv = np.ascontiguousarray(np.asarray(inputs["L0_attn_w_qkv"], dtype=np.float32))
    wout = np.ascontiguousarray(np.asarray(inputs["L0_attn_w_out"], dtype=np.float32))
    g1 = np.asarray(inputs["L0_ffn_ln1_g"], dtype=np.float32)
    b1g = np.asarray(inputs["L0_ffn_ln1_b"], dtype=np.float32)
    w1 = np.ascontiguousarray(np.asarray(inputs["L0_ffn_w1"], dtype=np.float32))
    b1 = np.asarray(inputs["L0_ffn_b1"], dtype=np.float32)
    g2 = np.asarray(inputs["L0_ffn_ln2_g"], dtype=np.float32)
    b2g = np.asarray(inputs["L0_ffn_ln2_b"], dtype=np.float32)
    w2 = np.ascontiguousarray(np.asarray(inputs["L0_ffn_w2"], dtype=np.float32))
    b2 = np.asarray(inputs["L0_ffn_b2"], dtype=np.float32)

    consts = _np_consts()
    if "nc" not in _NC_CACHE:
        _NC_CACHE["nc"] = build_nc()
    nc = _NC_CACHE["nc"]

    in_maps = []
    for c in range(8):
        b = c // 4
        q = c % 4
        m = np.zeros(8, np.float32)
        for r in range(8):
            if r // 4 == b and (r % 4) < q:
                m[r] = 1.0
        oh = np.zeros(8, np.float32)
        oh[c] = 1.0
        im = {
            "x": np.ascontiguousarray(x[b, q * T:(q + 1) * T, :]),
            "wqkv": wqkv,
            "wout": wout,
            "w1": w1,
            "w2": w2,
            "b1r": np.ascontiguousarray(b1.reshape(1, DFF)),
            "b2r": np.ascontiguousarray(b2.reshape(1, D)),
            "g1t": np.ascontiguousarray(np.broadcast_to(g1[None, :], (128, D))),
            "b1t": np.ascontiguousarray(np.broadcast_to(b1g[None, :], (128, D))),
            "g2t": np.ascontiguousarray(np.broadcast_to(g2[None, :], (128, DFF))),
            "b2t": np.ascontiguousarray(np.broadcast_to(b2g[None, :], (128, DFF))),
            "maskr": np.ascontiguousarray(np.broadcast_to(m[None, :], (128, 8))),
            "onehot": np.ascontiguousarray(np.broadcast_to(oh[None, :], (128, 8))),
        }
        im.update(consts)
        in_maps.append(im)

    res = run_bass_kernel_spmd(nc, in_maps, core_ids=list(range(8)))
    out = np.zeros((2, 8192, D), np.float32)
    for c in range(8):
        b = c // 4
        q = c % 4
        out[b, q * T:(q + 1) * T, :] = res.results[c]["y"]
    return out


# version 24 (final)
# speedup vs baseline: 1.2662x; 1.0969x over previous
import numpy as np
import ml_dtypes
import contextlib
import concourse.bass as bass
import concourse.mybir as mybir
from concourse.bass_utils import run_bass_kernel_spmd

F32 = mybir.dt.float32
BF16 = mybir.dt.bfloat16
I16 = mybir.dt.int16
AF = mybir.ActivationFunctionType
ALU = mybir.AluOpType
AXX = mybir.AxisListType.X

NPBF = ml_dtypes.bfloat16
T = 2048
D = 1024
DFF = 4096
H = 8
NT = 16
MAGIC = float(3 * 2.0 ** 22)
QC16 = float(0x5F37)
EPS = 1e-5


def _np_consts():
    N = 128
    n = np.arange(N)
    j65 = np.arange(65)
    jj = np.arange(1, 64)
    w = 2 * np.pi / N
    Rre = np.cos(w * np.outer(j65, n))
    Rim = np.zeros((64, N)); Rim[1:64] = -np.sin(w * np.outer(jj, n))
    cj = np.ones(65); cj[1:64] = 2.0
    Are = (cj[None, :] / N) * np.cos(w * np.outer(n, j65))
    Aim = np.zeros((N, 64)); Aim[:, 1:64] = (-2.0 / N) * np.sin(w * np.outer(n, jj))
    return {
        "cRreT": np.ascontiguousarray(Rre.T).astype(np.float32),
        "cRimT": np.ascontiguousarray(Rim.T).astype(np.float32),
        "cAreT": np.ascontiguousarray(Are.T).astype(np.float32),
        "cAimT": np.ascontiguousarray(Aim.T).astype(np.float32),
        "ident": np.eye(128, dtype=np.float32),
        "onesrow": np.ones((1, 128), dtype=np.float32),
        "onescol": np.ones((128, 1), dtype=np.float32),
    }


class TinyV:

    def __init__(self, v, sem):
        self._v = v
        self._s = sem

    def __getattr__(self, name):
        f = getattr(self._v, name)
        if not callable(f):
            return f

        def wrapped(*a, **k):
            r = f(*a, **k)
            self._s.inc(r)
            self._v.wait_ge(self._s.h, self._s.n)
            return r
        return wrapped


class Sem:
    def __init__(self, stack, nc, name):
        self.h = stack.enter_context(nc.semaphore(name))
        self.n = 0

    def inc(self, instr, k=1):
        instr.then_inc(self.h, k)
        self.n += k


DEBUG = False


def build_nc():
    nc = bass.Bass("TRN2", target_bir_lowering=False)
    es = contextlib.ExitStack()
    marks = {}

    def mark(name, sem):
        marks[name] = (sem.h, sem.n)

    def wait(eng, *names):
        for name in names:
            h, n = marks[name]
            if n > 0:
                eng.wait_ge(h, n)

    dx = nc.dram_tensor("x", [T, D], F32, kind="ExternalInput").ap()
    dwqkv = nc.dram_tensor("wqkv", [D, 3 * D], F32, kind="ExternalInput").ap()
    dwout = nc.dram_tensor("wout", [D, D], F32, kind="ExternalInput").ap()
    dw1 = nc.dram_tensor("w1", [DFF, D], F32, kind="ExternalInput").ap()
    dw2 = nc.dram_tensor("w2", [D, DFF], F32, kind="ExternalInput").ap()
    db1 = nc.dram_tensor("b1r", [1, DFF], F32, kind="ExternalInput").ap()
    db2 = nc.dram_tensor("b2r", [1, D], F32, kind="ExternalInput").ap()
    dg1 = nc.dram_tensor("g1t", [128, D], F32, kind="ExternalInput").ap()
    db1g = nc.dram_tensor("b1t", [128, D], F32, kind="ExternalInput").ap()
    dg2 = nc.dram_tensor("g2t", [128, DFF], F32, kind="ExternalInput").ap()
    db2g = nc.dram_tensor("b2t", [128, DFF], F32, kind="ExternalInput").ap()
    dmaskr = nc.dram_tensor("maskr", [128, 8], F32, kind="ExternalInput").ap()
    donehot = nc.dram_tensor("onehot", [128, 8], F32, kind="ExternalInput").ap()
    dconst = {k: nc.dram_tensor(k, list(v.shape), F32, kind="ExternalInput").ap()
              for k, v in _np_consts().items()}
    dy = nc.dram_tensor("y", [T, D], F32, kind="ExternalOutput").ap()
    KIND = "ExternalOutput" if DEBUG else "Internal"
    dtbl_in = nc.dram_tensor("tbl_in", [128, 128], F32).ap()
    dtbl_out = nc.dram_tensor("tbl_out", [128, 128], F32).ap()
    dw1q = nc.dram_tensor("w1q", [DFF, D], BF16, kind=KIND).ap()
    dw2q = nc.dram_tensor("w2q", [D, DFF], BF16, kind=KIND).ap()
    dvals = nc.dram_tensor("valsd", [D, T], BF16, kind=KIND).ap()
    dkvre = nc.dram_tensor("dkvre", [H * 65, T], BF16, kind=KIND).ap()
    dkvim = nc.dram_tensor("dkvim", [H * 64, T], BF16, kind=KIND).ap()

    def sb(name, shp, dt):
        return es.enter_context(nc.sbuf_tensor(name, shp, dt))

    wout_sb = sb("wout_sb", [128, 8 * D], BF16)
    swcols = sb("swcols", [128, 2], F32)
    b1row = es.enter_context(nc.sbuf_tensor("b1row", [1, DFF], BF16, side="right"))
    b2row = es.enter_context(nc.sbuf_tensor("b2row", [1, D], BF16, side="right"))
    rowar = sb("rowar", [1, 512], BF16)
    colar = sb("colar", [128, 90], F32)
    colar2 = sb("colar2", [128, 48], F32)
    srowcols = sb("srowcols", [128, 4], BF16)
    qk_vb = sb("qk_vb", [128, 2], BF16)
    qk_bi = sb("qk_bi", [128, 2], F32)
    qk_v64 = sb("qk_v64", [128, 64], F32)
    qk_sd64 = sb("qk_sd64", [128, 64], BF16)
    qk_t64a = sb("qk_t64a", [128, 64], F32)
    qk_y64a = sb("qk_y64a", [128, 64], F32)
    qk_y64b = sb("qk_y64b", [128, 64], F32)


    sG = Sem(es, nc, "sG")
    sXL = Sem(es, nc, "sXL")
    sXT = Sem(es, nc, "sXT")
    sWL = Sem(es, nc, "sWL")
    sWQL = Sem(es, nc, "sWQL")
    sWS1 = Sem(es, nc, "sWS1")
    sWS2 = Sem(es, nc, "sWS2")
    sKVS = Sem(es, nc, "sKVS")
    sKVL = Sem(es, nc, "sKVL")
    sVST = Sem(es, nc, "sVST")
    sTBL = Sem(es, nc, "sTBL")
    sGB = Sem(es, nc, "sGB")
    sVLD = Sem(es, nc, "sVLD")
    sXIN = Sem(es, nc, "sXIN")
    sWT1 = Sem(es, nc, "sWT1")
    sWT2 = Sem(es, nc, "sWT2")
    sTP = Sem(es, nc, "sTP")
    sDBG = Sem(es, nc, "sDBG")
    sMM = Sem(es, nc, "sMM")
    sA = Sem(es, nc, "sA")
    sV = Sem(es, nc, "sV")
    sGP = Sem(es, nc, "sGP")
    sCC = Sem(es, nc, "sCC")
    sOUT = Sem(es, nc, "sOUT")
    sWLs = [Sem(es, nc, f"sWLs{k}") for k in range(4)]
    sWQs = [Sem(es, nc, f"sWQs{k}") for k in range(4)]
    sWT2e = [Sem(es, nc, f"sWT2e{k}") for k in range(8)]
    sG2 = [Sem(es, nc, f"sG2{k}") for k in range(4)]
    sOUT2 = Sem(es, nc, "sOUT2")
    sB2 = [Sem(es, nc, f"sB2{k}") for k in range(4)]

    esA1 = contextlib.ExitStack()

    def sbA1(name, shp, dt):
        return esA1.enter_context(nc.sbuf_tensor(name, shp, dt, side="right"))
    wkv = sbA1("wkv", [128, 8 * T], BF16)
    wstg = [sbA1(f"wstg{i}", [128, D], F32) for i in range(4)]
    wqf32 = sbA1("wqf32", [128, D], F32)
    wqi16 = [sbA1(f"wqi16{i}", [128, D], I16) for i in range(2)]
    dqwn = [sbA1(f"dqwn{i}", [128, D], BF16) for i in range(2)]
    kh = sbA1("kh", [128, T], BF16)
    vh = sbA1("vh", [128, T], BF16)
    xtmp = [sbA1(f"xtmp{i}", [128, D], BF16) for i in range(2)]
    fkre = sbA1("fkre", [65, T], BF16)

    esA2 = contextlib.ExitStack()

    def sbA2(name, shp, dt):
        return esA2.enter_context(nc.sbuf_tensor(name, shp, dt))
    xT = sbA2("xT", [128, 8 * T], BF16)
    crre = sbA2("crre", [128, 65], BF16)
    crim = sbA2("crim", [128, 64], BF16)
    care = sbA2("care", [65, 128], BF16)
    caim = sbA2("caim", [64, 128], BF16)
    onesrow_f = sbA2("onesrow_f", [1, 128], F32)
    onescol_f = sbA2("onescol_f", [128, 1], F32)
    maskr = sbA2("maskr_sb", [128, 8], F32)
    onehot = sbA2("onehot_sb", [128, 8], F32)
    Smtx = sbA2("Smtx", [128, 16], F32)
    Sr64 = sbA2("Sr64", [128, 8], F32)
    P_sb = sbA2("P_sb", [128, 16], F32)
    wacc = sbA2("wacc", [128, 64], F32)
    wcolsum = sbA2("wcolsum", [128, 2], F32)
    swv = sbA2("swv", [1, 2], F32)
    cont = sbA2("cont", [128, 128], F32)
    G = sbA2("G_sb", [128, 128], F32)
    wq = sbA2("wq", [128, 8 * D], BF16)
    scrB = sbA2("scrB", [64, T], BF16)
    fqre = [sbA2(f"fqre{i}", [65, T], BF16) for i in range(2)]
    fqim = [sbA2(f"fqim{i}", [64, T], BF16) for i in range(2)]
    valstg = sbA2("valstg", [128, T], BF16)
    qh = sbA2("qh", [128, T], BF16)

    esA3 = contextlib.ExitStack()
    fkim = esA3.enter_context(nc.sbuf_tensor("fkim", [64, T], BF16))
    fvre = esA3.enter_context(nc.sbuf_tensor("fvre", [65, T], BF16))
    fvim = esA3.enter_context(nc.sbuf_tensor("fvim", [64, T], BF16))
    kvre_p1 = esA3.enter_context(nc.sbuf_tensor("kvre_p1", [65, T], BF16))
    kvim_p1 = esA3.enter_context(nc.sbuf_tensor("kvim_p1", [64, T], BF16))
    esA3b = contextlib.ExitStack()
    kvre = kvim = None

    esAP = contextlib.ExitStack()

    def psA(name, shp, dt):
        return esAP.enter_context(nc.psum_tensor(name, shp, dt))
    pk = psA("pk", [128, 512], F32)
    pv = psA("pv", [128, 512], F32)
    pfre = [psA(f"pfre{i}", [65, 512], F32) for i in range(2)]
    pfim = [psA(f"pfim{i}", [64, 512], F32) for i in range(2)]
    pmisc = psA("pmisc", [128, 512], F32)

    block = es.enter_context(nc.Block())
    if True:
        @block.scalar
        def _(sc):
            sc.preload_act_table(AF.Gelu)

        @block.gpsimd
        def _(g):
            g.memset(Smtx[:, :], 0.0)
            g.memset(Sr64[:, :], 0.0)
            for pad in (qk_v64, qk_t64a, qk_y64a, qk_y64b):
                g.memset(pad[:, :], 0.0)
            g.memset(qk_sd64[:, :], 0.0)
            for nm, dst in (("cRreT", crre), ("cRimT", crim), ("cAreT", care),
                            ("cAimT", caim)):
                sG.inc(g.dma_start(out=dst[:, :], in_=dconst[nm][:, :]), 16)
            sG.inc(g.dma_start(out=onesrow_f[:, :], in_=dconst["onesrow"][:, :]), 16)
            sG.inc(g.dma_start(out=onescol_f[:, :], in_=dconst["onescol"][:, :]), 16)
            sG.inc(g.dma_start(out=maskr[:, :], in_=dmaskr[:, :]), 16)
            sG.inc(g.dma_start(out=onehot[:, :], in_=donehot[:, :]), 16)
            sG.inc(g.dma_start(out=b1row[:, :], in_=db1[:, :]), 16)
            sG.inc(g.dma_start(out=b2row[:, :], in_=db2[:, :]), 16)
            mark("ld_consts", sG)
            for dc in range(8):
                sG.inc(g.dma_start(out=wkv[:, dc * T:(dc + 1) * T],
                                   in_=dwqkv[dc * 128:(dc + 1) * 128, D:3 * D]), 16)
            mark("ld_wkv", sG)
            pass

    for tt in range(NT):
        if True:
            @block.gpsimd
            def _(g, tt=tt):
                if tt >= 2:
                    g.wait_ge(sXT.h, 16 * (tt - 1))
                sXL.inc(g.dma_start(out=xtmp[tt % 2][:, :],
                                    in_=dx[tt * 128:(tt + 1) * 128, :]), 16)
                mark(f"xld_{tt}", sXL)

            @block.sync
            def _(s, tt=tt):
                s.wait_ge(sXL.h, 16 * min(tt + 2, NT))
                sXT.inc(s.dma_start_transpose(
                    xT[:, :].rearrange("p (c u t) -> p c u t", c=8, u=NT)[:, :, tt, :],
                    xtmp[tt % 2][:, :].rearrange("p (c d) -> p c d", c=8)), 16)
                mark(f"xT_{tt}", sXT)
                if tt == NT - 1:
                    mark("xT_all", sXT)

    if True:
        @block.gpsimd
        def _(g):
            for dc in range(8):
                sG.inc(g.dma_start(out=wq[:, dc * D:(dc + 1) * D],
                                   in_=dwqkv[dc * 128:(dc + 1) * 128, 0:D]), 16)
            mark("ld_wq", sG)
            for dc in range(8):
                sG.inc(g.dma_start(out=wout_sb[:, dc * D:(dc + 1) * D],
                                   in_=dwout[dc * 128:(dc + 1) * 128, :]), 16)
            mark("ld_wout", sG)
            mark("ld_all", sG)

    def _wsrc(i):
        if i < 32:
            return dw1[i * 128:(i + 1) * 128, :]
        j = i - 32
        return dw2[(j // 4) * 128:(j // 4 + 1) * 128,
                   (j % 4) * 1024:(j % 4 + 1) * 1024]

    def emit_wscan(u):
        @block.gpsimd
        def _(g, u=u):
            i = 2 * u
            if i >= 4:
                wait(g, f"wred_{i - 4}")
            sWLs[i % 4].inc(g.dma_start(out=wstg[i % 4][:, :], in_=_wsrc(i)), 16)

        @block.sync
        def _(s, u=u):
            i = 2 * u + 1
            if i >= 4:
                wait(s, f"wred_{i - 4}")
            sWLs[i % 4].inc(s.dma_start(out=wstg[i % 4][:, :], in_=_wsrc(i)), 16)

        @block.vector
        def _(v, u=u):
            for i in (2 * u, 2 * u + 1):
                v.wait_ge(sWLs[i % 4].h, 16 * (i // 4 + 1))
                sV.inc(v.tensor_reduce(wacc[:, i:i + 1], wstg[i % 4][:, :], AXX,
                                       ALU.add, apply_absolute_value=True))
                mark(f"wred_{i}", sV)
                if i == 63:
                    mark("wred_all", sV)

    def emit_sw_chain():
        @block.vector
        def _(v):
            wait(v, "wred_all")
            v.tensor_reduce(wcolsum[:, 0:1], wacc[:, 0:32], AXX, ALU.add)
            sV.inc(v.tensor_reduce(wcolsum[:, 1:2], wacc[:, 32:64], AXX, ALU.add))
            mark("wcolsum", sV)

        @block.tensor
        def _(t):
            wait(t, "wcolsum", "ld_consts")
            t.matmul(pmisc[0:1, 0:1], wcolsum[:, 0:1], onescol_f[:, :],
                     start=True, stop=True)
            sMM.inc(t.matmul(pmisc[0:1, 1:2], wcolsum[:, 1:2], onescol_f[:, :],
                             start=True, stop=True))
            mark("wtot_mm", sMM)

        @block.scalar
        def _(sc):
            wait(sc, "wtot_mm")
            sA.inc(sc.copy(swv[0:1, 0:2], pmisc[0:1, 0:2]))
            mark("wtot_ev", sA)

        @block.tensor
        def _(t):
            wait(t, "wtot_ev")
            sMM.inc(t.matmul(pmisc[:, 8:10], onesrow_f[0:1, :], swv[0:1, 0:2],
                             start=True, stop=True))
            mark("swb_mm", sMM)

        @block.scalar
        def _(sc):
            wait(sc, "swb_mm")
            sA.inc(sc.copy(swcols[:, 0:2], pmisc[:, 8:10]))
            mark("sw_bc", sA)

        @block.vector
        def _(v):
            wait(v, "sw_bc")
            tv = TinyV(v, sV)
            tv.tensor_scalar(wcolsum[:, 0:2], swcols[:, 0:2], 1.0 / (DFF * D), 1e-5,
                             ALU.mult, ALU.max)
            sV.inc(v.reciprocal(swcols[:, 0:2], wcolsum[:, 0:2]))
            mark("sw_cols", sV)

    def emit_wquant(i):
        if True:
            @block.sync
            def _(s, i=i):
                if i == 0:
                    wait(s, "sw_cols")
                if i >= 4:
                    wait(s, f"wq_rd_{i - 4}")
                if i < 32:
                    src = dw1[i * 128:(i + 1) * 128, :]
                else:
                    j = i - 32
                    src = dw2[(j // 4) * 128:(j // 4 + 1) * 128,
                              (j % 4) * 1024:(j % 4 + 1) * 1024]
                sWQs[i % 4].inc(s.dma_start(out=wstg[i % 4][:, :], in_=src), 16)

            @block.scalar
            def _(sc, i=i):
                sc.wait_ge(sWQs[i % 4].h, 16 * (i // 4 + 1))
                if i >= 2:
                    wait(sc, f"wq_cl_{i - 2}")
                swsl = swcols[:, 0:1] if i < 32 else swcols[:, 1:2]
                sc.activation(wqf32[:, :], wstg[i % 4][:, :], AF.Copy,
                              scale=swsl, bias=MAGIC)
                sA.inc(sc.activation(wqi16[i % 2][:, :], wqf32[:, :], AF.Copy, bias=-MAGIC))
                mark(f"wq_rd_{i}", sA)

            @block.vector
            def _(v, i=i):
                wait(v, f"wq_rd_{i}")
                if i >= 2:
                    v.wait_ge(sWS1.h, 16 * min(i, 32))
                    if i > 32:
                        v.wait_ge(sWS2.h, 16 * (i - 32))
                sV.inc(v.tensor_scalar(dqwn[i % 2][:, :], wqi16[i % 2][:, :], -1.0, 1.0,
                                       ALU.max, ALU.min))
                mark(f"wq_cl_{i}", sV)

            @block.gpsimd
            def _(g, i=i):
                wait(g, f"wq_cl_{i}")
                if i < 32:
                    dst = dw1q[i * 128:(i + 1) * 128, :]
                    sWS1.inc(g.dma_start(out=dst, in_=dqwn[i % 2][:, :]), 16)
                else:
                    j = i - 32
                    dst = dw2q[(j // 4) * 128:(j // 4 + 1) * 128,
                               (j % 4) * 1024:(j % 4 + 1) * 1024]
                    sWS2.inc(g.dma_start(out=dst, in_=dqwn[i % 2][:, :]), 16)
                if i == 31:
                    mark("w1q_done", sWS1)
                if i == 63:
                    mark("w2q_done", sWS2)

    for h in range(H):
        par = 0
        for tb in range(4):
            if True:
                @block.tensor
                def _(t, h=h, tb=tb):
                    if h == 0:
                        wait(t, f"xT_{4 * tb + 3}")
                    if h == 0 and tb == 0:
                        wait(t, "ld_wkv")
                    else:
                        wait(t, f"evkv_{h}_{tb - 1}" if tb > 0 else f"evkv_{h - 1}_3")
                    for tau, ps in (("k", pk), ("v", pv)):
                        col0 = (0 if tau == "k" else D) + h * 128
                        for dc in range(8):
                            ins = t.matmul(ps[:, :],
                                           wkv[:, dc * T + col0:dc * T + col0 + 128],
                                           xT[:, dc * T + tb * 512:dc * T + tb * 512 + 512],
                                           start=(dc == 0), stop=(dc == 7))
                            if tau == "v" and dc == 7:
                                sMM.inc(ins)
                    mark(f"mmkv_{h}_{tb}", sMM)

                @block.scalar
                def _(sc, h=h, tb=tb):
                    wait(sc, f"mmkv_{h}_{tb}")
                    if h > 0 and tb == 0:
                        wait(sc, f"kvm_{h - 1}")
                    tsl = slice(tb * 512, tb * 512 + 512)
                    sc.copy(kh[:, tsl], pk[:, :])
                    sA.inc(sc.copy(vh[:, tsl], pv[:, :]))
                    mark(f"evkv_{h}_{tb}", sA)

                @block.tensor
                def _(t, h=h, tb=tb):
                    wait(t, f"evkv_{h}_{tb}")
                    if h > 0:
                        wait(t, f"evdv_{h - 1}_{tb}")
                    tsl = slice(tb * 512, tb * 512 + 512)
                    t.matmul(pfre[tb % 2][:, :], crre[:, :], kh[:, tsl], start=True, stop=True)
                    sMM.inc(t.matmul(pfim[tb % 2][:, :], crim[:, :], kh[:, tsl],
                                     start=True, stop=True))
                    mark(f"mmdk_{h}_{tb}", sMM)

                @block.scalar
                def _(sc, h=h, tb=tb):
                    tsl = slice(tb * 512, tb * 512 + 512)
                    wait(sc, f"mmdk_{h}_{tb}")
                    if h > 0 and tb == 0:
                        wait(sc, f"kvm_{h - 1}")
                    sc.copy(fkre[:, tsl], pfre[tb % 2][:, :])
                    sA.inc(sc.copy(fkim[:, tsl], pfim[tb % 2][:, :]))
                    mark(f"evdk_{h}_{tb}", sA)

                @block.tensor
                def _(t, h=h, tb=tb):
                    wait(t, f"evdk_{h}_{tb}")
                    tsl = slice(tb * 512, tb * 512 + 512)
                    t.matmul(pfre[tb % 2][:, :], crre[:, :], vh[:, tsl], start=True, stop=True)
                    sMM.inc(t.matmul(pfim[tb % 2][:, :], crim[:, :], vh[:, tsl],
                                     start=True, stop=True))
                    mark(f"mmdv_{h}_{tb}", sMM)

                @block.scalar
                def _(sc, h=h, tb=tb):
                    tsl = slice(tb * 512, tb * 512 + 512)
                    wait(sc, f"mmdv_{h}_{tb}")
                    sc.copy(fvre[:, tsl], pfre[tb % 2][:, :])
                    sA.inc(sc.copy(fvim[:, tsl], pfim[tb % 2][:, :]))
                    mark(f"evdv_{h}_{tb}", sA)

            emit_wscan(h * 4 + tb)

        if True:
            @block.vector
            def _(v, h=h, par=par):
                wait(v, f"evdv_{h}_3")
                if h >= 1:
                    wait(v, f"kvst_{h - 1}")
                v.scalar_tensor_tensor(kvre_p1[:, :], fkre[:, :], 1.0, fvre[:, :],
                                       ALU.bypass, ALU.mult,
                                       accum_out=Sr64[0:65, h:h + 1])
                v.tensor_tensor(scrB[:, :], fkim[:, :], fvim[:, :], ALU.mult)
                v.scalar_tensor_tensor(kvre_p1[0:64, :], scrB[:, :], -1.0,
                                       kvre_p1[0:64, :], ALU.mult, ALU.add,
                                       accum_out=Smtx[0:64, 2 * h:2 * h + 1])
                v.tensor_tensor(scrB[:, :], fkre[0:64, :], fvim[:, :], ALU.mult)
                v.tensor_tensor(kvim_p1[:, :], fkim[:, :], fvre[0:64, :], ALU.mult)
                sV.inc(v.scalar_tensor_tensor(kvim_p1[:, :], scrB[:, :], 1.0,
                                              kvim_p1[:, :], ALU.bypass, ALU.add,
                                              accum_out=Smtx[0:64, 2 * h + 1:2 * h + 2]))
                mark(f"kvm_{h}", sV)

            @block.scalar
            def _(sc, h=h):
                wait(sc, f"kvm_{h}")
                sA.inc(sc.copy(Smtx[64:65, 2 * h:2 * h + 1], Sr64[64:65, h:h + 1]))
                mark(f"sfix_{h}", sA)

            @block.gpsimd
            def _(g, h=h, par=par):
                wait(g, f"kvm_{h}")
                sKVS.inc(g.dma_start(out=dkvre[h * 65:(h + 1) * 65, :], in_=kvre_p1[:, :]), 16)
                sKVS.inc(g.dma_start(out=dkvim[h * 64:(h + 1) * 64, :], in_=kvim_p1[:, :]), 16)
                mark(f"kvst_{h}", sKVS)

    esA3.close()
    kvre = [esA3b.enter_context(nc.sbuf_tensor(f"kvre{i}", [65, T], BF16)) for i in range(2)]
    kvim = [esA3b.enter_context(nc.sbuf_tensor(f"kvim{i}", [64, T], BF16)) for i in range(2)]

    emit_sw_chain()

    if True:
        @block.vector
        def _(v):
            wait(v, "sfix_7")
            last = None
            for r in range(8):
                last = v.tensor_scalar(cont[:, r * 16:(r + 1) * 16], Smtx[:, :],
                                       onehot[:, r:r + 1], None, ALU.mult)
            sV.inc(last)
            mark("cont_ready", sV)

        @block.gpsimd
        def _(g):
            wait(g, "cont_ready")
            sTBL.inc(g.dma_start(out=dtbl_in[:, :], in_=cont[:, :]), 16)
            g.wait_ge(sTBL.h, sTBL.n)
            sCC.inc(g.collective_compute(
                "AllReduce", ALU.add, replica_groups=[list(range(8))],
                ins=[dtbl_in[:, :].opt()], outs=[dtbl_out[:, :].opt()]))
            g.wait_ge(sCC.h, sCC.n)
            sTBL.inc(g.dma_start(out=G[:, :], in_=dtbl_out[:, :]), 16)
            mark("G_ready", sTBL)

        @block.vector
        def _(v):
            wait(v, "G_ready")
            tv = TinyV(v, sV)
            tv.tensor_scalar(P_sb[:, :], G[:, 0:16], maskr[:, 0:1], None, ALU.mult)
            for r in range(1, 8):
                tv.scalar_tensor_tensor(P_sb[:, :], G[:, r * 16:(r + 1) * 16],
                                        maskr[:, r:r + 1], P_sb[:, :],
                                        ALU.mult, ALU.add)
            sV.inc(v.engine_nop())
            mark("P_ready", sV)

    def emit_p2F(h):
        par = h % 2
        for tb in range(4):
            @block.tensor
            def _(t, h=h, tb=tb):
                if h == 0 and tb == 0:
                    wait(t, "ld_wq")
                else:
                    wait(t, f"evq_{h}_{tb - 1}" if tb > 0 else f"evq_{h - 1}_3")
                for dc in range(8):
                    ins = t.matmul(pk[:, :],
                                   wq[:, dc * D + h * 128:dc * D + h * 128 + 128],
                                   xT[:, dc * T + tb * 512:dc * T + tb * 512 + 512],
                                   start=(dc == 0), stop=(dc == 7))
                sMM.inc(ins)
                mark(f"mmq_{h}_{tb}", sMM)

            @block.scalar
            def _(sc, h=h, tb=tb):
                wait(sc, f"mmq_{h}_{tb}")
                if h > 0 and tb == 0:
                    wait(sc, f"mmfq_{h - 1}_3")
                sA.inc(sc.copy(qh[:, tb * 512:tb * 512 + 512], pk[:, :]))
                mark(f"evq_{h}_{tb}", sA)

            @block.tensor
            def _(t, h=h, tb=tb):
                wait(t, f"evq_{h}_{tb}")
                tsl = slice(tb * 512, tb * 512 + 512)
                t.matmul(pfre[tb % 2][:, :], crre[:, :], qh[:, tsl], start=True, stop=True)
                sMM.inc(t.matmul(pfim[tb % 2][:, :], crim[:, :], qh[:, tsl],
                                 start=True, stop=True))
                mark(f"mmfq_{h}_{tb}", sMM)

            @block.scalar
            def _(sc, h=h, tb=tb, par=par):
                wait(sc, f"mmfq_{h}_{tb}")
                if h > 1 and tb == 0:
                    wait(sc, f"outf_{h - 2}")
                tsl = slice(tb * 512, tb * 512 + 512)
                sc.copy(fqre[par][:, tsl], pfre[tb % 2][:, :])
                sA.inc(sc.copy(fqim[par][:, tsl], pfim[tb % 2][:, :]))
                mark(f"evfq_{h}_{tb}", sA)

    def emit_p2B(h):
        par = h % 2
        if True:
            @block.gpsimd
            def _(g, h=h, par=par):
                wait(g, "kvst_7")
                if h >= 2:
                    wait(g, f"outf_{h - 2}")
                sKVL.inc(g.dma_start(out=kvre[par][:, :], in_=dkvre[h * 65:(h + 1) * 65, :]), 16)
                sKVL.inc(g.dma_start(out=kvim[par][:, :], in_=dkvim[h * 64:(h + 1) * 64, :]), 16)
                mark(f"kvld_{h}", sKVL)

            @block.vector
            def _(v, h=h, par=par):
                wait(v, "P_ready")
                wait(v, f"kvld_{h}", f"evfq_{h}_3")
                v.tensor_tensor_scan(kvre[par][:, :], kvre[par][:, :], kvre[par][:, :],
                                     P_sb[0:65, 2 * h:2 * h + 1], ALU.add, ALU.bypass)
                v.tensor_tensor_scan(kvim[par][:, :], kvim[par][:, :], kvim[par][:, :],
                                     P_sb[0:64, 2 * h + 1:2 * h + 2], ALU.add, ALU.bypass)
                v.tensor_tensor(scrB[:, :], kvim[par][:, :], fqre[par][0:64, :], ALU.mult)
                v.tensor_tensor(fqre[par][:, :], kvre[par][:, :], fqre[par][:, :], ALU.mult)
                v.tensor_tensor(kvim[par][:, :], kvim[par][:, :], fqim[par][:, :], ALU.mult)
                v.tensor_tensor(fqre[par][0:64, :], kvim[par][:, :], fqre[par][0:64, :], ALU.add)
                v.tensor_tensor(fqim[par][:, :], kvre[par][0:64, :], fqim[par][:, :], ALU.mult)
                sV.inc(v.tensor_tensor(fqim[par][:, :], scrB[:, :], fqim[par][:, :],
                                       ALU.subtract))
                mark(f"outf_{h}", sV)

        for tb in range(4):
            @block.tensor
            def _(t, h=h, tb=tb, par=par):
                wait(t, f"outf_{h}")
                if not (h == 0 and tb == 0):
                    wait(t, f"evval_{h}_{tb - 1}" if tb > 0 else f"evval_{h - 1}_3")
                tsl = slice(tb * 512, tb * 512 + 512)
                t.matmul(pmisc[:, :], care[:, :], fqre[par][:, tsl], start=True, stop=False)
                sMM.inc(t.matmul(pmisc[:, :], caim[:, :], fqim[par][:, tsl],
                                 start=False, stop=True))
                mark(f"mmval_{h}_{tb}", sMM)

            @block.scalar
            def _(sc, h=h, tb=tb):
                wait(sc, f"mmval_{h}_{tb}")
                if tb == 0 and h > 0:
                    wait(sc, f"valst_{h - 1}")
                sA.inc(sc.copy(valstg[:, tb * 512:tb * 512 + 512], pmisc[:, :]))
                mark(f"evval_{h}_{tb}", sA)

        if True:
            @block.gpsimd
            def _(g, h=h):
                wait(g, f"evval_{h}_3")
                sVST.inc(g.dma_start(out=dvals[h * 128:(h + 1) * 128, :], in_=valstg[:, :]), 16)
                mark(f"valst_{h}", sVST)

        for k in range(8):
            emit_wquant(8 * h + k)

    for h in range(H):
        emit_p2F(h)
        if h >= 1:
            emit_p2B(h - 1)
    emit_p2B(H - 1)

    marks["vals_all"] = marks["valst_7"]

    if DEBUG:
        ddbg = nc.dram_tensor("dbg1", [128, 256], F32, kind="ExternalOutput").ap()
        if True:
            @block.gpsimd
            def _(g):
                wait(g, "sw_cols", "G_ready", "P_ready", "sfix_7", "wred_all")
                sDBG.inc(g.dma_start(out=ddbg[:, 0:64], in_=wacc[:, :]), 16)
                sDBG.inc(g.dma_start(out=ddbg[:, 64:66], in_=wcolsum[:, :]), 16)
                sDBG.inc(g.dma_start(out=ddbg[:, 66:68], in_=swcols[:, :]), 16)
                sDBG.inc(g.dma_start(out=ddbg[:, 68:84], in_=P_sb[:, :]), 16)
                sDBG.inc(g.dma_start(out=ddbg[:, 84:100], in_=Smtx[:, :]), 16)
                sDBG.inc(g.dma_start(out=ddbg[0:1, 100:102], in_=swv[:, :]), 16)
                sDBG.inc(g.dma_start(out=ddbg[:, 102:230], in_=G[:, :]), 16)
                g.wait_ge(sDBG.h, sDBG.n)

    esA3b.close()
    esA1.close()
    w1T = es.enter_context(nc.sbuf_tensor("w1T", [128, 8 * DFF], BF16, side="right"))

    if True:
        @block.sync
        def _(s):
            wait(s, "w1q_done", "w2q_done", "kvm_7", "xT_all", "mmdv_7_3")
            for dc in range(0, 8, 2):
                sWT1.inc(s.dma_start_transpose(w1T[:, dc * DFF:(dc + 1) * DFF],
                                               dw1q[:, dc * 128:(dc + 1) * 128]), 16)

        @block.sync
        def _(s):
            for dc in range(1, 8, 2):
                sWT1.inc(s.dma_start_transpose(w1T[:, dc * DFF:(dc + 1) * DFF],
                                               dw1q[:, dc * 128:(dc + 1) * 128]), 16)
        marks["w1T_ready"] = (sWT1.h, sWT1.n)

    esA2.close()
    esAP.close()
    w2T = es.enter_context(nc.sbuf_tensor("w2T", [128, 32 * D], BF16))
    g1tb = es.enter_context(nc.sbuf_tensor("g1tb", [128, D], BF16))
    b1tb = es.enter_context(nc.sbuf_tensor("b1tb", [128, D], BF16))
    g2s = [es.enter_context(nc.sbuf_tensor(f"g2s{i}", [128, 256], BF16, side="right")) for i in range(4)]
    b2s = [es.enter_context(nc.sbuf_tensor(f"b2s{i}", [128, 256], BF16, side="right")) for i in range(4)]
    valsin = es.enter_context(nc.sbuf_tensor("valsin", [128, D], BF16))
    xin = es.enter_context(nc.sbuf_tensor("xin", [128, D], BF16))
    y0t = [es.enter_context(nc.sbuf_tensor(f"y0t{i}", [128, D], BF16)) for i in range(3)]
    utt = [es.enter_context(nc.sbuf_tensor(f"ut{i}", [128, D], BF16)) for i in range(3)]
    dq1Tt = [es.enter_context(nc.sbuf_tensor(f"dq1T{i}", [128, D], BF16, side="right")) for i in range(2)]
    hga = [es.enter_context(nc.sbuf_tensor(f"hga{i}", [128, DFF], BF16)) for i in range(2)]
    hscr1 = es.enter_context(nc.sbuf_tensor("hscr", [128, D], BF16))
    hscr = [hscr1, hscr1]
    fst = hscr1
    dq2Tp = [es.enter_context(nc.sbuf_tensor(f"dq2Tp{i}", [128, 512], BF16)) for i in range(2)]
    qi16one = es.enter_context(nc.sbuf_tensor("qi16c", [128, 512], I16, side="right"))
    qi16t = [qi16one, qi16one]
    outt = [es.enter_context(nc.sbuf_tensor(f"outt{i}", [128, 128], F32, side="right")) for i in range(4)]
    dq1 = xin

    pA = es.enter_context(nc.psum_tensor("pA", [128, 512], F32))
    pB = es.enter_context(nc.psum_tensor("pB", [128, 512], F32))
    pC = es.enter_context(nc.psum_tensor("pC", [128, 512], F32))
    pD = es.enter_context(nc.psum_tensor("pD", [128, 512], F32))
    pE = es.enter_context(nc.psum_tensor("pE", [128, 512], F32))
    pF = es.enter_context(nc.psum_tensor("pF", [128, 512], F32))
    pG = es.enter_context(nc.psum_tensor("pG", [128, 512], F32))

    A_DONE = ("outf_7", "evval_7_3", "mmval_7_3", "vals_all")

    if True:
        @block.gpsimd
        def _(g):
            wait(g, *A_DONE)
            sGB.inc(g.dma_start(out=g1tb[:, :], in_=dg1[:, :]), 16)
            sGB.inc(g.dma_start(out=b1tb[:, :], in_=db1g[:, :]), 16)
            mark("gb_ready", sGB)

        @block.sync
        def _(s):
            wait(s, "w2q_done", *A_DONE)
            for e in range(8):
                for j in range(4):
                    hc = 4 * e + j
                    sWT2e[e].inc(s.dma_start_transpose(w2T[:, hc * D:(hc + 1) * D],
                                                       dw2q[:, hc * 128:(hc + 1) * 128]), 16)
        for e in range(8):
            marks[f"w2Tg_{e}"] = (sWT2e[e].h, 64)

    C_SUM0, C_M0, C_V0, C_USC, C_R0, C_SIG0, C_UB, C_QA, C_QB = 0, 1, 2, 3, 4, 5, 6, 7, 8
    C_MAX1, C_S1, C_S1S, C_AL1, C_TMP = 9, 10, 11, 12, 13
    C_GS, C_GSQ, C_M2, C_V2, C_R2, C_UB2, C_Q2 = 14, 15, 16, 17, 18, 19, 20
    C_MAX2, C_S2, C_S2S, C_BETA = 21, 22, 23, 24
    C_M3, C_V3, C_R3, C_B3, C_Q3 = 25, 26, 27, 28, 29

    def quake(tag, vin, ncols, vout, qo):
        blk = block
        cs = slice(0, ncols)
        if True:
            @blk.scalar
            def _(sc):
                wait(sc, f"qpre_{tag}")
                sc.copy(qk_vb[:, cs], vin)
                sA.inc(sc.copy(qk_v64[:, cs], vin))
                mark(f"qa1_{tag}", sA)

            @blk.vector
            def _(v):
                wait(v, f"qa1_{tag}")
                sV.inc(v.tensor_single_scalar(qk_bi[:, cs],
                                              qk_vb[:, cs].bitcast(I16), 0.0, ALU.add))
                mark(f"qv1_{tag}", sV)

            @blk.scalar
            def _(sc):
                wait(sc, f"qv1_{tag}")
                sA.inc(sc.activation(qk_sd64[:, cs].bitcast(I16), qk_bi[:, cs],
                                     AF.Copy, scale=-0.5, bias=QC16))
                mark(f"qa2_{tag}", sA)

            @blk.vector
            def _(v):
                wait(v, f"qa2_{tag}")
                w64 = slice(0, 64)
                v.tensor_single_scalar(qk_y64a[:, w64], qk_sd64[:, w64], 0.0, ALU.add)
                cur, nxt = qk_y64a, qk_y64b
                for _i in range(3):
                    v.tensor_tensor(qk_t64a[:, w64], qk_v64[:, w64], cur[:, w64], ALU.mult)
                    v.tensor_tensor(nxt[:, w64], qk_t64a[:, w64], cur[:, w64], ALU.mult)
                    v.tensor_scalar(qk_t64a[:, w64], nxt[:, w64], -0.5, 1.5,
                                    ALU.mult, ALU.add)
                    v.tensor_tensor(nxt[:, w64], cur[:, w64], qk_t64a[:, w64], ALU.mult)
                    cur, nxt = nxt, cur
                sV.inc(v.tensor_single_scalar(vout, cur[:, cs], 0.0, ALU.add))
                v.wait_ge(sV.h, sV.n)
                mark(f"qdone_{tag}", sV)

    sTP1 = [Sem(es, nc, f"sTP1{i}") for i in range(2)]
    sROW1 = [Sem(es, nc, f"sROW1{i}") for i in range(2)]
    sROW2 = [Sem(es, nc, f"sROW2{i}") for i in range(2)]
    sT2e = [[Sem(es, nc, f"sT2e{k}_{i}") for i in range(2)] for k in range(8)]

    def _ctx(tt):
        p3 = tt % 3
        co = 30 * p3
        c2o = 16 * p3
        return (tt % 2,
                lambda c: colar[:, co + c:co + c + 1],
                lambda c: colar2[:, c2o + c:c2o + c + 1],
                256 * (tt % 2), 0,
                y0t[p3], utt[p3], dq1Tt[tt % 2], fst, qi16t[tt % 2])

    def emit_F1(tt):
        p2, CL, C2, rof, rofp, y0, ut, dq1T, fst_, qi16c = _ctx(tt)
        co = 30 * (tt % 3)

        @block.gpsimd
        def _(g, tt=tt):
            if tt == 0:
                wait(g, *A_DONE)
            if tt >= 1:
                wait(g, f"woutmm_{tt - 1}_1")
            sVLD.inc(g.dma_start(
                out=valsin[:, :].rearrange("d (h t) -> d h t", h=8),
                in_=dvals[:, tt * 128:(tt + 1) * 128].rearrange("(h d) t -> d h t", h=8)), 16)
            mark(f"vld_{tt}", sVLD)
            if tt >= 1:
                wait(g, f"tp1_{tt - 1}")
            sXIN.inc(g.dma_start(out=xin[:, :], in_=dx[tt * 128:(tt + 1) * 128, :]), 16)
            mark(f"xin_{tt}", sXIN)

        @block.tensor
        def _(t, tt=tt):
            if tt == 0:
                wait(t, "ld_wout")
            wait(t, f"vld_{tt}")
            if tt >= 1:
                wait(t, f"y0d_{tt - 1}")
            for half, ps in ((0, pA), (1, pB)):
                for dc in range(8):
                    ins = t.matmul(ps[:, :], valsin[:, dc * 128:(dc + 1) * 128],
                                   wout_sb[:, dc * D + half * 512:dc * D + half * 512 + 512],
                                   start=(dc == 0), stop=(dc == 7))
                sMM.inc(ins)
                mark(f"woutmm_{tt}_{half}", sMM)

        @block.vector
        def _(v, tt=tt):
            wait(v, f"woutmm_{tt}_1", f"xin_{tt}")
            if tt >= 3:
                wait(v, f"y2d_{tt - 3}")
            v.scalar_tensor_tensor(y0[:, 0:512], pA[:, :], 1.0, xin[:, 0:512],
                                   ALU.bypass, ALU.add, accum_out=C2(0))
            sV.inc(v.scalar_tensor_tensor(y0[:, 512:1024], pB[:, :], 1.0,
                                          xin[:, 512:1024], ALU.bypass, ALU.add,
                                          accum_out=C2(1)))
            mark(f"y0d_{tt}", sV)

        @block.scalar
        def _(sc, tt=tt):
            wait(sc, f"y0d_{tt}")
            sc.activation(pG[:, :], y0[:, 0:512], AF.Square, accum_out=C2(2))
            sA.inc(sc.activation(pG[:, :], y0[:, 512:1024], AF.Square, accum_out=C2(3)))
            mark(f"sq0_{tt}", sA)

        @block.vector
        def _(v, tt=tt):
            wait(v, f"sq0_{tt}")
            v = TinyV(v, sV)
            v.tensor_tensor(CL(C_SUM0), C2(0), C2(1), ALU.add)
            v.tensor_single_scalar(CL(C_M0), CL(C_SUM0), 1.0 / D, ALU.mult)
            v.tensor_tensor(CL(C_TMP), C2(2), C2(3), ALU.add)
            v.tensor_single_scalar(CL(C_V0), CL(C_TMP), 1.0 / D, ALU.mult)
            v.scalar_tensor_tensor(CL(C_TMP), CL(C_M0), CL(C_M0), CL(C_V0),
                                   ALU.mult, ALU.subtract)
            v.tensor_single_scalar(CL(C_V0), CL(C_TMP), -1.0, ALU.mult)
            v.tensor_scalar(CL(C_QA), CL(C_V0), 1.0 + EPS, EPS * EPS, ALU.mult, ALU.add)
            v.tensor_single_scalar(CL(C_QB), CL(C_V0), EPS, ALU.add)
            mark(f"qpre_q1_{tt}", sV)

        quake(f"q1_{tt}", colar[:, co + C_QA:co + C_QB + 1], 2,
              colar[:, co + C_USC:co + C_R0 + 1], 0)

    def emit_F2(tt):
        p2, CL, C2, rof, rofp, y0, ut, dq1T, fst_, qi16c = _ctx(tt)

        @block.vector
        def _(v, tt=tt):
            wait(v, f"qdone_q1_{tt}", "gb_ready")
            if tt >= 3:
                wait(v, f"outd_{tt - 3}_7")
            v.scalar_tensor_tensor(dq1T[:, :], y0[:, :], CL(C_M0), g1tb[:, :],
                                   ALU.subtract, ALU.mult)
            v.scalar_tensor_tensor(ut[:, :], dq1T[:, :], CL(C_USC), b1tb[:, :],
                                   ALU.mult, ALU.add)
            tv = TinyV(v, sV)
            tv.tensor_tensor(CL(C_SIG0), CL(C_QB), CL(C_R0), ALU.mult)
            tv.tensor_reduce(CL(C_MAX1), ut[:, :], AXX, ALU.max,
                             apply_absolute_value=True)
            tv.tensor_scalar(CL(C_S1), CL(C_MAX1), 1e-5, 1.0 / 127.0, ALU.max, ALU.mult)
            tv.reciprocal(CL(C_S1), CL(C_S1))
            tv.tensor_tensor(CL(C_S1S), CL(C_S1), swcols[:, 0:1], ALU.mult)
            tv.reciprocal(CL(C_AL1), CL(C_S1S))
            tv.tensor_single_scalar(srowcols[:, 2 * p2:2 * p2 + 1],
                                    CL(C_S1S), 0.0, ALU.add)
            mark(f"h1d_{tt}", sV)

        for half in range(2):
            @block.scalar
            def _(sc, tt=tt, half=half):
                if half == 0:
                    wait(sc, f"h1d_{tt}")
                else:
                    wait(sc, f"c1v_{tt}_0")
                hs = slice(half * 512, half * 512 + 512)
                sc.activation(pG[:, :], ut[:, hs], AF.Copy, scale=CL(C_S1), bias=MAGIC)
                sA.inc(sc.activation(qi16c[:, :], pG[:, :], AF.Copy, bias=-MAGIC))
                mark(f"c1a_{tt}_{half}", sA)

            @block.vector
            def _(v, tt=tt, half=half):
                wait(v, f"c1a_{tt}_{half}")
                hs = slice(half * 512, half * 512 + 512)
                sV.inc(v.tensor_single_scalar(dq1[:, hs], qi16c[:, :], 0.0, ALU.add))
                mark(f"c1v_{tt}_{half}", sV)
                if half == 1:
                    mark(f"conv1_{tt}", sV)

        @block.sync
        def _(s, tt=tt):
            wait(s, f"conv1_{tt}")
            sTP1[tt % 2].inc(s.dma_start_transpose(
                dq1T[:, :].rearrange("p (c t) -> p c t", c=8),
                dq1[:, :].rearrange("p (c d) -> p c d", c=8)), 16)
            mark(f"tp1_{tt}", sTP1[tt % 2])

    def emit_Bp1(tt):
        p2, CL, C2, rof, rofp, y0, ut, dq1T, fst_, qi16c = _ctx(tt)
        co = 30 * (tt % 3)

        @block.sync
        def _(s, tt=tt):
            wait(s, f"h1d_{tt}")
            sROW1[p2].inc(s.dma_start(out=rowar[0:1, rof:rof + 128],
                                      in_=srowcols[:, 2 * p2:2 * p2 + 1]), 16)
            mark(f"s1rowev_{tt}", sROW1[p2])

        for e in range(8):
            @block.tensor
            def _(t, tt=tt, e=e):
                if e == 0:
                    wait(t, f"tp1_{tt}", f"s1rowev_{tt}")
                    if tt == 0:
                        wait(t, "w1T_ready")
                ps = pC if e % 2 == 0 else pD
                if tt > 0 or e >= 2:
                    wait(t, f"gelu_{tt}_{e - 2}" if e >= 2 else f"gelu_{tt - 1}_{6 + e}")
                for dc in range(8):
                    t.matmul(ps[:, :], dq1T[:, dc * 128:(dc + 1) * 128],
                             w1T[:, dc * DFF + e * 512:dc * DFF + e * 512 + 512],
                             start=(dc == 0), stop=False)
                sMM.inc(t.matmul(ps[:, :], rowar[0:1, rof:rof + 128],
                                 b1row[0:1, e * 512:e * 512 + 512],
                                 start=False, stop=True))
                mark(f"m1_{tt}_{e}", sMM)

            @block.scalar
            def _(sc, tt=tt, e=e):
                wait(sc, f"m1_{tt}_{e}")
                if tt >= 2:
                    wait(sc, f"tp2_{tt - 2}_{e}")
                ps = pC if e % 2 == 0 else pD
                sc.activation(hga[tt % 2][:, e * 512:e * 512 + 512], ps[:, :],
                              AF.Gelu, scale=CL(C_AL1), accum_out=C2(0 + e))
                sA.inc(sc.activation(pG[:, :], hga[tt % 2][:, e * 512:e * 512 + 512],
                                     AF.Square, accum_out=C2(8 + e)))
                mark(f"gelu_{tt}_{e}", sA)
                mark(f"sq2_{tt}_{e}", sA)

        @block.vector
        def _(v, tt=tt):
            wait(v, f"sq2_{tt}_7")
            v = TinyV(v, sV)
            v.tensor_reduce(CL(C_GS), colar2[:, 16 * (tt % 3) + 0:16 * (tt % 3) + 8], AXX, ALU.add)
            v.tensor_reduce(CL(C_GSQ), colar2[:, 16 * (tt % 3) + 8:16 * (tt % 3) + 16], AXX, ALU.add)
            v.tensor_single_scalar(CL(C_M2), CL(C_GS), 1.0 / DFF, ALU.mult)
            v.tensor_single_scalar(CL(C_V2), CL(C_GSQ), 1.0 / DFF, ALU.mult)
            v.scalar_tensor_tensor(CL(C_TMP), CL(C_M2), CL(C_M2), CL(C_V2),
                                   ALU.mult, ALU.subtract)
            v.tensor_single_scalar(CL(C_V2), CL(C_TMP), -1.0, ALU.mult)
            v.tensor_single_scalar(CL(C_Q2), CL(C_V2), EPS, ALU.add)
            mark(f"qpre_q2_{tt}", sV)

        quake(f"q2_{tt}", colar[:, co + C_Q2:co + C_Q2 + 1], 1,
              colar[:, co + C_R2:co + C_R2 + 1], 0)

    def emit_Bp2_h2(tt):
        p2, CL, C2, rof, rofp, y0, ut, dq1T, fst_, qi16c = _ctx(tt)
        co = 30 * (tt % 3)

        @block.vector
        def _(v, tt=tt):
            wait(v, f"qdone_q2_{tt}")
            for k in range(16):
                n = 16 * tt + k
                v.wait_ge(sG2[k % 4].h, 16 * (n // 4 + 1))
                v.wait_ge(sB2[k % 4].h, 16 * (n // 4 + 1))
                hs = slice(k * 256, k * 256 + 256)
                v.scalar_tensor_tensor(hscr1[:, 0:256], hga[tt % 2][:, hs], CL(C_M2),
                                       g2s[k % 4][:, :], ALU.subtract, ALU.mult)
                v.scalar_tensor_tensor(hga[tt % 2][:, hs], hscr1[:, 0:256], CL(C_R2),
                                       b2s[k % 4][:, :], ALU.mult, ALU.add)
                sV.inc(v.tensor_reduce(C2(k), hga[tt % 2][:, hs], AXX, ALU.max,
                                       apply_absolute_value=True))
                mark(f"h2e_{tt}_{k}", sV)
            v.wait_ge(sV.h, sV.n)
            tv = TinyV(v, sV)
            tv.tensor_reduce(CL(C_MAX2), colar2[:, 16 * (tt % 3) + 0:16 * (tt % 3) + 16],
                             AXX, ALU.max)
            tv.tensor_scalar(CL(C_S2), CL(C_MAX2), 1e-5, 1.0 / 127.0, ALU.max, ALU.mult)
            tv.reciprocal(CL(C_S2), CL(C_S2))
            tv.tensor_tensor(CL(C_S2S), CL(C_S2), swcols[:, 1:2], ALU.mult)
            tv.reciprocal(CL(C_TMP), CL(C_S2S))
            tv.tensor_tensor(CL(C_BETA), CL(C_SIG0), CL(C_TMP), ALU.mult)
            tv.tensor_single_scalar(srowcols[:, 2 * p2 + 1:2 * p2 + 2],
                                    CL(C_S2S), 0.0, ALU.add)
            mark(f"h2d_{tt}", sV)

        @block.gpsimd
        def _(g, tt=tt):
            for k in range(16):
                if k >= 4:
                    wait(g, f"h2e_{tt}_{k - 4}")
                elif tt >= 1:
                    wait(g, f"h2e_{tt - 1}_{12 + k}")
                sl = slice(k * 256, k * 256 + 256)
                sG2[k % 4].inc(g.dma_start(out=g2s[k % 4][:, :], in_=dg2[:, sl]), 16)
                sB2[k % 4].inc(g.dma_start(out=b2s[k % 4][:, :], in_=db2g[:, sl]), 16)

    def emit_Bp2_rest(tt):
        p2, CL, C2, rof, rofp, y0, ut, dq1T, fst_, qi16c = _ctx(tt)
        co = 30 * (tt % 3)

        @block.sync
        def _(s, tt=tt):
            wait(s, f"h2d_{tt}")
            sROW2[p2].inc(s.dma_start(out=rowar[0:1, rof + 128:rof + 256],
                                      in_=srowcols[:, 2 * p2 + 1:2 * p2 + 2]), 16)
            mark(f"s2rowev_{tt}", sROW2[p2])

        for e in range(8):
            @block.scalar
            def _(sc, tt=tt, e=e):
                if e == 0:
                    wait(sc, f"h2d_{tt}")
                if e >= 1:
                    wait(sc, f"c2v_{tt}_{e - 1}")
                hs = slice(e * 512, e * 512 + 512)
                sc.activation(pG[:, :], hga[tt % 2][:, hs], AF.Copy,
                              scale=CL(C_S2), bias=MAGIC)
                sA.inc(sc.activation(qi16c[:, :], pG[:, :], AF.Copy, bias=-MAGIC))
                mark(f"c2a_{tt}_{e}", sA)

            @block.vector
            def _(v, tt=tt, e=e):
                wait(v, f"c2a_{tt}_{e}")
                hs = slice(e * 512, e * 512 + 512)
                sV.inc(v.tensor_single_scalar(hga[tt % 2][:, hs], qi16c[:, :],
                                              0.0, ALU.add))
                mark(f"c2v_{tt}_{e}", sV)

        for e in range(8):
            @block.sync
            def _(s, tt=tt, e=e):
                wait(s, f"c2v_{tt}_{e}")
                if e >= 2:
                    wait(s, f"m2p_{tt}_{e - 2}")
                elif tt >= 1:
                    wait(s, f"m2p_{tt - 1}_{6 + e}")
                sT2e[e][tt % 2].inc(s.dma_start_transpose(
                    dq2Tp[e % 2][:, :].rearrange("p (c t) -> p c t", c=4),
                    hga[tt % 2][:, e * 512:(e + 1) * 512].rearrange("p (c d) -> p c d", c=4)), 16)
                mark(f"tp2_{tt}_{e}", sT2e[e][tt % 2])

            @block.tensor
            def _(t, tt=tt, e=e):
                wait(t, f"tp2_{tt}_{e}")
                if tt == 0:
                    wait(t, f"w2Tg_{e}")
                if e == 0 and tt > 0:
                    wait(t, f"fs_{tt - 1}")
                for half, ps in ((0, pE), (1, pF)):
                    for j in range(4):
                        hc = 4 * e + j
                        ins = t.matmul(ps[:, :], dq2Tp[e % 2][:, j * 128:(j + 1) * 128],
                                       w2T[:, hc * D + half * 512:hc * D + half * 512 + 512],
                                       start=(hc == 0), stop=False)
                if e < 7:
                    sMM.inc(ins)
                    mark(f"m2p_{tt}_{e}", sMM)
                else:
                    wait(t, f"s2rowev_{tt}")
                    for half, ps in ((0, pE), (1, pF)):
                        sMM.inc(t.matmul(ps[:, :], rowar[0:1, rof + 128:rof + 256],
                                         b2row[0:1, half * 512:half * 512 + 512],
                                         start=False, stop=True))
                        mark(f"m2_{tt}_{half}", sMM)
                    marks[f"m2p_{tt}_7"] = marks[f"m2_{tt}_1"]

        @block.scalar
        def _(sc, tt=tt):
            wait(sc, f"m2_{tt}_1")
            if tt >= 1:
                wait(sc, f"y2d_{tt - 1}")
            sc.activation(fst[:, 0:512], pE[:, :], AF.Copy, scale=CL(C_BETA))
            sA.inc(sc.activation(fst[:, 512:1024], pF[:, :], AF.Copy, scale=CL(C_BETA)))
            mark(f"fs_{tt}", sA)

        @block.vector
        def _(v, tt=tt):
            wait(v, f"fs_{tt}")
            sV.inc(v.scalar_tensor_tensor(ut[:, :], fst[:, :], 1.0, y0[:, :],
                                          ALU.bypass, ALU.add, accum_out=C2(8)))
            mark(f"y2d_{tt}", sV)

        @block.scalar
        def _(sc, tt=tt):
            wait(sc, f"y2d_{tt}")
            sc.activation(pG[:, :], ut[:, 0:512], AF.Square, accum_out=C2(9))
            sA.inc(sc.activation(pG[:, :], ut[:, 512:1024], AF.Square, accum_out=C2(10)))
            mark(f"sq3_{tt}", sA)

        @block.vector
        def _(v, tt=tt):
            wait(v, f"sq3_{tt}")
            v = TinyV(v, sV)
            v.tensor_single_scalar(CL(C_M3), C2(8), 1.0 / D, ALU.mult)
            v.tensor_tensor(CL(C_TMP), C2(9), C2(10), ALU.add)
            v.tensor_single_scalar(CL(C_V3), CL(C_TMP), 1.0 / D, ALU.mult)
            v.scalar_tensor_tensor(CL(C_TMP), CL(C_M3), CL(C_M3), CL(C_V3),
                                   ALU.mult, ALU.subtract)
            v.tensor_single_scalar(CL(C_V3), CL(C_TMP), -1.0, ALU.mult)
            v.scalar_tensor_tensor(CL(C_Q3), CL(C_QB), EPS, CL(C_V3),
                                   ALU.mult, ALU.add)
            mark(f"qpre_q3_{tt}", sV)

        quake(f"q3_{tt}", colar[:, co + C_Q3:co + C_Q3 + 1], 1,
              colar[:, co + C_R3:co + C_R3 + 1], 0)

        @block.vector
        def _(v, tt=tt):
            wait(v, f"qdone_q3_{tt}")
            v = TinyV(v, sV)
            v.scalar_tensor_tensor(CL(C_B3), CL(C_M3), -1.0, CL(C_R3),
                                   ALU.mult, ALU.mult)
            mark(f"st3_{tt}", sV)

        for qq in range(8):
            @block.scalar
            def _(sc, tt=tt, qq=qq):
                if qq == 0:
                    wait(sc, f"st3_{tt}")
                if qq >= 4:
                    wait(sc, f"ost_{tt}_{qq - 4}")
                elif tt >= 1:
                    wait(sc, f"ost_{tt - 1}_{4 + qq}")
                hs = slice(qq * 128, qq * 128 + 128)
                sA.inc(sc.activation(outt[qq % 4][:, :], ut[:, hs], AF.Identity,
                                     scale=CL(C_R3), bias=CL(C_B3)))
                mark(f"outd_{tt}_{qq}", sA)

            if qq % 2 == 0:
                @block.gpsimd
                def _(g, tt=tt, qq=qq):
                    wait(g, f"outd_{tt}_{qq}")
                    hs = slice(qq * 128, qq * 128 + 128)
                    sOUT.inc(g.dma_start(out=dy[tt * 128:(tt + 1) * 128, hs],
                                         in_=outt[qq % 4][:, :]), 16)
                    mark(f"ost_{tt}_{qq}", sOUT)
                    if tt == NT - 1 and qq == 6:
                        g.wait_ge(sOUT.h, 64 * NT)
            else:
                @block.sync
                def _(s, tt=tt, qq=qq):
                    wait(s, f"outd_{tt}_{qq}")
                    hs = slice(qq * 128, qq * 128 + 128)
                    sOUT2.inc(s.dma_start(out=dy[tt * 128:(tt + 1) * 128, hs],
                                          in_=outt[qq % 4][:, :]), 16)
                    mark(f"ost_{tt}_{qq}", sOUT2)
                    if tt == NT - 1 and qq == 7:
                        s.wait_ge(sOUT2.h, 64 * NT)

    for tt in range(NT):
        emit_F1(tt)
        if tt >= 2:
            emit_Bp2_h2(tt - 2)
        if tt >= 1:
            emit_Bp1(tt - 1)
        if tt >= 2:
            emit_Bp2_rest(tt - 2)
        emit_F2(tt)
    emit_Bp2_h2(NT - 2)
    emit_Bp1(NT - 1)
    emit_Bp2_rest(NT - 2)
    emit_Bp2_h2(NT - 1)
    emit_Bp2_rest(NT - 1)

    es.close()
    return nc


_NC_CACHE = {}


def kernel(**inputs):
    x = np.ascontiguousarray(np.asarray(inputs["L0_attn_x"], dtype=np.float32))
    wqkv = np.ascontiguousarray(np.asarray(inputs["L0_attn_w_qkv"], dtype=np.float32))
    wout = np.ascontiguousarray(np.asarray(inputs["L0_attn_w_out"], dtype=np.float32))
    g1 = np.asarray(inputs["L0_ffn_ln1_g"], dtype=np.float32)
    b1g = np.asarray(inputs["L0_ffn_ln1_b"], dtype=np.float32)
    w1 = np.ascontiguousarray(np.asarray(inputs["L0_ffn_w1"], dtype=np.float32))
    b1 = np.asarray(inputs["L0_ffn_b1"], dtype=np.float32)
    g2 = np.asarray(inputs["L0_ffn_ln2_g"], dtype=np.float32)
    b2g = np.asarray(inputs["L0_ffn_ln2_b"], dtype=np.float32)
    w2 = np.ascontiguousarray(np.asarray(inputs["L0_ffn_w2"], dtype=np.float32))
    b2 = np.asarray(inputs["L0_ffn_b2"], dtype=np.float32)

    consts = _np_consts()
    if "nc" not in _NC_CACHE:
        _NC_CACHE["nc"] = build_nc()
    nc = _NC_CACHE["nc"]

    in_maps = []
    for c in range(8):
        b = c // 4
        q = c % 4
        m = np.zeros(8, np.float32)
        for r in range(8):
            if r // 4 == b and (r % 4) < q:
                m[r] = 1.0
        oh = np.zeros(8, np.float32)
        oh[c] = 1.0
        im = {
            "x": np.ascontiguousarray(x[b, q * T:(q + 1) * T, :]),
            "wqkv": wqkv,
            "wout": wout,
            "w1": w1,
            "w2": w2,
            "b1r": np.ascontiguousarray(b1.reshape(1, DFF)),
            "b2r": np.ascontiguousarray(b2.reshape(1, D)),
            "g1t": np.ascontiguousarray(np.broadcast_to(g1[None, :], (128, D))),
            "b1t": np.ascontiguousarray(np.broadcast_to(b1g[None, :], (128, D))),
            "g2t": np.ascontiguousarray(np.broadcast_to(g2[None, :], (128, DFF))),
            "b2t": np.ascontiguousarray(np.broadcast_to(b2g[None, :], (128, DFF))),
            "maskr": np.ascontiguousarray(np.broadcast_to(m[None, :], (128, 8))),
            "onehot": np.ascontiguousarray(np.broadcast_to(oh[None, :], (128, 8))),
        }
        im.update(consts)
        in_maps.append(im)

    res = run_bass_kernel_spmd(nc, in_maps, core_ids=list(range(8)))
    out = np.zeros((2, 8192, D), np.float32)
    for c in range(8):
        b = c // 4
        q = c % 4
        out[b, q * T:(q + 1) * T, :] = res.results[c]["y"]
    return out
